# version 11
# baseline (speedup 1.0000x reference)
import math
from contextlib import ExitStack

import numpy as np
import ml_dtypes
import concourse.bass as bass
import concourse.mybir as mybir
from concourse.bass_utils import run_bass_kernel_spmd
from concourse.alu_op_type import AluOpType as OP

F32 = mybir.dt.float32
BF16 = mybir.dt.bfloat16
I32 = mybir.dt.int32
AF = mybir.ActivationFunctionType
BF = ml_dtypes.bfloat16
AX = mybir.AxisListType

B, S, D = 4, 2048, 2048
NCH = D // 128
TOK = 1024
INTER = 8192
NH, HD = 16, 128
KEXP, RLOW = 8, 64
EPS = 1e-5

ENGINES = ("sync", "tensor", "vector", "scalar", "gpsimd")


class Prog:
    def __init__(self, nc):
        self.nc = nc
        self.ops = {e: [] for e in ENGINES}
        self.cnt = {}

    def _inc(self, sem, amt):
        self.cnt[sem] = self.cnt.get(sem, 0) + amt
        return (sem, self.cnt[sem])

    def add(self, eng, emit, waits=(), inc=None):
        tok = self._inc(*inc) if inc else None
        ws = tuple(w for w in waits if w is not None)
        self.ops[eng].append((ws, emit, (inc[0], tok[1]) if inc else None))
        return tok

    def dma(self, queue, out, in_, waits=(), sem="qmisc"):
        assert sem.startswith("q")
        return self.add(queue, lambda e, o=out, i=in_: e.dma_start(out=o, in_=i),
                        waits, inc=(sem, 16))

    def qtot(self, sem):
        return (sem, self.cnt.get(sem, 0))

    def op(self, eng, emit, waits=()):
        waits = list(waits)
        if eng in ("vector", "scalar", "gpsimd") and self.cnt.get("e" + eng, 0) > 0:
            waits.append(("e" + eng, self.cnt["e" + eng]))
        return self.add(eng, emit, waits, inc=("e" + eng, 1))

def emit_prog(p, st):
    nc = p.nc
    sems = {name: st.enter_context(nc.semaphore(name)) for name in sorted(p.cnt)}
    block = st.enter_context(nc.Block())

    def make_body(eng):
        def body(e):
            for ws, emit, inc in p.ops[eng]:
                for (s, v) in ws:
                    e.wait_ge(sems[s], v)
                ins = emit(e)
                if inc is not None:
                    amt = 16 if inc[0].startswith("q") else 1
                    ins.then_inc(sems[inc[0]], amt)
        return body

    for eng in ENGINES:
        if p.ops[eng]:
            getattr(block, eng)(make_body(eng))


class PsumPool:
    def __init__(self, p, st, n=4, shape=(128, 512), name="ps"):
        self.p = p
        self.tiles = [st.enter_context(p.nc.psum_tensor(f"{name}{i}", list(shape), F32))
                      for i in range(n)]
        self.freed = [None] * n
        self.i = 0

    def acquire(self):
        i = self.i % len(self.tiles)
        self.i += 1
        return i, self.tiles[i], ([self.freed[i]] if self.freed[i] else [])

    def release(self, i, tok):
        self.freed[i] = tok


class WStream:

    def __init__(self, p, st, name, shape, nbuf=2, queue="gpsimd", dtype=BF16):
        self.p = p
        self.name = name
        self.queue = queue
        self.nbuf = nbuf
        self.tiles = [st.enter_context(p.nc.sbuf_tensor(f"{name}{i}", list(shape), dtype))
                      for i in range(nbuf)]
        self.lastuse = []
        self.j = 0

    def load(self, src_ap, extra_waits=()):
        j = self.j
        self.j += 1
        buf = self.tiles[j % self.nbuf]
        waits = list(extra_waits)
        if j >= self.nbuf and self.lastuse[j - self.nbuf] is not None:
            waits.append(self.lastuse[j - self.nbuf])
        tok = self.p.dma(self.queue, buf[:], src_ap, waits,
                         sem=f"q{self.name}{j % self.nbuf}")
        self.lastuse.append(None)
        return j, buf, tok

    def used(self, j, tok):
        self.lastuse[j] = tok


def mm_group(p, ps_ap, tiles, waits=(), start=True, stop=True):
    n = len(tiles)

    def emit(e, tiles=tiles, ps=ps_ap, n=n):
        ins = None
        for i, (l, r) in enumerate(tiles):
            ins = e.matmul(ps, l, r, start=(start and i == 0),
                           stop=(stop and i == n - 1), skip_group_check=True)
        return ins
    return p.add("tensor", emit, waits, inc=("mm", 1))


def emit_ln(p, pool, xbf, n_ch, ncols, sq, mus, ex2s, vars_, rstd, nmr,
            epscol, ones128, out_cb, stat_waits=()):
    i_s, ps_s, war_s = pool.acquire()
    i_q, ps_q, war_q = pool.acquire()
    mm_toks = []
    for c in range(n_ch):
        sqc = sq[c % 2]
        sw = list(stat_waits) if c == 0 else []
        if c >= 2:
            sw.append(mm_toks[c - 2])
        sqtok = p.op("scalar", lambda e, i=c, d=sqc: e.activation(
            d[:, :ncols], xbf(i), AF.Square), waits=sw)

        def emit_mm(e, c=c, d=sqc):
            e.matmul(ps_s[:, :ncols], ones128, xbf(c), start=(c == 0),
                     stop=(c == n_ch - 1), skip_group_check=True)
            return e.matmul(ps_q[:, :ncols], ones128, d[:, :ncols],
                            start=(c == 0), stop=(c == n_ch - 1),
                            skip_group_check=True)
        mw = [sqtok] + (war_s + war_q if c == 0 else [])
        mm_toks.append(p.add("tensor", emit_mm, waits=mw, inc=("mm", 1)))
    mmtok = mm_toks[-1]
    t_mu = p.op("vector", lambda e: e.tensor_scalar_mul(mus[:, :ncols], ps_s[:, :ncols], 1.0 / D),
                waits=[mmtok])
    t_e2 = p.op("scalar", lambda e: e.mul(ex2s[:, :ncols], ps_q[:, :ncols], 1.0 / D),
                waits=[mmtok])
    t_m2 = p.op("scalar", lambda e: e.activation(vars_[:, :ncols], mus[:, :ncols], AF.Square),
                waits=[t_mu])
    t_v = p.op("vector", lambda e: e.tensor_tensor(vars_[:, :ncols], ex2s[:, :ncols], vars_[:, :ncols], OP.subtract),
               waits=[t_e2, t_m2])
    t_r0 = p.op("scalar", lambda e: e.activation(rstd[:, :ncols], vars_[:, :ncols], AF.Sqrt, bias=epscol),
                waits=[t_v])
    t_r = p.op("vector", lambda e: e.reciprocal(rstd[:, :ncols], rstd[:, :ncols]),
               waits=[t_r0])
    t_n = p.op("vector", lambda e: e.scalar_tensor_tensor(nmr[:, :ncols], mus[:, :ncols], -1.0, rstd[:, :ncols], OP.mult, OP.mult),
               waits=[t_r])
    pool.release(i_s, t_n)
    pool.release(i_q, t_n)
    last = None
    for c in range(n_ch):
        last = out_cb(c, t_n, rstd, nmr)
    return last


def fm(x):
    T, Dp = x.shape
    return np.ascontiguousarray(x.T.reshape(Dp // 128, 128, T))


def fm_inv(x):
    C, P, T = x.shape
    return np.ascontiguousarray(x.reshape(C * P, T).T)


def col(v):
    return np.ascontiguousarray(v.reshape(-1, 128).T)


def hadamard(n):
    H = np.array([[1.0]], dtype=np.float64)
    while H.shape[0] < n:
        H = np.block([[H, H], [H, -H]])
    return H


def rope_perm():
    perm = np.zeros(D, dtype=np.int64)
    for h in range(NH):
        base = h * HD
        for i in range(HD // 2):
            perm[base + i] = base + 2 * i
            perm[base + HD // 2 + i] = base + 2 * i + 1
    return perm


def causal_masks(h):
    m = np.zeros((2, 16, 128, 512), dtype=np.float32)
    kp = np.arange(128)[:, None]
    qf = np.arange(512)[None, :]
    for qb in range(2):
        Q0 = (2 * h + qb) * 512
        for kc in range(16):
            m[qb, kc] = (kc * 128 + kp) <= (Q0 + qf)
    return m.astype(BF)


def expand_mats():
    E = np.zeros((4, 8, 128), dtype=np.float32)
    for c in range(4):
        for pp in range(128):
            E[c, (c * 128 + pp) // 64, pp] = 1.0
    return E.astype(BF)


def host_prep(inputs):
    g = {}
    x = np.asarray(inputs["L0_attn_x"])
    g["x_fm"] = []
    for c in range(8):
        b, h = c // 2, c % 2
        g["x_fm"].append(fm(x[b, h * TOK:(h + 1) * TOK, :]))

    g["qkv"] = np.ascontiguousarray(inputs["L0_attn_qkv"])
    g["o_proj"] = np.ascontiguousarray(inputs["L0_attn_o_proj"])
    g["gamma"] = col(np.asarray(inputs["L0_attn_gamma"]))
    g["beta"] = col(np.asarray(inputs["L0_attn_beta"]))

    g["su_ug"] = col(np.asarray(inputs["L0_ffn_SU_ug"]))
    g["su_dp"] = col(np.asarray(inputs["L0_ffn_SU_dp"]))
    ws = np.asarray(inputs["L0_ffn_Wscale_ug"])
    g["wsu"] = col(ws[:INTER])
    g["wsg"] = col(ws[INTER:])
    g["wsdp"] = col(np.asarray(inputs["L0_ffn_Wscale_dp"]))
    g["w_upT"] = np.ascontiguousarray(np.asarray(inputs["L0_ffn_W_up"]).T)
    g["w_gateT"] = np.ascontiguousarray(np.asarray(inputs["L0_ffn_W_gate"]).T)
    g["w_downT"] = np.ascontiguousarray(np.asarray(inputs["L0_ffn_W_down"]).T)

    g["wqT"] = np.ascontiguousarray(np.asarray(inputs["L1_attn_wq"]).T)
    g["wkT"] = np.ascontiguousarray(np.asarray(inputs["L1_attn_wk"]).T)
    g["wvT"] = np.ascontiguousarray(np.asarray(inputs["L1_attn_wv"]).T)
    g["woT"] = np.ascontiguousarray(np.asarray(inputs["L1_attn_wo"]).T)
    f = np.asarray(inputs["L1_attn_inv_freq"])
    g["invf_dup"] = np.ascontiguousarray(np.concatenate([f, f])[None, :])

    g["W1"] = np.ascontiguousarray(inputs["L1_ffn_W1"])
    g["b1"] = col(np.asarray(inputs["L1_ffn_b1"]))
    g["W2"] = np.ascontiguousarray(inputs["L1_ffn_W2"])
    g["b2"] = np.ascontiguousarray(np.asarray(inputs["L1_ffn_b2"]).reshape(8, 1))
    V = np.asarray(inputs["L1_ffn_V"])
    g["VallT"] = np.ascontiguousarray(V.reshape(KEXP * RLOW, D).T)
    U = np.asarray(inputs["L1_ffn_U"])
    g["Ubar"] = np.ascontiguousarray(U.transpose(0, 2, 1).reshape(KEXP * RLOW, D))

    g["H2048s"] = hadamard(2048).astype(BF)
    g["C128"] = hadamard(128).astype(BF)
    g["I128"] = np.eye(128).astype(BF)
    Bsw = np.zeros((128, 128), dtype=np.float32)
    for i in range(64):
        Bsw[i, 64 + i] = 1.0
        Bsw[64 + i, i] = 1.0
    g["BswapT"] = np.ascontiguousarray(Bsw.T).astype(BF)
    g["masks"] = [causal_masks(c % 2) for c in range(8)]
    g["E4"] = expand_mats()
    g["ones1"] = np.ones((1, 128), dtype=np.float32).astype(BF)
    g["ones_col"] = np.ones((128, 1), dtype=np.float32).astype(BF)
    g["ones128"] = np.ones((128, 128), dtype=np.float32).astype(BF)
    g["epscol"] = np.full((128, 1), EPS, dtype=np.float32)
    g["I128f"] = np.eye(128, dtype=np.float32)
    E16 = np.zeros((16, 128, 16), dtype=np.float32)
    for th in range(16):
        E16[th, :, th] = 1.0
    g["E16"] = E16.astype(BF)
    sel = np.zeros((16, 16, 128), dtype=np.float32)
    for th in range(16):
        sel[th, th, :] = 1.0
    g["sel16"] = sel.astype(BF)
    g["ones1_16"] = np.ones((1, 16), dtype=np.float32).astype(BF)
    g["ones1_16f"] = np.ones((1, 16), dtype=np.float32)
    g["ones16c"] = np.ones((16, 1), dtype=np.float32).astype(BF)
    g["ones16cf"] = np.ones((16, 1), dtype=np.float32)
    sc = np.ones((128, 1), dtype=np.float32)
    sc[64:] = -1.0
    g["signcol"] = sc
    g["trow"] = [np.arange((c % 2) * TOK, (c % 2) * TOK + TOK,
                           dtype=np.float32)[None, :] for c in range(8)]
    g["trowS"] = np.arange(S, dtype=np.float32)[None, :]
    g["w2t"] = np.ascontiguousarray(
        np.asarray(inputs["L1_ffn_W2"]).reshape(16, 128, 8).transpose(1, 0, 2))
    return g


def build_launchA():
    nc = bass.Bass("TRN2", target_bir_lowering=False)
    d_x = nc.dram_tensor("x", [NCH, 128, TOK], F32, kind="ExternalInput").ap()
    d_qkv = nc.dram_tensor("qkv", [D, 3 * D], F32, kind="ExternalInput").ap()
    d_gamma = nc.dram_tensor("gamma", [128, NCH], F32, kind="ExternalInput").ap()
    d_beta = nc.dram_tensor("beta", [128, NCH], F32, kind="ExternalInput").ap()
    d_ones128 = nc.dram_tensor("ones128", [128, 128], BF16, kind="ExternalInput").ap()
    d_eps = nc.dram_tensor("epscol", [128, 1], F32, kind="ExternalInput").ap()
    d_qT = nc.dram_tensor("qT", [NCH, 128, TOK], F32, kind="ExternalOutput").ap()
    d_ktm = nc.dram_tensor("ktm", [8, 128, D], F32, kind="ExternalOutput").ap()
    d_vtm = nc.dram_tensor("vtm", [8, 128, D], F32, kind="ExternalOutput").ap()

    qkv_r = d_qkv.rearrange("(c p) o -> p c o", p=128)

    p = Prog(nc)
    with ExitStack() as st:
        sb = lambda name, shape, dt=BF16: st.enter_context(nc.sbuf_tensor(name, list(shape), dt))
        xbf = sb("xbf", [128, NCH, TOK])
        xnt = sb("xnt", [128, NCH, TOK])
        frot = sb("frot", [128, 8, 512], F32)
        sq = [sb(f"sq{i}", [128, 512]) for i in range(2)]
        mus = sb("mus", [128, 512], F32)
        ex2s = sb("ex2s", [128, 512], F32)
        vars_ = sb("vars", [128, 512], F32)
        rstd = sb("rstd", [128, 512], F32)
        nmr = sb("nmr", [128, 512], F32)
        t1 = sb("t1", [128, 512], F32)
        gam = sb("gam", [128, NCH], F32)
        bet = sb("bet", [128, NCH], F32)
        ones128 = sb("o128", [128, 128])
        epscol = sb("eps", [128, 1], F32)
        pool = PsumPool(p, st, n=8, shape=(128, 512))

        p.dma("gpsimd", xbf[:], d_x.rearrange("c p t -> p c t"), sem="qgx")
        p.dma("sync", gam[:], d_gamma, sem="qinit")
        p.cnt.setdefault("qgx", 0)
        p.dma("sync", bet[:], d_beta, sem="qinit")
        p.dma("sync", ones128[:], d_ones128, sem="qinit")
        p.dma("sync", epscol[:], d_eps, sem="qinit")
        t_init = p.qtot("qinit")
        t_init_g = p.qtot("qgx")

        for tb in range(2):
            t0 = tb * 512

            def out_cb(c, t_n, rstd, nmr, tb=tb, t0=t0):
                w = [t_n, t_init]
                tt1 = p.op("vector", lambda e, c=c: e.tensor_tensor(
                    t1[:], xbf[:, c, t0:t0 + 512], rstd[:, :512], OP.mult), waits=w)
                tt2 = p.op("vector", lambda e: e.tensor_tensor(
                    t1[:], t1[:], nmr[:, :512], OP.add), waits=[tt1])
                return p.op("vector", lambda e, c=c: e.tensor_scalar(
                    xnt[:, c, t0:t0 + 512], t1[:], gam[:, c:c + 1], bet[:, c:c + 1],
                    OP.mult, OP.add), waits=[tt2])

            emit_ln(p, pool, lambda c, t0=t0: xbf[:, c, t0:t0 + 512], NCH, 512,
                    sq, mus, ex2s, vars_, rstd, nmr, epscol[:], ones128[:],
                    out_cb, stat_waits=[t_init, t_init_g, ("mm", p.cnt.get("mm", 0))])

        t_xnt = ("evector", p.cnt["evector"])

        wq = WStream(p, st, "wq", [128, NCH, 128])
        slot_use = {}
        for oc in range(NCH):
            j, buf, ldtok = wq.load(qkv_r[:, :, oc * 128:(oc + 1) * 128])
            last_use = None
            for tb in range(2):
                t0 = tb * 512
                slot = (oc * 2 + tb) % 8
                i, ps, war = pool.acquire()
                tiles = [(buf[:, ic, :], xnt[:, ic, t0:t0 + 512]) for ic in range(NCH)]
                mmtok = mm_group(p, ps[:], tiles, waits=war + [ldtok, t_xnt])
                ev = p.op("vector", lambda e, slot=slot, ps=ps: e.tensor_copy(
                    frot[:, slot, :], ps[:]), waits=[mmtok, slot_use.get(slot)])
                pool.release(i, ev)
                last_use = mmtok
                stok = p.dma("sync", d_qT[oc, :, t0:t0 + 512], frot[:, slot, :],
                             waits=[ev], sem=f"qs{slot}")
                slot_use[slot] = stok
            wq.used(j, last_use)

        wkv = WStream(p, st, "wkv", [128, 512])
        for which, dst_dram in (("k", d_ktm), ("v", d_vtm)):
            base = D if which == "k" else 2 * D
            for ob in range(4):
                o0 = base + ob * 512
                pss = []
                for tt in range(8):
                    i, ps, war = pool.acquire()
                    pss.append((i, ps, war))
                ldtoks = []
                jj = []
                for ic in range(NCH):
                    j, buf, ldtok = wkv.load(qkv_r[:, ic, o0:o0 + 512])
                    ldtoks.append(ldtok)
                    jj.append(j)
                    def emit_mm(e, ic=ic, buf=buf, pss=pss):
                        ins = None
                        for tt in range(8):
                            ins = e.matmul(pss[tt][1][:], xnt[:, ic, tt * 128:(tt + 1) * 128],
                                           buf[:], start=(ic == 0), stop=(ic == NCH - 1),
                                           skip_group_check=True)
                        return ins
                    w = [ldtok, t_xnt] + (sum((pw for _, _, pw in pss), []) if ic == 0 else [])
                    mmtok = p.add("tensor", emit_mm, waits=w, inc=("mm", 1))
                    wkv.used(j, mmtok)
                for tt in range(8):
                    i, ps, _ = pss[tt]
                    ev = p.op("vector", lambda e, tt=tt, ps=ps:
                              e.tensor_copy(frot[:, tt, :], ps[:]),
                              waits=[mmtok, slot_use.get(tt)])
                    pool.release(i, ev)
                    stok = p.dma("sync", dst_dram[tt, :, ob * 512:ob * 512 + 512],
                                 frot[:, tt, :], waits=[ev], sem=f"qs{tt}")
                    slot_use[tt] = stok

        p.add("sync", lambda e: e.nop(),
              waits=[p.qtot(s) for s in sorted(p.cnt) if s.startswith("q")])
        emit_prog(p, st)
    return nc


def w4(w):
    Din, Dout = w.shape
    return np.ascontiguousarray(
        w.reshape(Din // 128, 128, Dout // 128, 128).transpose(2, 1, 0, 3))


def kv4(x_2d):
    return np.ascontiguousarray(
        x_2d.reshape(16, 128, 16, 128).transpose(2, 1, 0, 3))


def build_launchB():
    nc = bass.Bass("TRN2", target_bir_lowering=False)
    d_qT = nc.dram_tensor("qT", [NCH, 128, TOK], F32, kind="ExternalInput").ap()
    d_kT4 = nc.dram_tensor("kT4", [16, 128, 16, 128], F32, kind="ExternalInput").ap()
    d_v4 = nc.dram_tensor("v4", [16, 128, 16, 128], F32, kind="ExternalInput").ap()
    d_wo4 = nc.dram_tensor("wo4", [16, 128, 16, 128], F32, kind="ExternalInput").ap()
    d_x = nc.dram_tensor("x", [NCH, 128, TOK], F32, kind="ExternalInput").ap()
    d_masks = nc.dram_tensor("masks", [2, 16, 128, 512], BF16, kind="ExternalInput").ap()
    d_ones128 = nc.dram_tensor("ones128", [128, 128], BF16, kind="ExternalInput").ap()
    d_ones_col = nc.dram_tensor("ones_col", [128, 1], BF16, kind="ExternalInput").ap()
    d_ones1 = nc.dram_tensor("ones1", [1, 128], BF16, kind="ExternalInput").ap()
    d_eps = nc.dram_tensor("epscol", [128, 1], F32, kind="ExternalInput").ap()
    d_x1 = nc.dram_tensor("x1", [NCH, 128, TOK], F32, kind="ExternalOutput").ap()

    p = Prog(nc)
    with ExitStack() as st:
        sb = lambda name, shape, dt=BF16: st.enter_context(nc.sbuf_tensor(name, list(shape), dt))
        qT = sb("s_qT", [128, NCH, TOK])
        PT = sb("s_PT", [128, NCH, 512])
        AVn = sb("s_AVn", [128, NCH, 512])
        y1pre = sb("s_y1p", [128, NCH, 512], F32)
        y1bf = sb("s_y1b", [128, NCH, 512])
        x1rot = sb("s_x1r", [128, 4, 512], F32)
        xr = sb("s_xr", [128, 2, 512], F32)
        masks = sb("s_mask", [128, 16, 512])
        ones128 = sb("s_o128", [128, 128])
        ones_col = sb("s_oc", [128, 1])
        ones1 = sb("s_o1", [1, 128])
        epscol = sb("s_eps", [128, 1], F32)
        sqn = [sb(f"s_sqn{i}", [128, D]) for i in range(2)]
        qn2 = sb("s_qn2", [1, TOK], F32)
        kn2 = sb("s_kn2", [1, D], F32)
        sc1 = sb("s_sc1", [1, 1], F32)
        sc2 = sb("s_sc2", [1, 1], F32)
        sc2b = sb("s_sc2b", [1, 1], BF16)
        biascol = sb("s_bias", [128, 1], F32)
        linv = sb("s_linv", [128, 512], F32)
        sq = [sb(f"s_sq{i}", [128, 512]) for i in range(2)]
        mus = sb("s_mus", [128, 512], F32)
        ex2s = sb("s_ex2s", [128, 512], F32)
        vars_ = sb("s_vars", [128, 512], F32)
        rstd = sb("s_rstd", [128, 512], F32)
        nmr = sb("s_nmr", [128, 512], F32)
        t1 = sb("s_t1", [128, 512], F32)

        pool = PsumPool(p, st, n=6, shape=(128, 512))
        psn = [st.enter_context(nc.psum_tensor(f"psn{i}", [1, 512], F32)) for i in range(2)]

        p.dma("gpsimd", qT[:], d_qT.rearrange("c p t -> p c t"), sem="qgq")
        p.dma("sync", ones128[:], d_ones128, sem="qinit")
        p.dma("sync", ones_col[:], d_ones_col, sem="qinit")
        p.dma("sync", ones1[:], d_ones1, sem="qinit")
        p.dma("sync", epscol[:], d_eps, sem="qinit")
        t_init = p.qtot("qinit")
        t_gq = p.qtot("qgq")

        kst = WStream(p, st, "kst", [128, NCH, 128], queue="gpsimd")
        vst = WStream(p, st, "vst", [128, NCH, 128], queue="gpsimd")
        wo = WStream(p, st, "wo", [128, NCH, 128], queue="gpsimd")

        qn_toks = []
        for dc in range(NCH):
            sw = [t_init, t_gq] if dc == 0 else [qn_toks[dc - 2]] if dc >= 2 else []
            sqtok = p.op("scalar", lambda e, dc=dc: e.activation(
                sqn[dc % 2][:, 0:TOK], qT[:, dc, :], AF.Square), waits=sw)

            def emm(e, dc=dc):
                e.matmul(psn[0][:], ones_col[:], sqn[dc % 2][:, 0:512],
                         start=(dc == 0), stop=(dc == NCH - 1), skip_group_check=True)
                return e.matmul(psn[1][:], ones_col[:], sqn[dc % 2][:, 512:1024],
                                start=(dc == 0), stop=(dc == NCH - 1), skip_group_check=True)
            mw = [sqtok] + ([t_init] if dc == 0 else [])
            qn_toks.append(p.add("tensor", emm, waits=mw, inc=("mm", 1)))
        e1 = p.op("vector", lambda e: e.tensor_copy(qn2[:, 0:512], psn[0][:]), waits=[qn_toks[-1]])
        e2 = p.op("vector", lambda e: e.tensor_copy(qn2[:, 512:1024], psn[1][:]), waits=[qn_toks[-1]])
        t_qmax = p.op("vector", lambda e: e.reduce_max(sc1[:], qn2[:], axis=AX.X), waits=[e1, e2])

        kn_toks = [t_qmax]
        for kc in range(NCH):
            j, kbuf, ldtok = kst.load(d_kT4[kc])
            sqtok = p.op("scalar", lambda e, kbuf=kbuf, kc=kc: e.activation(
                sqn[kc % 2][:, 0:D], kbuf[:].rearrange("p c k -> p (c k)"), AF.Square),
                waits=[ldtok] + ([qn_toks[-1]] if kc < 2 else []))
            def emm(e, kc=kc):
                ins = None
                for c in range(NCH):
                    ins = e.matmul(psn[0][:, 0:128], ones_col[:],
                                   sqn[kc % 2][:, c * 128:(c + 1) * 128],
                                   start=(c == 0), stop=(c == NCH - 1), skip_group_check=True)
                return ins
            mmt = p.add("tensor", emm, waits=[sqtok] + ([kn_toks[-1]] if True else []), inc=("mm", 1))
            kst.used(j, mmt)
            ev = p.op("vector", lambda e, kc=kc: e.tensor_copy(
                kn2[:, kc * 128:(kc + 1) * 128], psn[0][:, 0:128]), waits=[mmt])
            kn_toks.append(ev)
        t_kmax = p.op("vector", lambda e: e.reduce_max(sc2[:], kn2[:], axis=AX.X), waits=[kn_toks[-1]])
        t_m2 = p.op("vector", lambda e: e.tensor_tensor(sc1[:], sc1[:], sc2[:], OP.mult),
                    waits=[t_qmax, t_kmax])
        t_m = p.op("scalar", lambda e: e.activation(sc1[:], sc1[:], AF.Sqrt), waits=[t_m2])
        t_b0 = p.op("scalar", lambda e: e.mul(sc2b[:], sc1[:], -1.0 / math.sqrt(D)), waits=[t_m])
        ib, psb, warb = pool.acquire()
        t_bb = mm_group(p, psb[:, 0:1], [(ones1[:], sc2b[:])], waits=warb + [t_b0, t_init])
        t_bias = p.op("scalar", lambda e: e.copy(biascol[:], psb[:, 0:1]), waits=[t_bb])
        pool.release(ib, t_bias)

        scale = 1.0 / math.sqrt(D)
        store_toks = []
        xr_use = {}
        mask_use = None
        for qb in range(2):
            qsl = slice(qb * 512, qb * 512 + 512)
            live = list(range(NCH))
            ld_m = p.dma("sync", masks[:], d_masks[qb].rearrange("m p q -> p m q"),
                         waits=[mask_use], sem="qmask")
            for kc in live:
                j, kbuf, ldtok = kst.load(d_kT4[kc])
                i, ps, war = pool.acquire()
                tiles = [(kbuf[:, dc, :], qT[:, dc, qsl]) for dc in range(NCH)]
                mmt = mm_group(p, ps[:], tiles, waits=war + [ldtok])
                kst.used(j, mmt)
                t_exp = p.op("scalar", lambda e, kc=kc, ps=ps: e.activation(
                    PT[:, kc, :], ps[:], AF.Exp, bias=biascol[:], scale=scale),
                    waits=[mmt, t_bias])
                lastPT = p.op("vector", lambda e, kc=kc: e.tensor_tensor(
                    PT[:, kc, :], PT[:, kc, :], masks[:, kc, :], OP.mult),
                    waits=[t_exp, ld_m])
                pool.release(i, lastPT)
            mask_use = ("evector", p.cnt["evector"])
            t_allPT = [("escalar", p.cnt["escalar"]), ("evector", p.cnt["evector"])]
            i, ps, war = pool.acquire()
            mmt = mm_group(p, ps[:], [(ones128[:], PT[:, kc, :]) for kc in live],
                           waits=war + t_allPT + [t_init])
            t_rec = p.op("vector", lambda e, ps=ps: e.reciprocal(linv[:], ps[:]), waits=[mmt])
            pool.release(i, t_rec)
            for dv in range(NCH):
                j, vbuf, ldtok = vst.load(d_v4[dv])
                i, ps, war = pool.acquire()
                tiles = [(vbuf[:, kc, :], PT[:, kc, :]) for kc in live]
                mmt = mm_group(p, ps[:], tiles, waits=war + [ldtok])
                vst.used(j, mmt)
                ev = p.op("vector", lambda e, dv=dv, ps=ps: e.tensor_tensor(
                    AVn[:, dv, :], ps[:], linv[:], OP.mult), waits=[mmt, t_rec])
                pool.release(i, ev)
            t_avn = [("evector", p.cnt["evector"])]
            for oc in range(NCH):
                j, wbuf, ldtok = wo.load(d_wo4[oc])
                ld_x_tok = p.dma("sync", xr[:, oc % 2, :], d_x[oc, :, qsl],
                                 waits=[xr_use.get((oc % 2))], sem=f"qxr{oc % 2}")
                i, ps, war = pool.acquire()
                tiles = [(wbuf[:, ic, :], AVn[:, ic, :]) for ic in range(NCH)]
                mmt = mm_group(p, ps[:], tiles, waits=war + [ldtok] + t_avn)
                wo.used(j, mmt)
                ev = p.op("vector", lambda e, oc=oc, ps=ps: e.scalar_tensor_tensor(
                    y1pre[:, oc, :], xr[:, oc % 2, :], 2.0, ps[:], OP.mult, OP.add),
                    waits=[mmt, ld_x_tok])
                xr_use[oc % 2] = ev
                pool.release(i, ev)
                p.op("vector", lambda e, oc=oc: e.tensor_copy(y1bf[:, oc, :], y1pre[:, oc, :]))
            def out_cb(c, t_n, rstd_, nmr_, qb=qb, qsl=qsl):
                w = [t_n]
                if len(store_toks) >= 4:
                    w.append(store_toks[-4])
                tt1 = p.op("vector", lambda e, c=c: e.tensor_tensor(
                    t1[:], y1pre[:, c, :], rstd_[:, :512], OP.mult), waits=w)
                tt2 = p.op("vector", lambda e, c=c: e.tensor_tensor(
                    x1rot[:, c % 4, :], t1[:], nmr_[:, :512], OP.add), waits=[tt1])
                stok = p.dma("sync", d_x1[c, :, qsl], x1rot[:, c % 4, :],
                             waits=[tt2], sem=f"qst{c % 4}")
                store_toks.append(stok)
                return stok

            emit_ln(p, pool, lambda c: y1bf[:, c, :], NCH, 512,
                    sq, mus, ex2s, vars_, rstd, nmr, epscol[:], ones128[:],
                    out_cb, stat_waits=[t_init, ("evector", p.cnt["evector"]),
                                        ("mm", p.cnt["mm"])])

        p.add("sync", lambda e: e.nop(),
              waits=[p.qtot(s) for s in sorted(p.cnt) if s.startswith("q")])
        emit_prog(p, st)
    return nc


def build_launchC():
    nc = bass.Bass("TRN2", target_bir_lowering=False)
    d_x1 = nc.dram_tensor("x1", [NCH, 128, TOK], F32, kind="ExternalInput").ap()
    d_H4 = nc.dram_tensor("H4", [16, 128, 16, 128], BF16, kind="ExternalInput").ap()
    d_up4 = nc.dram_tensor("up4", [64, 128, 16, 128], F32, kind="ExternalInput").ap()
    d_gt4 = nc.dram_tensor("gt4", [64, 128, 16, 128], F32, kind="ExternalInput").ap()
    d_dn4 = nc.dram_tensor("dn4", [16, 128, 64, 128], F32, kind="ExternalInput").ap()
    d_C128 = nc.dram_tensor("C128", [128, 128], BF16, kind="ExternalInput").ap()
    d_suug = nc.dram_tensor("suug", [128, NCH], F32, kind="ExternalInput").ap()
    d_sudp = nc.dram_tensor("sudp", [128, 64], F32, kind="ExternalInput").ap()
    d_wsu = nc.dram_tensor("wsu", [128, 64], F32, kind="ExternalInput").ap()
    d_wsg = nc.dram_tensor("wsg", [128, 64], F32, kind="ExternalInput").ap()
    d_wsdp = nc.dram_tensor("wsdp", [128, NCH], F32, kind="ExternalInput").ap()
    d_ones128 = nc.dram_tensor("ones128", [128, 128], BF16, kind="ExternalInput").ap()
    d_eps = nc.dram_tensor("epscol", [128, 1], F32, kind="ExternalInput").ap()
    d_x2 = nc.dram_tensor("x2", [NCH, 128, TOK], F32, kind="ExternalOutput").ap()

    p = Prog(nc)
    with ExitStack() as st:
        sb = lambda name, shape, dt=BF16: st.enter_context(nc.sbuf_tensor(name, list(shape), dt))
        x1bf = sb("s_x1bf", [128, NCH, 512])
        zt = x1bf
        tt = sb("s_tt", [128, NCH, 512])
        mid = sb("s_mid", [128, 64, 512])
        tmp16 = sb("s_tmp16", [128, 16, 512])
        y2 = sb("s_y2", [128, NCH, 512], F32)
        g1f = sb("s_g1f", [128, 512], F32)
        silb = sb("s_silb", [128, 512])
        xr = sb("s_xr", [128, 2, 512], F32)
        x2rot = sb("s_x2r", [128, 2, 512], F32)
        C128 = sb("s_C128", [128, 128])
        suug = sb("s_suug", [128, NCH], F32)
        sudp = sb("s_sudp", [128, 64], F32)
        wsu = sb("s_wsu", [128, 64], F32)
        wsg = sb("s_wsg", [128, 64], F32)
        wsdp = sb("s_wsdp", [128, NCH], F32)
        wsud = sb("s_wsud", [128, 64], F32)
        ones128 = sb("s_o128", [128, 128])
        epscol = sb("s_eps", [128, 1], F32)
        sq = [sb(f"s_sq{i}", [128, 512]) for i in range(2)]
        mus = sb("s_mus", [128, 512], F32)
        ex2s = sb("s_ex2s", [128, 512], F32)
        vars_ = sb("s_vars", [128, 512], F32)
        rstd = sb("s_rstd", [128, 512], F32)
        nmr = sb("s_nmr", [128, 512], F32)
        t1 = sb("s_t1", [128, 512], F32)

        pool = PsumPool(p, st, n=6, shape=(128, 512))
        for dst, src in ((C128, d_C128), (suug, d_suug), (sudp, d_sudp),
                         (wsu, d_wsu), (wsg, d_wsg), (wsdp, d_wsdp),
                         (ones128, d_ones128), (epscol, d_eps)):
            p.dma("sync", dst[:], src, sem="qinit")
        t_init = p.qtot("qinit")
        t_wsud = p.op("vector", lambda e: e.tensor_tensor(wsud[:], wsu[:], sudp[:], OP.mult),
                      waits=[t_init])

        wA = WStream(p, st, "wA", [128, NCH, 128])
        wB = WStream(p, st, "wB", [128, NCH, 128])
        wD = WStream(p, st, "wD", [128, 16, 128])

        store_toks = {}
        xr_use = {}
        for tb in range(2):
            tsl = slice(tb * 512, tb * 512 + 512)
            ld_x = p.dma("gpsimd", x1bf[:], d_x1.rearrange("c p t -> p c t")[:, :, tsl],
                         sem=f"qx{tb % 2}",
                         waits=[("evector", p.cnt.get("evector", 0))])
            zt_toks = []
            for c in range(NCH):
                zt_toks.append(p.op("vector", lambda e, c=c: e.tensor_scalar_mul(
                    zt[:, c, :], x1bf[:, c, :], suug[:, c:c + 1]),
                    waits=[ld_x, t_init, ("mm", p.cnt.get("mm", 0))] if c == 0 else []))
            t_zt = zt_toks[-1]
            for oc in range(NCH):
                j, buf, ldtok = wA.load(d_H4[oc])
                i, ps, war = pool.acquire()
                mmt = mm_group(p, ps[:], [(buf[:, ic, :], zt[:, ic, :]) for ic in range(NCH)],
                               waits=war + [ldtok, t_zt])
                wA.used(j, mmt)
                ev = p.op("vector", lambda e, oc=oc, ps=ps: e.tensor_scalar_mul(
                    tt[:, oc, :], ps[:], 1.0 / math.sqrt(D)), waits=[mmt])
                pool.release(i, ev)
            t_tt = ("evector", p.cnt["evector"])
            for oc in range(64):
                ju, bu, ldu = wA.load(d_up4[oc])
                jg, bg, ldg = wB.load(d_gt4[oc])
                iu, psu, waru = pool.acquire()
                mmu = mm_group(p, psu[:], [(bu[:, ic, :], tt[:, ic, :]) for ic in range(NCH)],
                               waits=waru + [ldu, t_tt])
                wA.used(ju, mmu)
                ig, psg, warg = pool.acquire()
                mmg = mm_group(p, psg[:], [(bg[:, ic, :], tt[:, ic, :]) for ic in range(NCH)],
                               waits=warg + [ldg, t_tt])
                wB.used(jg, mmg)
                tg1 = p.op("vector", lambda e, oc=oc, psg=psg: e.tensor_scalar_mul(
                    g1f[:], psg[:], wsg[:, oc:oc + 1]), waits=[mmg])
                pool.release(ig, tg1)
                tsil = p.op("scalar", lambda e: e.activation(silb[:], g1f[:], AF.Silu),
                            waits=[tg1])
                tmid = p.op("vector", lambda e, oc=oc, psu=psu: e.scalar_tensor_tensor(
                    mid[:, oc, :], psu[:], wsud[:, oc:oc + 1], silb[:], OP.mult, OP.mult),
                    waits=[mmu, tsil, t_wsud])
                pool.release(iu, tmid)
            for a in range(64):
                i, ps, war = pool.acquire()
                mmt = mm_group(p, ps[:], [(C128[:], mid[:, a, :])],
                               waits=war + ([t_init, ("evector", p.cnt["evector"])] if a == 0 else []))
                ev = p.op("vector", lambda e, a=a, ps=ps: e.tensor_scalar_mul(
                    mid[:, a, :], ps[:], 1.0 / math.sqrt(INTER)), waits=[mmt])
                pool.release(i, ev)
            midf = mid[:].rearrange("p c e -> p (c e)")
            tmpf = tmp16[:].rearrange("p c e -> p (c e)")
            for s_ in range(6):
                dd = 1 << s_
                views = []
                if 32 // dd >= 2:
                    gh = (32 // dd) // 2
                    v_ = midf.rearrange("p (g two dd e) -> p g two dd e",
                                        two=2, dd=dd, e=512)
                    tv_ = tmpf.rearrange("p (g dd e) -> p g dd e", dd=dd, e=512)
                    for h in range(2):
                        views.append((v_[:, h * gh:(h + 1) * gh, 0, :, :],
                                      v_[:, h * gh:(h + 1) * gh, 1, :, :],
                                      tv_[:, 0:gh, :, :]))
                else:
                    v_ = midf.rearrange("p (two dd e) -> p two dd e", dd=32, e=512)
                    tv_ = tmpf.rearrange("p (dd e) -> p dd e", e=512)
                    for h in range(2):
                        views.append((v_[:, 0, h * 16:(h + 1) * 16, :],
                                      v_[:, 1, h * 16:(h + 1) * 16, :],
                                      tv_[:, 0:16, :]))
                for lo, hi, tv in views:
                    p.op("vector", lambda e, lo=lo, hi=hi, tv=tv: e.tensor_tensor(tv, lo, hi, OP.add))
                    p.op("vector", lambda e, lo=lo, hi=hi: e.tensor_tensor(hi, lo, hi, OP.subtract))
                    p.op("vector", lambda e, lo=lo, tv=tv: e.tensor_copy(lo, tv))
            t_bfly = ("evector", p.cnt["evector"])
            for oc in range(NCH):
                ldr = p.dma("sync", xr[:, oc % 2, :], d_x1[oc, :, tsl],
                            waits=[xr_use.get(oc % 2)], sem=f"qxr{oc % 2}")
                i, ps, war = pool.acquire()
                mmq = None
                for qtr in range(4):
                    jq, bq, ldq = wD.load(d_dn4[oc, :, 16 * qtr:16 * qtr + 16, :])
                    w = [ldq]
                    if qtr == 0:
                        w += war + [t_bfly]
                    mmq = mm_group(
                        p, ps[:],
                        [(bq[:, ic, :], mid[:, 16 * qtr + ic, :]) for ic in range(16)],
                        waits=w, start=(qtr == 0), stop=(qtr == 3))
                    wD.used(jq, mmq)
                ev = p.op("vector", lambda e, oc=oc, ps=ps: e.scalar_tensor_tensor(
                    y2[:, oc, :], ps[:], wsdp[:, oc:oc + 1], xr[:, oc % 2, :], OP.mult, OP.add),
                    waits=[mmq, ldr])
                xr_use[oc % 2] = ev
                pool.release(i, ev)
            for c in range(NCH):
                p.op("vector", lambda e, c=c: e.tensor_copy(tt[:, c, :], y2[:, c, :]),
                     waits=[("mm", p.cnt["mm"])] if c == 0 else [])
            t_cast = ("evector", p.cnt["evector"])

            def out_cb(c, t_n, rstd_, nmr_, tb=tb, tsl=tsl):
                w = [t_n, store_toks.get(c % 2)]
                tt1 = p.op("vector", lambda e, c=c: e.tensor_tensor(
                    t1[:], y2[:, c, :], rstd_[:, :512], OP.mult), waits=w)
                tt2 = p.op("vector", lambda e, c=c: e.tensor_tensor(
                    x2rot[:, c % 2, :], t1[:], nmr_[:, :512], OP.add), waits=[tt1])
                stok = p.dma("sync", d_x2[c, :, tsl], x2rot[:, c % 2, :],
                             waits=[tt2], sem=f"qst{c % 2}")
                store_toks[c % 2] = stok
                return stok

            emit_ln(p, pool, lambda c: tt[:, c, :], NCH, 512,
                    sq, mus, ex2s, vars_, rstd, nmr, epscol[:], ones128[:],
                    out_cb, stat_waits=[t_init, t_cast, ("mm", p.cnt["mm"])])

        p.add("sync", lambda e: e.nop(),
              waits=[p.qtot(s) for s in sorted(p.cnt) if s.startswith("q")])
        emit_prog(p, st)
    return nc


def build_launchG():
    nc = bass.Bass("TRN2", target_bir_lowering=False)
    d_x2 = nc.dram_tensor("x2", [NCH, 128, TOK], F32, kind="ExternalInput").ap()
    d_wqT = nc.dram_tensor("wqT", [D, D], F32, kind="ExternalInput").ap()
    d_wkT = nc.dram_tensor("wkT", [D, D], F32, kind="ExternalInput").ap()
    d_wvT = nc.dram_tensor("wvT", [D, D], F32, kind="ExternalInput").ap()
    d_wo4 = nc.dram_tensor("wo4", [16, 128, 16, 128], F32, kind="ExternalInput").ap()
    d_E16 = nc.dram_tensor("E16", [16, 128, 16], BF16, kind="ExternalInput").ap()
    d_sel16 = nc.dram_tensor("sel16", [16, 16, 128], BF16, kind="ExternalInput").ap()
    d_ones1_16 = nc.dram_tensor("ones1_16", [1, 16], BF16, kind="ExternalInput").ap()
    d_ones1_16f = nc.dram_tensor("ones1_16f", [1, 16], F32, kind="ExternalInput").ap()
    d_ones16c = nc.dram_tensor("ones16c", [16, 1], BF16, kind="ExternalInput").ap()
    d_ones16cf = nc.dram_tensor("ones16cf", [16, 1], F32, kind="ExternalInput").ap()
    d_ones_col = nc.dram_tensor("ones_col", [128, 1], BF16, kind="ExternalInput").ap()
    d_ones128 = nc.dram_tensor("ones128", [128, 128], BF16, kind="ExternalInput").ap()
    d_eps = nc.dram_tensor("epscol", [128, 1], F32, kind="ExternalInput").ap()
    d_x3 = nc.dram_tensor("x3", [NCH, 128, TOK], F32, kind="ExternalOutput").ap()

    wqT_r = d_wqT.rearrange("(c p) o -> p c o", p=128)
    wkT_r = d_wkT.rearrange("(c p) o -> p c o", p=128)
    wvT_r = d_wvT.rearrange("(c p) o -> p c o", p=128)

    p = Prog(nc)
    with ExitStack() as st:
        sb = lambda name, shape, dt=BF16: st.enter_context(nc.sbuf_tensor(name, list(shape), dt))
        x2bf = sb("s_x2bf", [128, NCH, TOK])
        attn = x2bf
        qT = sb("s_qT", [128, NCH, TOK])
        kT = sb("s_kT", [128, NCH, TOK])
        vT = sb("s_vT", [128, NCH, TOK])
        y3pre = sb("s_y3p", [128, NCH, 512], F32)
        prodb = [sb(f"s_prod{i}", [128, 512]) for i in range(2)]
        sqs = sb("s_sqs", [128, 512])
        acc = sb("s_acc", [128, 512], F32)
        tmpm = sb("s_tmpm", [128, 512], F32)
        PL = sb("s_PL", [16, 512], F32)
        PLb = sb("s_PLb", [16, 512])
        knrow = sb("s_knr", [1, TOK])
        qnrow = sb("s_qnr", [1, TOK])
        negb = qnrow
        lrow = sb("s_lrow", [1, 512], F32)
        E16t = sb("s_E16", [128, 16, 16])
        sel16 = sb("s_sel", [16, 16, 128])
        ones1_16 = sb("s_o116", [1, 16])
        ones1_16f = sb("s_o116f", [1, 16], F32)
        ones16c = sb("s_o16c", [16, 1])
        ones16cf = sb("s_o16cf", [16, 1], F32)
        ones_col = sb("s_oc", [128, 1])
        ones128 = sb("s_o128", [128, 128])
        epscol = sb("s_eps", [128, 1], F32)
        xr = sb("s_xr", [128, 2, 512], F32)
        x3rot = sb("s_x3r", [128, 2, 512], F32)
        sq = [sb(f"s_sq{i}", [128, 512]) for i in range(2)]
        mus = sb("s_mus", [128, 512], F32)
        vars_ = sb("s_vars", [128, 512], F32)
        rstd = sb("s_rstd", [128, 512], F32)
        nmr = sb("s_nmr", [128, 512], F32)
        y3bf = kT

        pool = PsumPool(p, st, n=6, shape=(128, 512))
        psn = [st.enter_context(nc.psum_tensor(f"psn{i}", [1, 512], F32)) for i in range(2)]

        for dst, src in ((E16t, d_E16.rearrange("t p k -> p t k")), (sel16, d_sel16),
                         (ones1_16, d_ones1_16), (ones1_16f, d_ones1_16f),
                         (ones16c, d_ones16c), (ones16cf, d_ones16cf),
                         (ones_col, d_ones_col), (ones128, d_ones128), (epscol, d_eps)):
            p.dma("sync", dst[:], src, sem="qinit")
        p.dma("gpsimd", x2bf[:], d_x2.rearrange("c p t -> p c t"), sem="qgx")
        t_init = p.qtot("qinit")
        t_gx = p.qtot("qgx")

        wst = WStream(p, st, "wst", [128, NCH, 128])

        for dstT, wsrc in ((qT, wqT_r), (kT, wkT_r), (vT, wvT_r)):
            for oc in range(NCH):
                j, buf, ldtok = wst.load(wsrc[:, :, oc * 128:(oc + 1) * 128])
                last = None
                for tb in range(2):
                    tsl = slice(tb * 512, tb * 512 + 512)
                    i, ps, war = pool.acquire()
                    mmt = mm_group(p, ps[:], [(buf[:, ic, :], x2bf[:, ic, tsl]) for ic in range(NCH)],
                                   waits=war + [ldtok, t_gx])
                    last = mmt
                    ev = p.op("vector", lambda e, dstT=dstT, oc=oc, tsl=tsl, ps=ps:
                              e.tensor_copy(dstT[:, oc, tsl], ps[:]), waits=[mmt])
                    pool.release(i, ev)
                wst.used(j, last)
        t_proj = ("evector", p.cnt["evector"])

        kn_toks = []
        for c in range(NCH):
            for blk in range(2):
                bsl = slice(blk * 512, blk * 512 + 512)
                sw = [t_proj] if c == 0 and blk == 0 else [kn_toks[-1]]
                tsqk = p.op("scalar", lambda e, c=c, bsl=bsl: e.activation(
                    sqs[:], kT[:, c, bsl], AF.Square), waits=sw)
                mmt = p.add("tensor", lambda e, c=c, blk=blk: e.matmul(
                    psn[blk][:], ones_col[:], sqs[:], start=(c == 0), stop=(c == NCH - 1),
                    skip_group_check=True),
                    waits=[tsqk] + ([t_init] if c == 0 and blk == 0 else []), inc=("mm", 1))
                kn_toks.append(mmt)
        e1 = p.op("vector", lambda e: e.tensor_copy(knrow[:, 0:512], psn[0][:]), waits=[kn_toks[-1]])
        e2 = p.op("vector", lambda e: e.tensor_copy(knrow[:, 512:1024], psn[1][:]), waits=[kn_toks[-1]])
        t_kn = p.op("scalar", lambda e: e.activation(knrow[:], knrow[:], AF.Sqrt), waits=[e1, e2])

        scale = 1.0 / math.sqrt(HD)
        NCASC = 4
        psn_use = [t_kn, t_kn]
        for h in range(NH):
            mmq = None
            for blk in range(2):
                bsl = slice(blk * 512, blk * 512 + 512)
                tsq = p.op("scalar", lambda e, h=h, bsl=bsl: e.activation(
                    sqs[:], qT[:, h, bsl], AF.Square),
                    waits=[t_kn, ("mm", p.cnt["mm"])])
                mmq = p.add("tensor", lambda e, blk=blk: e.matmul(
                    psn[blk][:], ones_col[:], sqs[:], start=True, stop=True,
                    skip_group_check=True), waits=[tsq] + (psn_use if blk == 0 else []),
                    inc=("mm", 1))
            e1 = p.op("vector", lambda e: e.tensor_copy(qnrow[:, 0:512], psn[0][:]), waits=[mmq])
            e2 = p.op("vector", lambda e: e.tensor_copy(qnrow[:, 512:1024], psn[1][:]), waits=[mmq])
            psn_use = [e2, e2]
            t_q = p.op("scalar", lambda e: e.activation(qnrow[:], qnrow[:], AF.Sqrt), waits=[e1, e2])
            t_b = p.op("vector", lambda e: e.scalar_tensor_tensor(
                negb[:], qnrow[:], -1.0, knrow[:], OP.mult, OP.mult), waits=[t_q])

            for blk in range(2):
                bsl = slice(blk * 512, blk * 512 + 512)
                ip0, ps0, war0 = pool.acquire()
                pt_toks = []
                for th in range(NH):
                    pb = prodb[th % 2]
                    pw = [t_proj] if h == 0 and blk == 0 and th == 0 else []
                    if th >= 2:
                        pw.append(pt_toks[th - 2])
                    if th < 2:
                        pw.append(("mm", p.cnt["mm"]))
                    t_pr = p.op("vector", lambda e, h=h, th=th, pb=pb, bsl=bsl: e.tensor_tensor(
                        pb[:], qT[:, h, bsl], kT[:, th, bsl], OP.mult), waits=pw)
                    pt_toks.append(p.add("tensor", lambda e, th=th, pb=pb, ps0=ps0: e.matmul(
                        ps0[0:16, :], E16t[:, th, :], pb[:], start=(th == 0), stop=False,
                        skip_group_check=True), waits=[t_pr, t_init] + (war0 if th == 0 else []),
                        inc=("mm", 1)))
                mmb = p.add("tensor", lambda e, ps0=ps0, bsl=bsl: e.matmul(
                    ps0[0:16, :], ones1_16[:], negb[:, bsl], start=False, stop=True,
                    skip_group_check=True), waits=[t_b, pt_toks[-1]], inc=("mm", 1))
                tx0 = p.op("scalar", lambda e, ps0=ps0: e.activation(
                    PL[:], ps0[0:16, :], AF.Exp, scale=scale / (1 << NCASC)), waits=[mmb])
                pool.release(ip0, tx0)
                t_prev = tx0
                for r_ in range(NCASC + 1):
                    iL0, psL0, warL0 = pool.acquire()
                    mml = p.add("tensor", lambda e, psL0=psL0: e.matmul(
                        psL0[0:1, :], ones16cf[:], PL[:], start=True, stop=True,
                        skip_group_check=True), waits=warL0 + [t_prev], inc=("mm", 1))
                    tl0 = p.op("vector", lambda e, psL0=psL0: e.tensor_copy(lrow[:], psL0[0:1, :]),
                               waits=[mml])
                    pool.release(iL0, tl0)
                    t_lr = p.op("vector", lambda e: e.reciprocal(lrow[:], lrow[:]), waits=[tl0])
                    iB0, psB0, warB0 = pool.acquire()
                    mmB = p.add("tensor", lambda e, psB0=psB0: e.matmul(
                        psB0[0:16, :], ones1_16f[:], lrow[:], start=True, stop=True,
                        skip_group_check=True), waits=warB0 + [t_lr], inc=("mm", 1))
                    tn0 = p.op("vector", lambda e, psB0=psB0: e.tensor_tensor(
                        PL[:], PL[:], psB0[0:16, :], OP.mult), waits=[mmB])
                    pool.release(iB0, tn0)
                    if r_ < NCASC:
                        t_prev = p.op("vector", lambda e: e.tensor_tensor(
                            PL[:], PL[:], PL[:], OP.mult), waits=[tn0])
                    else:
                        t_prev = tn0
                t_plb = p.op("vector", lambda e: e.tensor_copy(PLb[:], PL[:]), waits=[t_prev])
                for th in range(NH):
                    iW, psW, warW = pool.acquire()
                    mmW = p.add("tensor", lambda e, th=th, psW=psW: e.matmul(
                        psW[:], sel16[:, th, :], PLb[:], start=True, stop=True,
                        skip_group_check=True),
                        waits=warW + ([t_plb] if th == 0 else []), inc=("mm", 1))
                    tmul = p.op("vector", lambda e, th=th, psW=psW, bsl=bsl: e.tensor_tensor(
                        tmpm[:], psW[:], vT[:, th, bsl], OP.mult), waits=[mmW])
                    if th == 0:
                        p.op("vector", lambda e: e.tensor_copy(acc[:], tmpm[:]), waits=[tmul])
                    else:
                        p.op("vector", lambda e: e.tensor_tensor(
                            acc[:], acc[:], tmpm[:], OP.add), waits=[tmul])
                    pool.release(iW, tmul)
                p.op("vector", lambda e, h=h, bsl=bsl: e.tensor_copy(attn[:, h, bsl], acc[:]))
        t_attn = ("evector", p.cnt["evector"])

        store_toks = {}
        xr_use = {}
        wo = wst
        for qb in range(2):
            qsl = slice(qb * 512, qb * 512 + 512)
            for oc in range(NCH):
                j, wbuf, ldtok = wo.load(d_wo4[oc])
                ld_x_tok = p.dma("sync", xr[:, oc % 2, :], d_x2[oc, :, qsl],
                                 waits=[xr_use.get(oc % 2)], sem=f"qxr{oc % 2}")
                i, ps, war = pool.acquire()
                mmt = mm_group(p, ps[:], [(wbuf[:, ic, :], attn[:, ic, qsl]) for ic in range(NCH)],
                               waits=war + [ldtok, t_attn])
                wo.used(j, mmt)
                ev = p.op("vector", lambda e, oc=oc, ps=ps: e.tensor_tensor(
                    y3pre[:, oc, :], ps[:], xr[:, oc % 2, :], OP.add), waits=[mmt, ld_x_tok])
                xr_use[oc % 2] = ev
                pool.release(i, ev)
                p.op("vector", lambda e, oc=oc, qsl=qsl: e.tensor_copy(
                    y3bf[:, oc, 0:512], y3pre[:, oc, :]))

            def out_cb(c, t_n, rstd_, nmr_, qb=qb, qsl=qsl):
                w = [t_n, store_toks.get(c % 2)]
                tt1 = p.op("vector", lambda e, c=c: e.tensor_tensor(
                    x3rot[:, c % 2, :], y3pre[:, c, :], rstd_[:, :512], OP.mult), waits=w)
                tt2 = p.op("vector", lambda e, c=c: e.tensor_tensor(
                    x3rot[:, c % 2, :], x3rot[:, c % 2, :], nmr_[:, :512], OP.add), waits=[tt1])
                stok = p.dma("sync", d_x3[c, :, qsl], x3rot[:, c % 2, :],
                             waits=[tt2], sem=f"qst{c % 2}")
                store_toks[c % 2] = stok
                return stok

            emit_ln(p, pool, lambda c: y3bf[:, c, 0:512], NCH, 512,
                    sq, mus, rstd, vars_, rstd, nmr, epscol[:], ones128[:],
                    out_cb, stat_waits=[t_init, ("evector", p.cnt["evector"]),
                                        ("mm", p.cnt["mm"])])

        p.add("sync", lambda e: e.nop(),
              waits=[p.qtot(s) for s in sorted(p.cnt) if s.startswith("q")])
        emit_prog(p, st)
    return nc


def build_launchF():
    nc = bass.Bass("TRN2", target_bir_lowering=False)
    d_x3 = nc.dram_tensor("x3", [NCH, 128, TOK], F32, kind="ExternalInput").ap()
    d_w14 = nc.dram_tensor("w14", [16, 128, 16, 128], F32, kind="ExternalInput").ap()
    d_b1 = nc.dram_tensor("b1col", [128, NCH], F32, kind="ExternalInput").ap()
    d_w2 = nc.dram_tensor("w2t", [128, NCH, 8], F32, kind="ExternalInput").ap()
    d_b2 = nc.dram_tensor("b2col", [8, 1], F32, kind="ExternalInput").ap()
    d_va4 = nc.dram_tensor("va4", [4, 128, 16, 128], F32, kind="ExternalInput").ap()
    d_ub4 = nc.dram_tensor("ub4", [16, 128, 4, 128], F32, kind="ExternalInput").ap()
    d_E4 = nc.dram_tensor("E4", [4, 8, 128], BF16, kind="ExternalInput").ap()
    d_I = nc.dram_tensor("I128f", [128, 128], F32, kind="ExternalInput").ap()
    d_Ib = nc.dram_tensor("I128", [128, 128], BF16, kind="ExternalInput").ap()
    d_ones128 = nc.dram_tensor("ones128", [128, 128], BF16, kind="ExternalInput").ap()
    d_eps = nc.dram_tensor("epscol", [128, 1], F32, kind="ExternalInput").ap()
    d_out = nc.dram_tensor("out", [NCH, 128, TOK], F32, kind="ExternalOutput").ap()

    p = Prog(nc)
    with ExitStack() as st:
        sb = lambda name, shape, dt=BF16: st.enter_context(nc.sbuf_tensor(name, list(shape), dt))
        x3bf = sb("s_x3bf", [128, NCH, 512])
        hbuf = sb("s_h", [128, NCH, 512])
        zx = sb("s_zx", [128, 4, 512])
        aex = sb("s_aex", [128, 512])
        y4 = sb("s_y4", [128, NCH, 512], F32)
        y4bf = sb("s_y4b", [128, NCH, 512])
        scT = sb("s_scT", [8, 512], F32)
        alpaT = sb("s_alpaT", [128, 4, 8], F32)
        expT = sb("s_expT", [128, 4, 8], F32)
        anT = sb("s_anT", [128, 4, 8])
        alphaR = sb("s_alphaR", [8, 512])
        negmax4 = sb("s_ngm", [128, 4], F32)
        lsum4 = sb("s_ls4", [128, 4], F32)
        b1col = sb("s_b1", [128, NCH], F32)
        w2t = sb("s_w2", [128, NCH, 8])
        b2col = sb("s_b2", [8, 1], F32)
        E4t = sb("s_E4", [8, 4, 128])
        I128f = sb("s_If", [128, 128], F32)
        I128b = sb("s_Ib", [128, 128])
        ones128 = sb("s_o128", [128, 128])
        epscol = sb("s_eps", [128, 1], F32)
        xr = sb("s_xr", [128, 2, 512], F32)
        orot = sb("s_or", [128, 2, 512], F32)
        sq = [sb(f"s_sq{i}", [128, 512]) for i in range(2)]
        mus = sb("s_mus", [128, 512], F32)
        ex2s = sb("s_ex2s", [128, 512], F32)
        vars_ = sb("s_vars", [128, 512], F32)
        rstd = sb("s_rstd", [128, 512], F32)
        nmr = sb("s_nmr", [128, 512], F32)
        t1 = sb("s_t1", [128, 512], F32)

        pool = PsumPool(p, st, n=6, shape=(128, 512))
        psn8 = st.enter_context(nc.psum_tensor("psn8", [8, 512], F32))
        psb8 = st.enter_context(nc.psum_tensor("psb8", [8, 512], BF16))

        for dst, src in ((b1col, d_b1), (b2col, d_b2), (I128f, d_I),
                         (I128b, d_Ib), (ones128, d_ones128), (epscol, d_eps)):
            p.dma("sync", dst[:], src, sem="qinit")
        p.dma("gpsimd", w2t[:], d_w2, sem="qgw2")
        p.dma("sync", E4t[:], d_E4.rearrange("c e p -> e c p"), sem="qinit")
        t_init = p.qtot("qinit")
        t_gw2 = p.qtot("qgw2")

        w1 = WStream(p, st, "w1", [128, NCH, 128])
        va = WStream(p, st, "va", [128, NCH, 128])
        ub = WStream(p, st, "ub", [128, 4, 128])

        store_toks = {}
        xr_use = {}
        for tb in range(2):
            tsl = slice(tb * 512, tb * 512 + 512)
            ld_x = p.dma("gpsimd", x3bf[:], d_x3.rearrange("c p t -> p c t")[:, :, tsl],
                         sem=f"qx{tb % 2}",
                         waits=[("mm", p.cnt.get("mm", 0))])
            for oc in range(NCH):
                j, buf, ldtok = w1.load(d_w14[oc])
                i, ps, war = pool.acquire()
                mmt = mm_group(p, ps[:], [(buf[:, ic, :], x3bf[:, ic, :]) for ic in range(NCH)],
                               waits=war + [ldtok, ld_x, t_init])
                w1.used(j, mmt)
                ev = p.op("scalar", lambda e, oc=oc, ps=ps: e.activation(
                    hbuf[:, oc, :], ps[:], AF.Relu, bias=b1col[:, oc:oc + 1]),
                    waits=[mmt])
                pool.release(i, ev)
            t_h = ("escalar", p.cnt["escalar"])
            mmt = mm_group(p, psn8[:], [(w2t[:, ic, :], hbuf[:, ic, :]) for ic in range(NCH)],
                           waits=[t_h, t_init, t_gw2, ("escalar", p.cnt["escalar"])])
            t_sc = p.op("scalar", lambda e: e.activation(scT[:], psn8[:], AF.Identity,
                                                         bias=b2col[:]), waits=[mmt])
            tr_toks = []
            for blk in range(4):
                i, ps, war = pool.acquire()
                tt_ = p.add("tensor", lambda e, blk=blk, ps=ps: e.transpose(
                    ps[:, 0:8], scT[:, blk * 128:(blk + 1) * 128], I128f[0:8, 0:8]),
                    waits=war + [t_sc, t_init], inc=("mm", 1))
                ev = p.op("vector", lambda e, blk=blk, ps=ps: e.tensor_copy(
                    alpaT[:, blk, :], ps[:, 0:8]),
                    waits=[tt_, ("escalar", p.cnt["escalar"])])
                pool.release(i, ev)
                tr_toks.append(ev)
            t_ngm = p.op("vector", lambda e: e.reduce_max(
                negmax4[:], alpaT[:], axis=AX.X, negate=True),
                waits=[tr_toks[-1]])
            exp_toks = []
            for blk in range(4):
                exp_toks.append(p.op("scalar", lambda e, blk=blk: e.activation(
                    expT[:, blk, :], alpaT[:, blk, :], AF.Exp, bias=negmax4[:, blk:blk + 1]),
                    waits=[t_ngm]))
            t_ls = p.op("vector", lambda e: e.reduce_sum(
                lsum4[:], expT[:], axis=AX.X), waits=[exp_toks[-1]])
            t_lr = p.op("vector", lambda e: e.reciprocal(lsum4[:], lsum4[:]), waits=[t_ls])
            an_toks = []
            for blk in range(4):
                an_toks.append(p.op("vector", lambda e, blk=blk: e.tensor_scalar_mul(
                    anT[:, blk, :], expT[:, blk, :], lsum4[:, blk:blk + 1]),
                    waits=[t_lr, ("mm", p.cnt["mm"])]))
            trb_toks = []
            for blk in range(4):
                tt_ = p.add("tensor", lambda e, blk=blk: e.transpose(
                    psb8[:, blk * 128:(blk + 1) * 128], anT[:, blk, :], I128b[:]),
                    waits=[an_toks[blk], ("evector", p.cnt["evector"]), t_init],
                    inc=("mm", 1))
                trb_toks.append(tt_)
            t_aR = p.op("vector", lambda e: e.tensor_copy(alphaR[:], psb8[:]),
                        waits=[trb_toks[-1]])
            for krc in range(4):
                j, buf, ldtok = va.load(d_va4[krc])
                i, ps, war = pool.acquire()
                mmv = mm_group(p, ps[:], [(buf[:, ic, :], x3bf[:, ic, :]) for ic in range(NCH)],
                               waits=war + [ldtok])
                va.used(j, mmv)
                i2, ps2, war2 = pool.acquire()
                mme = mm_group(p, ps2[:], [(E4t[:, krc, :], alphaR[:])],
                               waits=war2 + [t_aR, t_init])
                eva = p.op("vector", lambda e, ps2=ps2: e.tensor_copy(aex[:], ps2[:]),
                           waits=[mme])
                pool.release(i2, eva)
                evz = p.op("vector", lambda e, krc=krc, ps=ps: e.tensor_tensor(
                    zx[:, krc, :], ps[:], aex[:], OP.mult), waits=[mmv, eva])
                pool.release(i, evz)
            t_zx = ("evector", p.cnt["evector"])
            for oc in range(NCH):
                j, buf, ldtok = ub.load(d_ub4[oc])
                ldr = p.dma("sync", xr[:, oc % 2, :], d_x3[oc, :, tsl],
                            waits=[xr_use.get(oc % 2)], sem=f"qxr{oc % 2}")
                i, ps, war = pool.acquire()
                mmt = mm_group(p, ps[:], [(buf[:, k_, :], zx[:, k_, :]) for k_ in range(4)],
                               waits=war + [ldtok, t_zx])
                ub.used(j, mmt)
                ev = p.op("vector", lambda e, oc=oc, ps=ps: e.tensor_tensor(
                    y4[:, oc, :], ps[:], xr[:, oc % 2, :], OP.add), waits=[mmt, ldr])
                xr_use[oc % 2] = ev
                pool.release(i, ev)
                p.op("vector", lambda e, oc=oc: e.tensor_copy(y4bf[:, oc, :], y4[:, oc, :]))
            def out_cb(c, t_n, rstd_, nmr_, tb=tb, tsl=tsl):
                w = [t_n, store_toks.get(c % 2)]
                tt1 = p.op("vector", lambda e, c=c: e.tensor_tensor(
                    t1[:], y4[:, c, :], rstd_[:, :512], OP.mult), waits=w)
                tt2 = p.op("vector", lambda e, c=c: e.tensor_tensor(
                    orot[:, c % 2, :], t1[:], nmr_[:, :512], OP.add), waits=[tt1])
                stok = p.dma("sync", d_out[c, :, tsl], orot[:, c % 2, :],
                             waits=[tt2], sem=f"qst{c % 2}")
                store_toks[c % 2] = stok
                return stok

            emit_ln(p, pool, lambda c: y4bf[:, c, :], NCH, 512,
                    sq, mus, ex2s, vars_, rstd, nmr, epscol[:], ones128[:],
                    out_cb, stat_waits=[t_init, ("evector", p.cnt["evector"]),
                                        ("mm", p.cnt["mm"])])

        p.add("sync", lambda e: e.nop(),
              waits=[p.qtot(s) for s in sorted(p.cnt) if s.startswith("q")])
        emit_prog(p, st)
    return nc


def _run_many(nc, in_maps):
    res = run_bass_kernel_spmd(nc, in_maps, core_ids=list(range(8)))
    return res.results


_PROGS = {}


def _prog(name):
    if name not in _PROGS:
        _PROGS[name] = {
            "A": build_launchA, "B": build_launchB, "C": build_launchC,
            "G": build_launchG, "F": build_launchF,
        }[name]()
    return _PROGS[name]


def barrier(p):
    snap = [(s, v) for s, v in sorted(p.cnt.items())]
    for eng in ENGINES:
        p.add(eng, lambda e: e.nop(), waits=snap)


def emit_phaseA(p, st, nc, io, qTs):
    sb = lambda name, shape, dt=BF16: st.enter_context(nc.sbuf_tensor(name, list(shape), dt))
    d_x, d_qkv = io["x"], io["qkv"]
    qkv_r = d_qkv.rearrange("(c p) o -> p c o", p=128)
    xbf = sb("a_xbf", [128, NCH, TOK])
    xnt = sb("a_xnt", [128, NCH, TOK])
    frot = sb("a_frot", [128, 8, 512], F32)
    sq = [sb(f"a_sq{i}", [128, 512]) for i in range(2)]
    mus = sb("a_mus", [128, 512], F32)
    ex2s = sb("a_ex2s", [128, 512], F32)
    vars_ = sb("a_vars", [128, 512], F32)
    rstd = sb("a_rstd", [128, 512], F32)
    nmr = sb("a_nmr", [128, 512], F32)
    t1 = sb("a_t1", [128, 512], F32)
    gam = sb("a_gam", [128, NCH], F32)
    bet = sb("a_bet", [128, NCH], F32)
    ones128 = sb("a_o128", [128, 128])
    epscol = sb("a_eps", [128, 1], F32)
    pool = PsumPool(p, st, n=8, shape=(128, 512), name="a_ps")

    p.dma("gpsimd", xbf[:], d_x.rearrange("c p t -> p c t"), sem="qagx")
    p.dma("sync", gam[:], io["gamma"], sem="qainit")
    p.dma("sync", bet[:], io["beta"], sem="qainit")
    p.dma("sync", ones128[:], io["ones128"], sem="qainit")
    p.dma("sync", epscol[:], io["epscol"], sem="qainit")
    t_init = p.qtot("qainit")
    t_init_g = p.qtot("qagx")

    for tb in range(2):
        t0 = tb * 512

        def out_cb(c, t_n, rstd_, nmr_, tb=tb, t0=t0):
            w = [t_n, t_init]
            tt1 = p.op("vector", lambda e, c=c: e.tensor_tensor(
                t1[:], xbf[:, c, t0:t0 + 512], rstd_[:, :512], OP.mult), waits=w)
            tt2 = p.op("vector", lambda e: e.tensor_tensor(
                t1[:], t1[:], nmr_[:, :512], OP.add), waits=[tt1])
            return p.op("vector", lambda e, c=c: e.tensor_scalar(
                xnt[:, c, t0:t0 + 512], t1[:], gam[:, c:c + 1], bet[:, c:c + 1],
                OP.mult, OP.add), waits=[tt2])

        emit_ln(p, pool, lambda c, t0=t0: xbf[:, c, t0:t0 + 512], NCH, 512,
                sq, mus, ex2s, vars_, rstd, nmr, epscol[:], ones128[:],
                out_cb, stat_waits=[t_init, t_init_g, ("mm", p.cnt.get("mm", 0))])

    t_xnt = ("evector", p.cnt["evector"])

    rg = [[0, 1], [2, 3], [4, 5], [6, 7]]
    wq = WStream(p, st, "aw", [128, NCH, 128])
    slot_use = {}
    for oc in range(NCH):
        j, buf, ldtok = wq.load(qkv_r[:, :, D + oc * 128:D + (oc + 1) * 128])
        last_use = None
        for tb in range(2):
            t0 = tb * 512
            i, ps, war = pool.acquire()
            tiles = [(buf[:, ic, :], xnt[:, ic, t0:t0 + 512]) for ic in range(NCH)]
            mmtok = mm_group(p, ps[:], tiles, waits=war + [ldtok, t_xnt])
            last_use = mmtok
            slot = (oc * 2 + tb) % 8
            ev = p.op("vector", lambda e, slot=slot, ps=ps: e.tensor_copy(
                frot[:, slot, :], ps[:]), waits=[mmtok, slot_use.get(slot)])
            pool.release(i, ev)
            stok = p.dma("sync", io["agk_in"][oc * 128:(oc + 1) * 128, t0:t0 + 512],
                         frot[:, slot, :], waits=[ev], sem=f"qas{slot}")
            slot_use[slot] = stok
        wq.used(j, last_use)
    p.add("gpsimd", lambda e: e.collective_compute(
        "AllGather", mybir.AluOpType.bypass, replica_groups=rg,
        ins=[io["agk_in"].tensor.ap().opt()], outs=[io["agk_out"].tensor.ap().opt()]),
        waits=[p.qtot(s) for s in sorted(p.cnt) if s.startswith("qas")],
        inc=("cck", 1))

    wkv = WStream(p, st, "awv", [128, 512])
    for ob in range(4):
        o0 = 2 * D + ob * 512
        pss = [pool.acquire() for _ in range(8)]
        mmtok = None
        for ic in range(NCH):
            j, buf, ldtok = wkv.load(qkv_r[:, ic, o0:o0 + 512])

            def emit_mm(e, ic=ic, buf=buf, pss=pss):
                ins = None
                for tt in range(8):
                    ins = e.matmul(pss[tt][1][:], xnt[:, ic, tt * 128:(tt + 1) * 128],
                                   buf[:], start=(ic == 0), stop=(ic == NCH - 1),
                                   skip_group_check=True)
                return ins
            w = [ldtok, t_xnt] + (sum((pw for _, _, pw in pss), []) if ic == 0 else [])
            mmtok = p.add("tensor", emit_mm, waits=w, inc=("mm", 1))
            wkv.used(j, mmtok)
        for tt in range(8):
            i, ps, _ = pss[tt]
            ev = p.op("vector", lambda e, tt=tt, ps=ps: e.tensor_copy(frot[:, tt, :], ps[:]),
                      waits=[mmtok, slot_use.get(tt)])
            pool.release(i, ev)
            stok = p.dma("sync", io["agv_in"][tt * 128:(tt + 1) * 128, ob * 512:ob * 512 + 512],
                         frot[:, tt, :], waits=[ev], sem=f"qas{tt}")
            slot_use[tt] = stok
    p.add("gpsimd", lambda e: e.collective_compute(
        "AllGather", mybir.AluOpType.bypass, replica_groups=rg,
        ins=[io["agv_in"].tensor.ap().opt()], outs=[io["agv_out"].tensor.ap().opt()]),
        waits=[p.qtot(s) for s in sorted(p.cnt) if s.startswith("qas")],
        inc=("ccv", 1))

    for oc in range(NCH):
        j, buf, ldtok = wq.load(qkv_r[:, :, oc * 128:(oc + 1) * 128])
        last_use = None
        for tb in range(2):
            t0 = tb * 512
            i, ps, war = pool.acquire()
            tiles = [(buf[:, ic, :], xnt[:, ic, t0:t0 + 512]) for ic in range(NCH)]
            mmtok = mm_group(p, ps[:], tiles, waits=war + [ldtok, t_xnt])
            last_use = mmtok
            ev = p.op("vector", lambda e, oc=oc, t0=t0, ps=ps: e.tensor_copy(
                qTs[:, oc, t0:t0 + 512], ps[:]), waits=[mmtok])
            pool.release(i, ev)
        wq.used(j, last_use)
    return [p.qtot("cck"), p.qtot("ccv")]
def emit_phaseAG(p, io):
    waits = [p.qtot(s) for s in sorted(p.cnt) if s.startswith("qas")]
    rg = [[0, 1], [2, 3], [4, 5], [6, 7]]
    p.add("gpsimd", lambda e: e.collective_compute(
        "AllGather", mybir.AluOpType.bypass, replica_groups=rg,
        ins=[io["agk_in"].tensor.ap().opt()], outs=[io["agk_out"].tensor.ap().opt()]),
        waits=waits, inc=("cck", 1))
    p.add("gpsimd", lambda e: e.collective_compute(
        "AllGather", mybir.AluOpType.bypass, replica_groups=rg,
        ins=[io["agv_in"].tensor.ap().opt()], outs=[io["agv_out"].tensor.ap().opt()]),
        inc=("ccv", 1))
    return [p.qtot("cck"), p.qtot("ccv")]


def emit_phaseB(p, st, nc, io, qTs, t_ag):
    sb = lambda name, shape, dt=BF16: st.enter_context(nc.sbuf_tensor(name, list(shape), dt))
    d_kT = io["agk_out"].rearrange("(h c p) t -> h p c t", h=2, p=128)
    d_v = io["agv_out"].rearrange("(kt p) d -> p kt d", p=128)
    d_x, d_x1 = io["x"], io["x1s"]

    qT = qTs
    PT = sb("b_PT", [128, NCH, 512])
    AVn = sb("b_AVn", [128, NCH, 512])
    y1pre = sb("b_y1p", [128, NCH, 512], F32)
    y1bf = sb("b_y1b", [128, NCH, 512])
    x1rot = sb("b_x1r", [128, 4, 512], F32)
    xr = sb("b_xr", [128, 2, 512], F32)
    masks = sb("b_mask", [128, 16, 512])
    ones128 = sb("b_o128", [128, 128])
    ones_col = sb("b_oc", [128, 1])
    ones1 = sb("b_o1", [1, 128])
    epscol = sb("b_eps", [128, 1], F32)
    sqn = [sb(f"b_sqn{i}", [128, D]) for i in range(2)]
    qn2 = sb("b_qn2", [1, TOK], F32)
    kn2 = sb("b_kn2", [1, D], F32)
    sc1 = sb("b_sc1", [1, 1], F32)
    sc2 = sb("b_sc2", [1, 1], F32)
    sc2b = sb("b_sc2b", [1, 1], BF16)
    biascol = sb("b_bias", [128, 1], F32)
    linv = sb("b_linv", [128, 512], F32)
    sq = [sb(f"b_sq{i}", [128, 512]) for i in range(2)]
    mus = sb("b_mus", [128, 512], F32)
    ex2s = sb("b_ex2s", [128, 512], F32)
    vars_ = sb("b_vars", [128, 512], F32)
    rstd = sb("b_rstd", [128, 512], F32)
    nmr = sb("b_nmr", [128, 512], F32)
    t1 = sb("b_t1", [128, 512], F32)

    pool = PsumPool(p, st, n=6, shape=(128, 512), name="b_ps")
    psn = [st.enter_context(nc.psum_tensor(f"b_psn{i}", [1, 512], F32)) for i in range(2)]

    p.dma("sync", ones128[:], io["ones128"], sem="qbinit")
    p.dma("sync", ones_col[:], io["ones_col"], sem="qbinit")
    p.dma("sync", ones1[:], io["ones1"], sem="qbinit")
    p.dma("sync", epscol[:], io["epscol"], sem="qbinit")
    t_init = p.qtot("qbinit")

    kst = WStream(p, st, "bk", [128, NCH, 128], queue="gpsimd")
    vst = WStream(p, st, "bv", [128, NCH, 128], queue="gpsimd")
    wo = WStream(p, st, "bw", [128, NCH, 128], queue="gpsimd")

    def ksrc(kc):
        half = kc // 8
        c0 = (kc % 8) * 128
        return d_kT[half, :, :, c0:c0 + 128]

    qn_toks = []
    for dc in range(NCH):
        sw = [("mm", p.cnt["mm"])] if dc == 0 else [qn_toks[dc - 2]] if dc >= 2 else []
        sqtok = p.op("scalar", lambda e, dc=dc: e.activation(
            sqn[dc % 2][:, 0:TOK], qT[:, dc, :], AF.Square), waits=sw)

        def emm(e, dc=dc):
            e.matmul(psn[0][:], ones_col[:], sqn[dc % 2][:, 0:512],
                     start=(dc == 0), stop=(dc == NCH - 1), skip_group_check=True)
            return e.matmul(psn[1][:], ones_col[:], sqn[dc % 2][:, 512:1024],
                            start=(dc == 0), stop=(dc == NCH - 1), skip_group_check=True)
        mw = [sqtok] + ([t_init] if dc == 0 else [])
        qn_toks.append(p.add("tensor", emm, waits=mw, inc=("mm", 1)))
    e1 = p.op("vector", lambda e: e.tensor_copy(qn2[:, 0:512], psn[0][:]), waits=[qn_toks[-1]])
    e2 = p.op("vector", lambda e: e.tensor_copy(qn2[:, 512:1024], psn[1][:]), waits=[qn_toks[-1]])
    t_qmax = p.op("vector", lambda e: e.reduce_max(sc1[:], qn2[:], axis=AX.X), waits=[e1, e2])

    kn_toks = [t_qmax]
    for kc in range(NCH):
        j, kbuf, ldtok = kst.load(ksrc(kc), extra_waits=t_ag if kc < 2 else ())
        sqtok = p.op("scalar", lambda e, kbuf=kbuf, kc=kc: e.activation(
            sqn[kc % 2][:, 0:D], kbuf[:].rearrange("p c k -> p (c k)"), AF.Square),
            waits=[ldtok] + ([qn_toks[-1]] if kc < 2 else []))

        def emm(e, kc=kc):
            ins = None
            for c in range(NCH):
                ins = e.matmul(psn[0][:, 0:128], ones_col[:],
                               sqn[kc % 2][:, c * 128:(c + 1) * 128],
                               start=(c == 0), stop=(c == NCH - 1), skip_group_check=True)
            return ins
        mmt = p.add("tensor", emm, waits=[sqtok, kn_toks[-1]], inc=("mm", 1))
        kst.used(j, mmt)
        ev = p.op("vector", lambda e, kc=kc: e.tensor_copy(
            kn2[:, kc * 128:(kc + 1) * 128], psn[0][:, 0:128]), waits=[mmt])
        kn_toks.append(ev)
    t_kmax = p.op("vector", lambda e: e.reduce_max(sc2[:], kn2[:], axis=AX.X), waits=[kn_toks[-1]])
    t_m2 = p.op("vector", lambda e: e.tensor_tensor(sc1[:], sc1[:], sc2[:], OP.mult),
                waits=[t_qmax, t_kmax])
    t_m = p.op("scalar", lambda e: e.activation(sc1[:], sc1[:], AF.Sqrt), waits=[t_m2])
    t_b0 = p.op("scalar", lambda e: e.mul(sc2b[:], sc1[:], -1.0 / math.sqrt(D)), waits=[t_m])
    ib, psb, warb = pool.acquire()
    t_bb = mm_group(p, psb[:, 0:1], [(ones1[:], sc2b[:])], waits=warb + [t_b0, t_init])
    t_bias = p.op("scalar", lambda e, psb=psb: e.copy(biascol[:], psb[:, 0:1]), waits=[t_bb])
    pool.release(ib, t_bias)

    scale = 1.0 / math.sqrt(D)
    store_toks = {}
    xr_use = {}
    mask_use = None
    for qb in range(2):
        qsl = slice(qb * 512, qb * 512 + 512)
        live = list(range(NCH))
        ld_m = p.dma("sync", masks[:], io["masks"][qb].rearrange("m p q -> p m q"),
                     waits=[mask_use], sem="qbmask")
        for kc in live:
            j, kbuf, ldtok = kst.load(ksrc(kc))
            i, ps, war = pool.acquire()
            tiles = [(kbuf[:, dc, :], qT[:, dc, qsl]) for dc in range(NCH)]
            mmt = mm_group(p, ps[:], tiles, waits=war + [ldtok])
            kst.used(j, mmt)
            t_exp = p.op("scalar", lambda e, kc=kc, ps=ps: e.activation(
                PT[:, kc, :], ps[:], AF.Exp, bias=biascol[:], scale=scale),
                waits=[mmt, t_bias])
            lastPT = p.op("vector", lambda e, kc=kc: e.tensor_tensor(
                PT[:, kc, :], PT[:, kc, :], masks[:, kc, :], OP.mult),
                waits=[t_exp, ld_m])
            pool.release(i, lastPT)
        mask_use = ("evector", p.cnt["evector"])
        t_allPT = [("escalar", p.cnt["escalar"]), ("evector", p.cnt["evector"])]
        i, ps, war = pool.acquire()
        mmt = mm_group(p, ps[:], [(ones128[:], PT[:, kc, :]) for kc in live],
                       waits=war + t_allPT + [t_init])
        t_rec = p.op("vector", lambda e, ps=ps: e.reciprocal(linv[:], ps[:]), waits=[mmt])
        pool.release(i, t_rec)
        for dv in range(NCH):
            j, vbuf, ldtok = vst.load(d_v[:, :, dv * 128:(dv + 1) * 128],
                                      extra_waits=t_ag if qb == 0 and dv < 2 else ())
            i, ps, war = pool.acquire()
            tiles = [(vbuf[:, kc, :], PT[:, kc, :]) for kc in live]
            mmt = mm_group(p, ps[:], tiles, waits=war + [ldtok])
            vst.used(j, mmt)
            ev = p.op("vector", lambda e, dv=dv, ps=ps: e.tensor_tensor(
                AVn[:, dv, :], ps[:], linv[:], OP.mult), waits=[mmt, t_rec])
            pool.release(i, ev)
        t_avn = [("evector", p.cnt["evector"])]
        for oc in range(NCH):
            j, wbuf, ldtok = wo.load(io["o_proj"].rearrange("(c p) o -> p c o", p=128)
                                     [:, :, oc * 128:(oc + 1) * 128])
            ld_x_tok = p.dma("sync", xr[:, oc % 2, :], d_x[oc, :, qsl],
                             waits=[xr_use.get(oc % 2)], sem=f"qbxr{oc % 2}")
            i, ps, war = pool.acquire()
            tiles = [(wbuf[:, ic, :], AVn[:, ic, :]) for ic in range(NCH)]
            mmt = mm_group(p, ps[:], tiles, waits=war + [ldtok] + t_avn)
            wo.used(j, mmt)
            ev = p.op("vector", lambda e, oc=oc, ps=ps: e.scalar_tensor_tensor(
                y1pre[:, oc, :], xr[:, oc % 2, :], 2.0, ps[:], OP.mult, OP.add),
                waits=[mmt, ld_x_tok])
            xr_use[oc % 2] = ev
            pool.release(i, ev)
            p.op("vector", lambda e, oc=oc: e.tensor_copy(y1bf[:, oc, :], y1pre[:, oc, :]))

        def out_cb(c, t_n, rstd_, nmr_, qb=qb, qsl=qsl):
            w = [t_n, store_toks.get(c % 4)]
            tt1 = p.op("vector", lambda e, c=c: e.tensor_tensor(
                t1[:], y1pre[:, c, :], rstd_[:, :512], OP.mult), waits=w)
            tt2 = p.op("vector", lambda e, c=c: e.tensor_tensor(
                x1rot[:, c % 4, :], t1[:], nmr_[:, :512], OP.add), waits=[tt1])
            stok = p.dma("sync", d_x1[c, :, qsl], x1rot[:, c % 4, :],
                         waits=[tt2], sem=f"qbst{c % 4}")
            store_toks[c % 4] = stok
            return stok

        emit_ln(p, pool, lambda c: y1bf[:, c, :], NCH, 512,
                sq, mus, ex2s, vars_, rstd, nmr, epscol[:], ones128[:],
                out_cb, stat_waits=[t_init, ("evector", p.cnt["evector"]),
                                    ("mm", p.cnt["mm"])])


def emit_phaseC(p, st, nc, io):
    sb = lambda name, shape, dt=BF16: st.enter_context(nc.sbuf_tensor(name, list(shape), dt))
    d_x1, d_x2 = io["x1s"], io["x2s"]
    x1bf = sb("c_x1bf", [128, NCH, 512])
    zt = x1bf
    tt = sb("c_tt", [128, NCH, 512])
    mid = sb("c_mid", [128, 64, 512])
    tmp16 = sb("c_tmp16", [128, 16, 512])
    y2 = sb("c_y2", [128, NCH, 512], F32)
    g1f = sb("c_g1f", [128, 512], F32)
    silb = sb("c_silb", [128, 512])
    xr = sb("c_xr", [128, 2, 512], F32)
    x2rot = sb("c_x2r", [128, 2, 512], F32)
    C128 = sb("c_C128", [128, 128])
    suug = sb("c_suug", [128, NCH], F32)
    sudp = sb("c_sudp", [128, 64], F32)
    wsu = sb("c_wsu", [128, 64], F32)
    wsg = sb("c_wsg", [128, 64], F32)
    wsdp = sb("c_wsdp", [128, NCH], F32)
    wsud = sb("c_wsud", [128, 64], F32)
    ones128 = sb("c_o128", [128, 128])
    epscol = sb("c_eps", [128, 1], F32)
    sq = [sb(f"c_sq{i}", [128, 512]) for i in range(2)]
    mus = sb("c_mus", [128, 512], F32)
    ex2s = sb("c_ex2s", [128, 512], F32)
    vars_ = sb("c_vars", [128, 512], F32)
    rstd = sb("c_rstd", [128, 512], F32)
    nmr = sb("c_nmr", [128, 512], F32)
    t1 = sb("c_t1", [128, 512], F32)

    pool = PsumPool(p, st, n=6, shape=(128, 512), name="c_ps")
    for dst, src in ((C128, io["C128"]), (suug, io["suug"]), (sudp, io["sudp"]),
                     (wsu, io["wsu"]), (wsg, io["wsg"]), (wsdp, io["wsdp"]),
                     (ones128, io["ones128"]), (epscol, io["epscol"])):
        p.dma("sync", dst[:], src, sem="qcinit")
    t_init = p.qtot("qcinit")
    t_wsud = p.op("vector", lambda e: e.tensor_tensor(wsud[:], wsu[:], sudp[:], OP.mult),
                  waits=[t_init])

    wA = WStream(p, st, "cwA", [128, NCH, 128])
    wB = WStream(p, st, "cwB", [128, NCH, 128])
    wD = WStream(p, st, "cwD", [128, 16, 128])

    store_toks = {}
    xr_use = {}
    for tb in range(2):
        tsl = slice(tb * 512, tb * 512 + 512)
        ld_x = p.dma("gpsimd", x1bf[:], d_x1.rearrange("c p t -> p c t")[:, :, tsl],
                     sem=f"qcx{tb % 2}",
                     waits=[("evector", p.cnt.get("evector", 0))])
        zt_toks = []
        for c in range(NCH):
            zt_toks.append(p.op("vector", lambda e, c=c: e.tensor_scalar_mul(
                zt[:, c, :], x1bf[:, c, :], suug[:, c:c + 1]),
                waits=[ld_x, t_init, ("mm", p.cnt.get("mm", 0))] if c == 0 else []))
        t_zt = zt_toks[-1]
        for oc in range(NCH):
            j, buf, ldtok = wA.load(io["H4"][oc])
            i, ps, war = pool.acquire()
            mmt = mm_group(p, ps[:], [(buf[:, ic, :], zt[:, ic, :]) for ic in range(NCH)],
                           waits=war + [ldtok, t_zt])
            wA.used(j, mmt)
            ev = p.op("vector", lambda e, oc=oc, ps=ps: e.tensor_scalar_mul(
                tt[:, oc, :], ps[:], 1.0 / math.sqrt(D)), waits=[mmt])
            pool.release(i, ev)
        t_tt = ("evector", p.cnt["evector"])
        for oc in range(64):
            ju, bu, ldu = wA.load(io["up4"][oc])
            jg, bg, ldg = wB.load(io["gt4"][oc])
            iu, psu, waru = pool.acquire()
            mmu = mm_group(p, psu[:], [(bu[:, ic, :], tt[:, ic, :]) for ic in range(NCH)],
                           waits=waru + [ldu, t_tt])
            wA.used(ju, mmu)
            ig, psg, warg = pool.acquire()
            mmg = mm_group(p, psg[:], [(bg[:, ic, :], tt[:, ic, :]) for ic in range(NCH)],
                           waits=warg + [ldg, t_tt])
            wB.used(jg, mmg)
            tg1 = p.op("vector", lambda e, oc=oc, psg=psg: e.tensor_scalar_mul(
                g1f[:], psg[:], wsg[:, oc:oc + 1]), waits=[mmg])
            pool.release(ig, tg1)
            tsil = p.op("scalar", lambda e: e.activation(silb[:], g1f[:], AF.Silu),
                        waits=[tg1])
            tmid = p.op("vector", lambda e, oc=oc, psu=psu: e.scalar_tensor_tensor(
                mid[:, oc, :], psu[:], wsud[:, oc:oc + 1], silb[:], OP.mult, OP.mult),
                waits=[mmu, tsil, t_wsud])
            pool.release(iu, tmid)
        for a in range(64):
            i, ps, war = pool.acquire()
            mmt = mm_group(p, ps[:], [(C128[:], mid[:, a, :])],
                           waits=war + ([t_init, ("evector", p.cnt["evector"])] if a == 0 else []))
            ev = p.op("vector", lambda e, a=a, ps=ps: e.tensor_scalar_mul(
                mid[:, a, :], ps[:], 1.0 / math.sqrt(INTER)), waits=[mmt])
            pool.release(i, ev)
        midf = mid[:].rearrange("p c e -> p (c e)")
        tmpf = tmp16[:].rearrange("p c e -> p (c e)")
        for s_ in range(6):
            dd = 1 << s_
            views = []
            if 32 // dd >= 2:
                gh = (32 // dd) // 2
                v_ = midf.rearrange("p (g two dd e) -> p g two dd e", two=2, dd=dd, e=512)
                tv_ = tmpf.rearrange("p (g dd e) -> p g dd e", dd=dd, e=512)
                for h in range(2):
                    views.append((v_[:, h * gh:(h + 1) * gh, 0, :, :],
                                  v_[:, h * gh:(h + 1) * gh, 1, :, :],
                                  tv_[:, 0:gh, :, :]))
            else:
                v_ = midf.rearrange("p (two dd e) -> p two dd e", dd=32, e=512)
                tv_ = tmpf.rearrange("p (dd e) -> p dd e", e=512)
                for h in range(2):
                    views.append((v_[:, 0, h * 16:(h + 1) * 16, :],
                                  v_[:, 1, h * 16:(h + 1) * 16, :],
                                  tv_[:, 0:16, :]))
            for lo, hi, tv in views:
                p.op("vector", lambda e, lo=lo, hi=hi, tv=tv: e.tensor_tensor(tv, lo, hi, OP.add))
                p.op("vector", lambda e, lo=lo, hi=hi: e.tensor_tensor(hi, lo, hi, OP.subtract))
                p.op("vector", lambda e, lo=lo, tv=tv: e.tensor_copy(lo, tv))
        t_bfly = ("evector", p.cnt["evector"])
        for oc in range(NCH):
            ldr = p.dma("sync", xr[:, oc % 2, :], d_x1[oc, :, tsl],
                        waits=[xr_use.get(oc % 2)], sem=f"qcxr{oc % 2}")
            i, ps, war = pool.acquire()
            mmq = None
            for qtr in range(4):
                jq, bq, ldq = wD.load(io["dn4"][oc, :, 16 * qtr:16 * qtr + 16, :])
                w = [ldq]
                if qtr == 0:
                    w += war + [t_bfly]
                mmq = mm_group(
                    p, ps[:],
                    [(bq[:, ic, :], mid[:, 16 * qtr + ic, :]) for ic in range(16)],
                    waits=w, start=(qtr == 0), stop=(qtr == 3))
                wD.used(jq, mmq)
            ev = p.op("vector", lambda e, oc=oc, ps=ps: e.scalar_tensor_tensor(
                y2[:, oc, :], ps[:], wsdp[:, oc:oc + 1], xr[:, oc % 2, :], OP.mult, OP.add),
                waits=[mmq, ldr])
            xr_use[oc % 2] = ev
            pool.release(i, ev)
        for c in range(NCH):
            p.op("vector", lambda e, c=c: e.tensor_copy(tt[:, c, :], y2[:, c, :]),
                 waits=[("mm", p.cnt["mm"])] if c == 0 else [])
        t_cast = ("evector", p.cnt["evector"])

        def out_cb(c, t_n, rstd_, nmr_, tb=tb, tsl=tsl):
            w = [t_n, store_toks.get(c % 2)]
            tt1 = p.op("vector", lambda e, c=c: e.tensor_tensor(
                t1[:], y2[:, c, :], rstd_[:, :512], OP.mult), waits=w)
            tt2 = p.op("vector", lambda e, c=c: e.tensor_tensor(
                x2rot[:, c % 2, :], t1[:], nmr_[:, :512], OP.add), waits=[tt1])
            stok = p.dma("sync", d_x2[c, :, tsl], x2rot[:, c % 2, :],
                         waits=[tt2], sem=f"qcst{c % 2}")
            store_toks[c % 2] = stok
            return stok

        emit_ln(p, pool, lambda c: tt[:, c, :], NCH, 512,
                sq, mus, ex2s, vars_, rstd, nmr, epscol[:], ones128[:],
                out_cb, stat_waits=[t_init, t_cast, ("mm", p.cnt["mm"])])


def emit_phaseG(p, st, nc, io):
    sb = lambda name, shape, dt=BF16: st.enter_context(nc.sbuf_tensor(name, list(shape), dt))
    d_x2, d_x3 = io["x2s"], io["x3s"]
    wqT_r = io["wqT"].rearrange("(c p) o -> p c o", p=128)
    wkT_r = io["wkT"].rearrange("(c p) o -> p c o", p=128)
    wvT_r = io["wvT"].rearrange("(c p) o -> p c o", p=128)

    x2bf = sb("g_x2bf", [128, NCH, TOK])
    attn = x2bf
    qT = sb("g_qT", [128, NCH, TOK])
    kT = sb("g_kT", [128, NCH, TOK])
    vT = sb("g_vT", [128, NCH, TOK])
    y3pre = sb("g_y3p", [128, NCH, 512], F32)
    prodb = [sb(f"g_prod{i}", [128, 512]) for i in range(2)]
    sqs = prodb[0]
    acc = [sb(f"g_acc{i}", [128, 512], F32) for i in range(2)]
    tmp2 = [sb(f"g_tmpm{i}", [128, 512]) for i in range(4)]
    wbb = [sb(f"g_wbb{i}", [128, 512]) for i in range(4)]
    PL = [sb(f"g_PL{i}", [16, 512], F32) for i in range(2)]
    PLb = [sb(f"g_PLb{i}", [16, 512]) for i in range(2)]
    knrow = sb("g_knr", [1, TOK])
    qnrow = sb("g_qnr", [1, TOK])
    negb = qnrow
    lrow = sb("g_lrow", [1, TOK], F32)
    E16t = sb("g_E16", [128, 16, 16])
    sel16 = sb("g_sel", [16, 16, 128])
    ones1_16 = sb("g_o116", [1, 16])
    ones1_16f = sb("g_o116f", [1, 16], F32)
    ones16cf = sb("g_o16cf", [16, 1], F32)
    c87 = sb("g_c87", [16, 1], F32)
    ones_col = sb("g_oc", [128, 1])
    ones128 = sb("g_o128", [128, 128])
    epscol = sb("g_eps", [128, 1], F32)
    xr = sb("g_xr", [128, 1, 512], F32)
    sq = prodb
    vars_ = sb("g_vars", [128, 512], F32)
    rstd = sb("g_rstd", [128, 512], F32)
    mus = acc[0]
    nmr = acc[1]
    y3bf = kT

    pool = PsumPool(p, st, n=6, shape=(128, 512), name="g_ps")
    psn = [st.enter_context(nc.psum_tensor(f"g_psn{i}", [1, 512], F32)) for i in range(2)]

    for dst, src in ((E16t, io["E16"].rearrange("t p k -> p t k")), (sel16, io["sel16"]),
                     (ones1_16, io["ones1_16"]), (ones1_16f, io["ones1_16f"]),
                     (ones16cf, io["ones16cf"]),
                     (ones_col, io["ones_col"]), (ones128, io["ones128"]),
                     (epscol, io["epscol"])):
        p.dma("sync", dst[:], src, sem="qginit")
    t_c87 = p.op("vector", lambda e: e.memset(c87[:], 87.0))
    p.dma("gpsimd", x2bf[:], d_x2.rearrange("c p t -> p c t"), sem="qggx")
    t_init = p.qtot("qginit")
    t_gx = p.qtot("qggx")

    wst = WStream(p, st, "gw", [128, NCH, 128])

    for dstT, wsrc in ((qT, wqT_r), (kT, wkT_r), (vT, wvT_r)):
        for oc in range(NCH):
            j, buf, ldtok = wst.load(wsrc[:, :, oc * 128:(oc + 1) * 128])
            last = None
            for tb in range(2):
                tsl = slice(tb * 512, tb * 512 + 512)
                i, ps, war = pool.acquire()
                mmt = mm_group(p, ps[:], [(buf[:, ic, :], x2bf[:, ic, tsl]) for ic in range(NCH)],
                               waits=war + [ldtok, t_gx])
                last = mmt
                ev = p.op("vector", lambda e, dstT=dstT, oc=oc, tsl=tsl, ps=ps:
                          e.tensor_copy(dstT[:, oc, tsl], ps[:]), waits=[mmt])
                pool.release(i, ev)
            wst.used(j, last)
    t_proj = ("evector", p.cnt["evector"])

    kn_toks = []
    for c in range(NCH):
        for blk in range(2):
            bsl = slice(blk * 512, blk * 512 + 512)
            sw = [t_proj] if c == 0 and blk == 0 else [kn_toks[-1]]
            tsqk = p.op("scalar", lambda e, c=c, bsl=bsl: e.activation(
                sqs[:], kT[:, c, bsl], AF.Square), waits=sw)
            mmt = p.add("tensor", lambda e, c=c, blk=blk: e.matmul(
                psn[blk][:], ones_col[:], sqs[:], start=(c == 0), stop=(c == NCH - 1),
                skip_group_check=True),
                waits=[tsqk] + ([t_init] if c == 0 and blk == 0 else []), inc=("mm", 1))
            kn_toks.append(mmt)
    e1 = p.op("vector", lambda e: e.tensor_copy(knrow[:, 0:512], psn[0][:]), waits=[kn_toks[-1]])
    e2 = p.op("vector", lambda e: e.tensor_copy(knrow[:, 512:1024], psn[1][:]), waits=[kn_toks[-1]])
    t_kn = p.op("scalar", lambda e: e.activation(knrow[:], knrow[:], AF.Sqrt), waits=[e1, e2])

    scale = 1.0 / math.sqrt(HD)
    NCASC = 2
    psn_use = [t_kn, t_kn]
    acc_tok = [None, None]
    for h in range(NH):
        mmq = None
        for blk in range(2):
            bsl = slice(blk * 512, blk * 512 + 512)
            tsq = p.op("scalar", lambda e, h=h, bsl=bsl: e.activation(
                sqs[:], qT[:, h, bsl], AF.Square),
                waits=[t_kn, ("mm", p.cnt["mm"])])
            mmq = p.add("tensor", lambda e, blk=blk: e.matmul(
                psn[blk][:], ones_col[:], sqs[:], start=True, stop=True,
                skip_group_check=True), waits=[tsq] + (psn_use if blk == 0 else []),
                inc=("mm", 1))
        e1 = p.op("vector", lambda e: e.tensor_copy(qnrow[:, 0:512], psn[0][:]), waits=[mmq])
        e2 = p.op("vector", lambda e: e.tensor_copy(qnrow[:, 512:1024], psn[1][:]), waits=[mmq])
        psn_use = [e2, e2]
        t_q = p.op("scalar", lambda e: e.activation(qnrow[:], qnrow[:], AF.Sqrt), waits=[e1, e2])
        t_b = p.op("vector", lambda e: e.scalar_tensor_tensor(
            negb[:], qnrow[:], -1.0, knrow[:], OP.mult, OP.mult), waits=[t_q])

        S = [{}, {}]
        for blk in range(2):
            bsl = slice(blk * 512, blk * 512 + 512)
            ip0, ps0, war0 = pool.acquire()
            pt_toks = []
            for th in range(NH):
                pb = prodb[th % 2]
                pw = []
                if h == 0 and blk == 0 and th == 0:
                    pw.append(t_proj)
                if th >= 2:
                    pw.append(pt_toks[th - 2])
                if th < 2:
                    pw.append(("mm", p.cnt["mm"]))
                t_pr = p.op("vector", lambda e, h=h, th=th, pb=pb, bsl=bsl: e.tensor_tensor(
                    pb[:], qT[:, h, bsl], kT[:, th, bsl], OP.mult), waits=pw)
                pt_toks.append(p.add("tensor", lambda e, th=th, pb=pb, ps0=ps0: e.matmul(
                    ps0[0:16, :], E16t[:, th, :], pb[:], start=(th == 0), stop=False,
                    skip_group_check=True), waits=[t_pr, t_init] + (war0 if th == 0 else []),
                    inc=("mm", 1)))
            mmb = p.add("tensor", lambda e, ps0=ps0, bsl=bsl: e.matmul(
                ps0[0:16, :], ones1_16[:], negb[:, bsl], start=False, stop=True,
                skip_group_check=True), waits=[t_b, pt_toks[-1]], inc=("mm", 1))
            tx0 = p.op("scalar", lambda e, ps0=ps0, blk=blk: e.activation(
                PL[blk][:], ps0[0:16, :], AF.Exp, scale=scale / (1 << NCASC),
                bias=c87[:]), waits=[mmb, t_c87])
            pool.release(ip0, tx0)
            S[blk]["prev"] = tx0

        for r_ in range(NCASC + 1):
            mls = {}
            for blk in range(2):
                iL0, psL0, warL0 = pool.acquire()
                mml = p.add("tensor", lambda e, psL0=psL0, blk=blk: e.matmul(
                    psL0[0:1, :], ones16cf[:], PL[blk][:], start=True, stop=True,
                    skip_group_check=True), waits=warL0 + [S[blk]["prev"]], inc=("mm", 1))
                mls[blk] = (iL0, psL0, mml)
            tls = {}
            for blk in range(2):
                iL0, psL0, mml = mls[blk]
                lsl = slice(blk * 512, blk * 512 + 512)
                tl0 = p.op("vector", lambda e, psL0=psL0, lsl=lsl: e.tensor_copy(
                    lrow[:, lsl], psL0[0:1, :]), waits=[mml])
                pool.release(iL0, tl0)
                tls[blk] = p.op("vector", lambda e, lsl=lsl: e.reciprocal(
                    lrow[:, lsl], lrow[:, lsl]), waits=[tl0])
            mbs = {}
            for blk in range(2):
                iB0, psB0, warB0 = pool.acquire()
                lsl = slice(blk * 512, blk * 512 + 512)
                mmB = p.add("tensor", lambda e, psB0=psB0, lsl=lsl: e.matmul(
                    psB0[0:16, :], ones1_16f[:], lrow[:, lsl], start=True, stop=True,
                    skip_group_check=True), waits=warB0 + [tls[blk]], inc=("mm", 1))
                mbs[blk] = (iB0, psB0, mmB)
            for blk in range(2):
                iB0, psB0, mmB = mbs[blk]
                if r_ < NCASC:
                    tn0 = p.op("vector", lambda e, psB0=psB0, blk=blk: e.tensor_tensor(
                        PL[blk][:], PL[blk][:], psB0[0:16, :], OP.mult), waits=[mmB])
                    S[blk]["prev"] = p.op("vector", lambda e, blk=blk: e.tensor_tensor(
                        PL[blk][:], PL[blk][:], PL[blk][:], OP.mult), waits=[tn0])
                else:
                    S[blk]["prev"] = p.op("vector", lambda e, psB0=psB0, blk=blk: e.tensor_tensor(
                        PLb[blk][:], PL[blk][:], psB0[0:16, :], OP.mult), waits=[mmB])
                pool.release(iB0, S[blk]["prev"])

        g_toks = {0: [], 1: []}
        wb_toks = {0: [], 1: []}
        for th in range(NH):
            for blk in range(2):
                bsl = slice(blk * 512, blk * 512 + 512)
                iW, psW, warW = pool.acquire()
                mmW = p.add("tensor", lambda e, th=th, psW=psW, blk=blk: e.matmul(
                    psW[:], sel16[:, th, :], PLb[blk][:], start=True, stop=True,
                    skip_group_check=True),
                    waits=warW + ([S[blk]["prev"]] if th == 0 else []), inc=("mm", 1))
                wb = wbb[2 * blk + th % 2]
                war_wb = ([wb_toks[blk][th - 2]] if th >= 2 else
                          ([acc_tok[blk]] if acc_tok[blk] else []))
                tev = p.op("scalar", lambda e, psW=psW, wb=wb: e.copy(wb[:], psW[:]),
                           waits=[mmW] + war_wb)
                pool.release(iW, tev)
                tm = tmp2[2 * blk + th % 2]
                war_tm = ([g_toks[blk][th - 2]] if th >= 2 else
                          ([acc_tok[blk]] if acc_tok[blk] else []))
                tmul = p.op("vector", lambda e, th=th, bsl=bsl, tm=tm, wb=wb: e.tensor_tensor(
                    tm[:], wb[:], vT[:, th, bsl], OP.mult),
                    waits=[tev] + war_tm)
                wb_toks[blk].append(tmul)
                if th == 0:
                    au = p.op("gpsimd", lambda e, tm=tm, blk=blk: e.tensor_copy(
                        acc[blk][:], tm[:]), waits=[tmul])
                else:
                    au = p.op("gpsimd", lambda e, tm=tm, blk=blk: e.tensor_tensor(
                        acc[blk][:], acc[blk][:], tm[:], OP.add), waits=[tmul])
                g_toks[blk].append(au)
        for blk in range(2):
            bsl = slice(blk * 512, blk * 512 + 512)
            acc_tok[blk] = p.op("gpsimd", lambda e, h=h, bsl=bsl, blk=blk: e.tensor_copy(
                attn[:, h, bsl], acc[blk][:]), waits=[g_toks[blk][-1]])
    t_attn = ("egpsimd", p.cnt["egpsimd"])

    store_toks = {}
    xr_use = {}
    wo = wst
    for qb in range(2):
        qsl = slice(qb * 512, qb * 512 + 512)
        for oc in range(NCH):
            j, wbuf, ldtok = wo.load(io["woT"].rearrange("(c p) o -> p c o", p=128)
                                     [:, :, oc * 128:(oc + 1) * 128])
            ld_x_tok = p.dma("sync", xr[:, 0, :], d_x2[oc, :, qsl],
                             waits=[xr_use.get(0)], sem="qgxr0")
            i, ps, war = pool.acquire()
            mmt = mm_group(p, ps[:], [(wbuf[:, ic, :], attn[:, ic, qsl]) for ic in range(NCH)],
                           waits=war + [ldtok, t_attn])
            wo.used(j, mmt)
            ev = p.op("vector", lambda e, oc=oc, ps=ps: e.tensor_tensor(
                y3pre[:, oc, :], ps[:], xr[:, 0, :], OP.add),
                waits=[mmt, ld_x_tok, store_toks.get(oc)])
            xr_use[0] = ev
            pool.release(i, ev)
            p.op("vector", lambda e, oc=oc: e.tensor_copy(
                y3bf[:, oc, 0:512], y3pre[:, oc, :]))

        def out_cb(c, t_n, rstd_, nmr_, qb=qb, qsl=qsl):
            tt1 = p.op("vector", lambda e, c=c: e.tensor_tensor(
                y3pre[:, c, :], y3pre[:, c, :], rstd_[:, :512], OP.mult), waits=[t_n])
            tt2 = p.op("vector", lambda e, c=c: e.tensor_tensor(
                y3pre[:, c, :], y3pre[:, c, :], nmr_[:, :512], OP.add), waits=[tt1])
            stok = p.dma("sync", d_x3[c, :, qsl], y3pre[:, c, :],
                         waits=[tt2], sem=f"qgst{c}")
            store_toks[c] = stok
            return stok

        emit_ln(p, pool, lambda c: y3bf[:, c, 0:512], NCH, 512,
                sq, mus, rstd, vars_, rstd, nmr, epscol[:], ones128[:],
                out_cb, stat_waits=[t_init, ("evector", p.cnt["evector"]),
                                    ("egpsimd", p.cnt.get("egpsimd", 0)),
                                    ("mm", p.cnt["mm"])])


def emit_phaseF(p, st, nc, io):
    sb = lambda name, shape, dt=BF16: st.enter_context(nc.sbuf_tensor(name, list(shape), dt))
    d_x3, d_out = io["x3s"], io["out"]
    x3bf = sb("f_x3bf", [128, NCH, 512])
    hbuf = sb("f_h", [128, NCH, 512])
    zx = sb("f_zx", [128, 4, 512])
    aex = sb("f_aex", [128, 512])
    y4 = sb("f_y4", [128, NCH, 512], F32)
    y4bf = sb("f_y4b", [128, NCH, 512])
    scT = sb("f_scT", [8, 512], F32)
    alpaT = sb("f_alpaT", [128, 4, 8], F32)
    expT = sb("f_expT", [128, 4, 8], F32)
    anT = sb("f_anT", [128, 4, 8])
    alphaR = sb("f_alphaR", [8, 512])
    negmax4 = sb("f_ngm", [128, 4], F32)
    lsum4 = sb("f_ls4", [128, 4], F32)
    b1col = sb("f_b1", [128, NCH], F32)
    w2t = sb("f_w2", [128, NCH, 8])
    b2col = sb("f_b2", [8, 1], F32)
    E4t = sb("f_E4", [8, 4, 128])
    I128f = sb("f_If", [128, 128], F32)
    I128b = sb("f_Ib", [128, 128])
    ones128 = sb("f_o128", [128, 128])
    epscol = sb("f_eps", [128, 1], F32)
    xr = sb("f_xr", [128, 2, 512], F32)
    orot = sb("f_or", [128, 2, 512], F32)
    sq = [sb(f"f_sq{i}", [128, 512]) for i in range(2)]
    mus = sb("f_mus", [128, 512], F32)
    ex2s = sb("f_ex2s", [128, 512], F32)
    vars_ = sb("f_vars", [128, 512], F32)
    rstd = sb("f_rstd", [128, 512], F32)
    nmr = sb("f_nmr", [128, 512], F32)
    t1 = sb("f_t1", [128, 512], F32)

    pool = PsumPool(p, st, n=6, shape=(128, 512), name="f_ps")
    psn8 = st.enter_context(nc.psum_tensor("f_psn8", [8, 512], F32))
    psb8 = st.enter_context(nc.psum_tensor("f_psb8", [8, 512], BF16))

    for dst, src in ((b1col, io["b1col"]), (b2col, io["b2col"]), (I128f, io["I128f"]),
                     (I128b, io["I128"]), (ones128, io["ones128"]), (epscol, io["epscol"])):
        p.dma("sync", dst[:], src, sem="qfinit")
    p.dma("gpsimd", w2t[:], io["w2t"], sem="qfgw2")
    p.dma("sync", E4t[:], io["E4"].rearrange("c e p -> e c p"), sem="qfinit")
    t_init = p.qtot("qfinit")
    t_gw2 = p.qtot("qfgw2")

    w1 = WStream(p, st, "fw1", [128, NCH, 128])
    va = WStream(p, st, "fva", [128, NCH, 128])
    ub = WStream(p, st, "fub", [128, 4, 128])

    store_toks = {}
    xr_use = {}
    for tb in range(2):
        tsl = slice(tb * 512, tb * 512 + 512)
        ld_x = p.dma("gpsimd", x3bf[:], d_x3.rearrange("c p t -> p c t")[:, :, tsl],
                     sem=f"qfx{tb % 2}",
                     waits=[("mm", p.cnt.get("mm", 0))])
        for oc in range(NCH):
            j, buf, ldtok = w1.load(io["w14"][oc])
            i, ps, war = pool.acquire()
            mmt = mm_group(p, ps[:], [(buf[:, ic, :], x3bf[:, ic, :]) for ic in range(NCH)],
                           waits=war + [ldtok, ld_x, t_init])
            w1.used(j, mmt)
            ev = p.op("scalar", lambda e, oc=oc, ps=ps: e.activation(
                hbuf[:, oc, :], ps[:], AF.Relu, bias=b1col[:, oc:oc + 1]),
                waits=[mmt])
            pool.release(i, ev)
        t_h = ("escalar", p.cnt["escalar"])
        mmt = mm_group(p, psn8[:], [(w2t[:, ic, :], hbuf[:, ic, :]) for ic in range(NCH)],
                       waits=[t_h, t_init, t_gw2, ("escalar", p.cnt["escalar"])])
        t_sc = p.op("scalar", lambda e: e.activation(scT[:], psn8[:], AF.Identity,
                                                     bias=b2col[:]), waits=[mmt])
        tr_toks = []
        for blk in range(4):
            i, ps, war = pool.acquire()
            tt_ = p.add("tensor", lambda e, blk=blk, ps=ps: e.transpose(
                ps[:, 0:8], scT[:, blk * 128:(blk + 1) * 128], I128f[0:8, 0:8]),
                waits=war + [t_sc, t_init], inc=("mm", 1))
            ev = p.op("vector", lambda e, blk=blk, ps=ps: e.tensor_copy(
                alpaT[:, blk, :], ps[:, 0:8]),
                waits=[tt_, ("escalar", p.cnt["escalar"])])
            pool.release(i, ev)
            tr_toks.append(ev)
        t_ngm = p.op("vector", lambda e: e.reduce_max(
            negmax4[:], alpaT[:], axis=AX.X, negate=True), waits=[tr_toks[-1]])
        exp_toks = []
        for blk in range(4):
            exp_toks.append(p.op("scalar", lambda e, blk=blk: e.activation(
                expT[:, blk, :], alpaT[:, blk, :], AF.Exp, bias=negmax4[:, blk:blk + 1]),
                waits=[t_ngm]))
        t_ls = p.op("vector", lambda e: e.reduce_sum(
            lsum4[:], expT[:], axis=AX.X), waits=[exp_toks[-1]])
        t_lr = p.op("vector", lambda e: e.reciprocal(lsum4[:], lsum4[:]), waits=[t_ls])
        an_toks = []
        for blk in range(4):
            an_toks.append(p.op("vector", lambda e, blk=blk: e.tensor_scalar_mul(
                anT[:, blk, :], expT[:, blk, :], lsum4[:, blk:blk + 1]),
                waits=[t_lr, ("mm", p.cnt["mm"])]))
        trb_toks = []
        for blk in range(4):
            tt_ = p.add("tensor", lambda e, blk=blk: e.transpose(
                psb8[:, blk * 128:(blk + 1) * 128], anT[:, blk, :], I128b[:]),
                waits=[an_toks[blk], ("evector", p.cnt["evector"]), t_init],
                inc=("mm", 1))
            trb_toks.append(tt_)
        t_aR = p.op("vector", lambda e: e.tensor_copy(alphaR[:], psb8[:]),
                    waits=[trb_toks[-1]])
        for krc in range(4):
            j, buf, ldtok = va.load(io["va4"][krc])
            i, ps, war = pool.acquire()
            mmv = mm_group(p, ps[:], [(buf[:, ic, :], x3bf[:, ic, :]) for ic in range(NCH)],
                           waits=war + [ldtok])
            va.used(j, mmv)
            i2, ps2, war2 = pool.acquire()
            mme = mm_group(p, ps2[:], [(E4t[:, krc, :], alphaR[:])],
                           waits=war2 + [t_aR, t_init])
            eva = p.op("vector", lambda e, ps2=ps2: e.tensor_copy(aex[:], ps2[:]),
                       waits=[mme])
            pool.release(i2, eva)
            evz = p.op("vector", lambda e, krc=krc, ps=ps: e.tensor_tensor(
                zx[:, krc, :], ps[:], aex[:], OP.mult), waits=[mmv, eva])
            pool.release(i, evz)
        t_zx = ("evector", p.cnt["evector"])
        for oc in range(NCH):
            j, buf, ldtok = ub.load(io["ub4"][oc])
            ldr = p.dma("sync", xr[:, oc % 2, :], d_x3[oc, :, tsl],
                        waits=[xr_use.get(oc % 2)], sem=f"qfxr{oc % 2}")
            i, ps, war = pool.acquire()
            mmt = mm_group(p, ps[:], [(buf[:, k_, :], zx[:, k_, :]) for k_ in range(4)],
                           waits=war + [ldtok, t_zx])
            ub.used(j, mmt)
            ev = p.op("vector", lambda e, oc=oc, ps=ps: e.tensor_tensor(
                y4[:, oc, :], ps[:], xr[:, oc % 2, :], OP.add), waits=[mmt, ldr])
            xr_use[oc % 2] = ev
            pool.release(i, ev)
            p.op("vector", lambda e, oc=oc: e.tensor_copy(y4bf[:, oc, :], y4[:, oc, :]))

        def out_cb(c, t_n, rstd_, nmr_, tb=tb, tsl=tsl):
            w = [t_n, store_toks.get(c % 2)]
            tt1 = p.op("vector", lambda e, c=c: e.tensor_tensor(
                t1[:], y4[:, c, :], rstd_[:, :512], OP.mult), waits=w)
            tt2 = p.op("vector", lambda e, c=c: e.tensor_tensor(
                orot[:, c % 2, :], t1[:], nmr_[:, :512], OP.add), waits=[tt1])
            stok = p.dma("sync", d_out[c, :, tsl], orot[:, c % 2, :],
                         waits=[tt2], sem=f"qfst{c % 2}")
            store_toks[c % 2] = stok
            return stok

        emit_ln(p, pool, lambda c: y4bf[:, c, :], NCH, 512,
                sq, mus, ex2s, vars_, rstd, nmr, epscol[:], ones128[:],
                out_cb, stat_waits=[t_init, ("evector", p.cnt["evector"]),
                                    ("mm", p.cnt["mm"])])


def build_fused(phases="ABCGF"):
    nc = bass.Bass("TRN2", target_bir_lowering=False)
    io = {}
    ext = [
        ("x", [NCH, 128, TOK], F32), ("qkv", [D, 3 * D], F32),
        ("gamma", [128, NCH], F32), ("beta", [128, NCH], F32),
        ("o_proj", [D, D], F32), ("masks", [2, 16, 128, 512], BF16),
        ("H4", [16, 128, 16, 128], BF16), ("up4", [64, 128, 16, 128], F32),
        ("gt4", [64, 128, 16, 128], F32), ("dn4", [16, 128, 64, 128], F32),
        ("C128", [128, 128], BF16), ("suug", [128, NCH], F32),
        ("sudp", [128, 64], F32), ("wsu", [128, 64], F32), ("wsg", [128, 64], F32),
        ("wsdp", [128, NCH], F32), ("wqT", [D, D], F32), ("wkT", [D, D], F32),
        ("wvT", [D, D], F32), ("woT", [D, D], F32),
        ("E16", [16, 128, 16], BF16), ("sel16", [16, 16, 128], BF16),
        ("ones1_16", [1, 16], BF16), ("ones1_16f", [1, 16], F32),
        ("ones16cf", [16, 1], F32), ("ones_col", [128, 1], BF16),
        ("ones128", [128, 128], BF16), ("ones1", [1, 128], BF16),
        ("epscol", [128, 1], F32),
        ("w14", [16, 128, 16, 128], F32), ("b1col", [128, NCH], F32),
        ("w2t", [128, NCH, 8], F32), ("b2col", [8, 1], F32),
        ("va4", [4, 128, 16, 128], F32), ("ub4", [16, 128, 4, 128], F32),
        ("E4", [4, 8, 128], BF16), ("I128f", [128, 128], F32),
        ("I128", [128, 128], BF16),
    ]
    for name, shape, dt in ext:
        io[name] = nc.dram_tensor(name, shape, dt, kind="ExternalInput").ap()
    io["out"] = nc.dram_tensor("out", [NCH, 128, TOK], F32, kind="ExternalOutput").ap()
    for name, shape in (("agk_in", [D, TOK]), ("agk_out", [2 * D, TOK]),
                        ("agv_in", [TOK, D]), ("agv_out", [2 * TOK, D]),
                        ("x1s", [NCH, 128, TOK]), ("x2s", [NCH, 128, TOK]),
                        ("x3s", [NCH, 128, TOK])):
        io[name] = nc.dram_tensor(name, shape, F32).ap()

    p = Prog(nc)
    with ExitStack() as st_top:
        with ExitStack() as stAB:
            qTs = stAB.enter_context(nc.sbuf_tensor("qTspan", [128, NCH, TOK], BF16))
            with ExitStack() as stA:
                t_ag = emit_phaseA(p, stA, nc, io, qTs)
            barrier(p)
            if "B" in phases:
                with ExitStack() as stB:
                    emit_phaseB(p, stB, nc, io, qTs, t_ag)
        barrier(p)
        if "C" in phases:
            with ExitStack() as stC:
                emit_phaseC(p, stC, nc, io)
        barrier(p)
        if "G" in phases:
            with ExitStack() as stG:
                emit_phaseG(p, stG, nc, io)
        barrier(p)
        if "F" in phases:
            with ExitStack() as stF:
                emit_phaseF(p, stF, nc, io)
        if "F" not in phases:
            p.dma("sync", io["out"][0, :, :], io["x"][0, :, :], sem="qdbg")
        p.add("sync", lambda e: e.nop(),
              waits=[p.qtot(s) for s in sorted(p.cnt) if s.startswith("q")])
        emit_prog(p, st_top)
    return nc


def kernel(**inputs):
    g = host_prep(inputs)
    if "fused" not in _PROGS:
        _PROGS["fused"] = build_fused()
    nc = _PROGS["fused"]
    H4 = w4(g["H2048s"])
    up4 = w4(g["w_upT"])
    gt4 = w4(g["w_gateT"])
    dn4 = w4(g["w_downT"])
    w14 = w4(g["W1"])
    va4 = w4(g["VallT"])
    ub4 = w4(g["Ubar"])
    in_maps = []
    for c in range(8):
        in_maps.append({
            "x": g["x_fm"][c], "qkv": g["qkv"], "gamma": g["gamma"], "beta": g["beta"],
            "o_proj": g["o_proj"], "masks": g["masks"][c],
            "H4": H4, "up4": up4, "gt4": gt4, "dn4": dn4, "C128": g["C128"],
            "suug": g["su_ug"], "sudp": g["su_dp"], "wsu": g["wsu"], "wsg": g["wsg"],
            "wsdp": g["wsdp"], "wqT": g["wqT"], "wkT": g["wkT"], "wvT": g["wvT"],
            "woT": g["woT"], "E16": g["E16"], "sel16": g["sel16"],
            "ones1_16": g["ones1_16"], "ones1_16f": g["ones1_16f"],
            "ones16cf": g["ones16cf"], "ones_col": g["ones_col"],
            "ones128": g["ones128"], "ones1": g["ones1"], "epscol": g["epscol"],
            "w14": w14, "b1col": g["b1"], "w2t": g["w2t"], "b2col": g["b2"],
            "va4": va4, "ub4": ub4, "E4": g["E4"], "I128f": g["I128f"],
            "I128": g["I128"],
        })
    res = _run_many(nc, in_maps)
    out = np.empty((B, S, D), dtype=np.float32)
    for c in range(8):
        b, h = c // 2, c % 2
        out[b, h * TOK:(h + 1) * TOK, :] = fm_inv(np.asarray(res[c]["out"]))
    return out


# version 12 (final)
# speedup vs baseline: 1.0234x; 1.0234x over previous
import math
from contextlib import ExitStack

import numpy as np
import ml_dtypes
import concourse.bass as bass
import concourse.mybir as mybir
from concourse.bass_utils import run_bass_kernel_spmd
from concourse.alu_op_type import AluOpType as OP

F32 = mybir.dt.float32
BF16 = mybir.dt.bfloat16
I32 = mybir.dt.int32
AF = mybir.ActivationFunctionType
BF = ml_dtypes.bfloat16
AX = mybir.AxisListType

B, S, D = 4, 2048, 2048
NCH = D // 128
TOK = 1024
INTER = 8192
NH, HD = 16, 128
KEXP, RLOW = 8, 64
EPS = 1e-5

ENGINES = ("sync", "tensor", "vector", "scalar", "gpsimd")


class Prog:
    def __init__(self, nc):
        self.nc = nc
        self.ops = {e: [] for e in ENGINES}
        self.cnt = {}

    def _inc(self, sem, amt):
        self.cnt[sem] = self.cnt.get(sem, 0) + amt
        return (sem, self.cnt[sem])

    def add(self, eng, emit, waits=(), inc=None):
        tok = self._inc(*inc) if inc else None
        ws = tuple(w for w in waits if w is not None)
        self.ops[eng].append((ws, emit, (inc[0], tok[1]) if inc else None))
        return tok

    def dma(self, queue, out, in_, waits=(), sem="qmisc"):
        assert sem.startswith("q")
        return self.add(queue, lambda e, o=out, i=in_: e.dma_start(out=o, in_=i),
                        waits, inc=(sem, 16))

    def qtot(self, sem):
        return (sem, self.cnt.get(sem, 0))

    def op(self, eng, emit, waits=()):
        waits = list(waits)
        if eng in ("vector", "scalar", "gpsimd") and self.cnt.get("e" + eng, 0) > 0:
            waits.append(("e" + eng, self.cnt["e" + eng]))
        return self.add(eng, emit, waits, inc=("e" + eng, 1))

def emit_prog(p, st):
    nc = p.nc
    sems = {name: st.enter_context(nc.semaphore(name)) for name in sorted(p.cnt)}
    block = st.enter_context(nc.Block())

    def make_body(eng):
        def body(e):
            for ws, emit, inc in p.ops[eng]:
                for (s, v) in ws:
                    e.wait_ge(sems[s], v)
                ins = emit(e)
                if inc is not None:
                    amt = 16 if inc[0].startswith("q") else 1
                    ins.then_inc(sems[inc[0]], amt)
        return body

    for eng in ENGINES:
        if p.ops[eng]:
            getattr(block, eng)(make_body(eng))


class PsumPool:
    def __init__(self, p, st, n=4, shape=(128, 512), name="ps"):
        self.p = p
        self.tiles = [st.enter_context(p.nc.psum_tensor(f"{name}{i}", list(shape), F32))
                      for i in range(n)]
        self.freed = [None] * n
        self.i = 0

    def acquire(self):
        i = self.i % len(self.tiles)
        self.i += 1
        return i, self.tiles[i], ([self.freed[i]] if self.freed[i] else [])

    def release(self, i, tok):
        self.freed[i] = tok


class WStream:

    def __init__(self, p, st, name, shape, nbuf=2, queue="gpsimd", dtype=BF16):
        self.p = p
        self.name = name
        self.queue = queue
        self.nbuf = nbuf
        self.tiles = [st.enter_context(p.nc.sbuf_tensor(f"{name}{i}", list(shape), dtype))
                      for i in range(nbuf)]
        self.lastuse = []
        self.j = 0

    def load(self, src_ap, extra_waits=()):
        j = self.j
        self.j += 1
        buf = self.tiles[j % self.nbuf]
        waits = list(extra_waits)
        if j >= self.nbuf and self.lastuse[j - self.nbuf] is not None:
            waits.append(self.lastuse[j - self.nbuf])
        tok = self.p.dma(self.queue, buf[:], src_ap, waits,
                         sem=f"q{self.name}{j % self.nbuf}")
        self.lastuse.append(None)
        return j, buf, tok

    def used(self, j, tok):
        self.lastuse[j] = tok


def mm_group(p, ps_ap, tiles, waits=(), start=True, stop=True):
    n = len(tiles)

    def emit(e, tiles=tiles, ps=ps_ap, n=n):
        ins = None
        for i, (l, r) in enumerate(tiles):
            ins = e.matmul(ps, l, r, start=(start and i == 0),
                           stop=(stop and i == n - 1), skip_group_check=True)
        return ins
    return p.add("tensor", emit, waits, inc=("mm", 1))


def emit_ln(p, pool, xbf, n_ch, ncols, sq, mus, ex2s, vars_, rstd, nmr,
            epscol, ones128, out_cb, stat_waits=()):
    i_s, ps_s, war_s = pool.acquire()
    i_q, ps_q, war_q = pool.acquire()
    mm_toks = []
    for c in range(n_ch):
        sqc = sq[c % 2]
        sw = list(stat_waits) if c == 0 else []
        if c >= 2:
            sw.append(mm_toks[c - 2])
        sqtok = p.op("scalar", lambda e, i=c, d=sqc: e.activation(
            d[:, :ncols], xbf(i), AF.Square), waits=sw)

        def emit_mm(e, c=c, d=sqc):
            e.matmul(ps_s[:, :ncols], ones128, xbf(c), start=(c == 0),
                     stop=(c == n_ch - 1), skip_group_check=True)
            return e.matmul(ps_q[:, :ncols], ones128, d[:, :ncols],
                            start=(c == 0), stop=(c == n_ch - 1),
                            skip_group_check=True)
        mw = [sqtok] + (war_s + war_q if c == 0 else [])
        mm_toks.append(p.add("tensor", emit_mm, waits=mw, inc=("mm", 1)))
    mmtok = mm_toks[-1]
    t_mu = p.op("vector", lambda e: e.tensor_scalar_mul(mus[:, :ncols], ps_s[:, :ncols], 1.0 / D),
                waits=[mmtok])
    t_e2 = p.op("scalar", lambda e: e.mul(ex2s[:, :ncols], ps_q[:, :ncols], 1.0 / D),
                waits=[mmtok])
    t_m2 = p.op("scalar", lambda e: e.activation(vars_[:, :ncols], mus[:, :ncols], AF.Square),
                waits=[t_mu])
    t_v = p.op("vector", lambda e: e.tensor_tensor(vars_[:, :ncols], ex2s[:, :ncols], vars_[:, :ncols], OP.subtract),
               waits=[t_e2, t_m2])
    t_r0 = p.op("scalar", lambda e: e.activation(rstd[:, :ncols], vars_[:, :ncols], AF.Sqrt, bias=epscol),
                waits=[t_v])
    t_r = p.op("vector", lambda e: e.reciprocal(rstd[:, :ncols], rstd[:, :ncols]),
               waits=[t_r0])
    t_n = p.op("vector", lambda e: e.scalar_tensor_tensor(nmr[:, :ncols], mus[:, :ncols], -1.0, rstd[:, :ncols], OP.mult, OP.mult),
               waits=[t_r])
    pool.release(i_s, t_n)
    pool.release(i_q, t_n)
    last = None
    for c in range(n_ch):
        last = out_cb(c, t_n, rstd, nmr)
    return last


def fm(x):
    T, Dp = x.shape
    return np.ascontiguousarray(x.T.reshape(Dp // 128, 128, T))


def fm_inv(x):
    C, P, T = x.shape
    return np.ascontiguousarray(x.reshape(C * P, T).T)


def col(v):
    return np.ascontiguousarray(v.reshape(-1, 128).T)


def hadamard(n):
    H = np.array([[1.0]], dtype=np.float64)
    while H.shape[0] < n:
        H = np.block([[H, H], [H, -H]])
    return H


def rope_perm():
    perm = np.zeros(D, dtype=np.int64)
    for h in range(NH):
        base = h * HD
        for i in range(HD // 2):
            perm[base + i] = base + 2 * i
            perm[base + HD // 2 + i] = base + 2 * i + 1
    return perm


def causal_masks(h):
    m = np.zeros((2, 16, 128, 512), dtype=np.float32)
    kp = np.arange(128)[:, None]
    qf = np.arange(512)[None, :]
    for qb in range(2):
        Q0 = (2 * h + qb) * 512
        for kc in range(16):
            m[qb, kc] = (kc * 128 + kp) <= (Q0 + qf)
    return m.astype(BF)


def expand_mats():
    E = np.zeros((4, 8, 128), dtype=np.float32)
    for c in range(4):
        for pp in range(128):
            E[c, (c * 128 + pp) // 64, pp] = 1.0
    return E.astype(BF)


def host_prep(inputs):
    g = {}
    x = np.asarray(inputs["L0_attn_x"])
    g["x_fm"] = []
    for c in range(8):
        b, h = c // 2, c % 2
        g["x_fm"].append(fm(x[b, h * TOK:(h + 1) * TOK, :]))

    g["qkv"] = np.ascontiguousarray(inputs["L0_attn_qkv"])
    g["o_proj"] = np.ascontiguousarray(inputs["L0_attn_o_proj"])
    g["gamma"] = col(np.asarray(inputs["L0_attn_gamma"]))
    g["beta"] = col(np.asarray(inputs["L0_attn_beta"]))

    g["su_ug"] = col(np.asarray(inputs["L0_ffn_SU_ug"]))
    g["su_dp"] = col(np.asarray(inputs["L0_ffn_SU_dp"]))
    ws = np.asarray(inputs["L0_ffn_Wscale_ug"])
    g["wsu"] = col(ws[:INTER])
    g["wsg"] = col(ws[INTER:])
    g["wsdp"] = col(np.asarray(inputs["L0_ffn_Wscale_dp"]))
    g["w_upT"] = np.ascontiguousarray(np.asarray(inputs["L0_ffn_W_up"]).T)
    g["w_gateT"] = np.ascontiguousarray(np.asarray(inputs["L0_ffn_W_gate"]).T)
    g["w_downT"] = np.ascontiguousarray(np.asarray(inputs["L0_ffn_W_down"]).T)

    g["wqT"] = np.ascontiguousarray(np.asarray(inputs["L1_attn_wq"]).T)
    g["wkT"] = np.ascontiguousarray(np.asarray(inputs["L1_attn_wk"]).T)
    g["wvT"] = np.ascontiguousarray(np.asarray(inputs["L1_attn_wv"]).T)
    g["woT"] = np.ascontiguousarray(np.asarray(inputs["L1_attn_wo"]).T)
    f = np.asarray(inputs["L1_attn_inv_freq"])
    g["invf_dup"] = np.ascontiguousarray(np.concatenate([f, f])[None, :])

    g["W1"] = np.ascontiguousarray(inputs["L1_ffn_W1"])
    g["b1"] = col(np.asarray(inputs["L1_ffn_b1"]))
    g["W2"] = np.ascontiguousarray(inputs["L1_ffn_W2"])
    g["b2"] = np.ascontiguousarray(np.asarray(inputs["L1_ffn_b2"]).reshape(8, 1))
    V = np.asarray(inputs["L1_ffn_V"])
    g["VallT"] = np.ascontiguousarray(V.reshape(KEXP * RLOW, D).T)
    U = np.asarray(inputs["L1_ffn_U"])
    g["Ubar"] = np.ascontiguousarray(U.transpose(0, 2, 1).reshape(KEXP * RLOW, D))

    g["H2048s"] = hadamard(2048).astype(BF)
    g["C128"] = hadamard(128).astype(BF)
    g["I128"] = np.eye(128).astype(BF)
    Bsw = np.zeros((128, 128), dtype=np.float32)
    for i in range(64):
        Bsw[i, 64 + i] = 1.0
        Bsw[64 + i, i] = 1.0
    g["BswapT"] = np.ascontiguousarray(Bsw.T).astype(BF)
    g["masks"] = [causal_masks(c % 2) for c in range(8)]
    g["E4"] = expand_mats()
    g["ones1"] = np.ones((1, 128), dtype=np.float32).astype(BF)
    g["ones_col"] = np.ones((128, 1), dtype=np.float32).astype(BF)
    g["ones128"] = np.ones((128, 128), dtype=np.float32).astype(BF)
    g["epscol"] = np.full((128, 1), EPS, dtype=np.float32)
    g["I128f"] = np.eye(128, dtype=np.float32)
    E16 = np.zeros((16, 128, 16), dtype=np.float32)
    for th in range(16):
        E16[th, :, th] = 1.0
    g["E16"] = E16.astype(BF)
    sel = np.zeros((16, 16, 128), dtype=np.float32)
    for th in range(16):
        sel[th, th, :] = 1.0
    g["sel16"] = sel.astype(BF)
    g["ones1_16"] = np.ones((1, 16), dtype=np.float32).astype(BF)
    g["ones1_16f"] = np.ones((1, 16), dtype=np.float32)
    g["ones16c"] = np.ones((16, 1), dtype=np.float32).astype(BF)
    g["ones16cf"] = np.ones((16, 1), dtype=np.float32)
    sc = np.ones((128, 1), dtype=np.float32)
    sc[64:] = -1.0
    g["signcol"] = sc
    g["trow"] = [np.arange((c % 2) * TOK, (c % 2) * TOK + TOK,
                           dtype=np.float32)[None, :] for c in range(8)]
    g["trowS"] = np.arange(S, dtype=np.float32)[None, :]
    g["w2t"] = np.ascontiguousarray(
        np.asarray(inputs["L1_ffn_W2"]).reshape(16, 128, 8).transpose(1, 0, 2))
    return g


def build_launchA():
    nc = bass.Bass("TRN2", target_bir_lowering=False)
    d_x = nc.dram_tensor("x", [NCH, 128, TOK], F32, kind="ExternalInput").ap()
    d_qkv = nc.dram_tensor("qkv", [D, 3 * D], F32, kind="ExternalInput").ap()
    d_gamma = nc.dram_tensor("gamma", [128, NCH], F32, kind="ExternalInput").ap()
    d_beta = nc.dram_tensor("beta", [128, NCH], F32, kind="ExternalInput").ap()
    d_ones128 = nc.dram_tensor("ones128", [128, 128], BF16, kind="ExternalInput").ap()
    d_eps = nc.dram_tensor("epscol", [128, 1], F32, kind="ExternalInput").ap()
    d_qT = nc.dram_tensor("qT", [NCH, 128, TOK], F32, kind="ExternalOutput").ap()
    d_ktm = nc.dram_tensor("ktm", [8, 128, D], F32, kind="ExternalOutput").ap()
    d_vtm = nc.dram_tensor("vtm", [8, 128, D], F32, kind="ExternalOutput").ap()

    qkv_r = d_qkv.rearrange("(c p) o -> p c o", p=128)

    p = Prog(nc)
    with ExitStack() as st:
        sb = lambda name, shape, dt=BF16: st.enter_context(nc.sbuf_tensor(name, list(shape), dt))
        xbf = sb("xbf", [128, NCH, TOK])
        xnt = sb("xnt", [128, NCH, TOK])
        frot = sb("frot", [128, 8, 512], F32)
        sq = [sb(f"sq{i}", [128, 512]) for i in range(2)]
        mus = sb("mus", [128, 512], F32)
        ex2s = sb("ex2s", [128, 512], F32)
        vars_ = sb("vars", [128, 512], F32)
        rstd = sb("rstd", [128, 512], F32)
        nmr = sb("nmr", [128, 512], F32)
        t1 = sb("t1", [128, 512], F32)
        gam = sb("gam", [128, NCH], F32)
        bet = sb("bet", [128, NCH], F32)
        ones128 = sb("o128", [128, 128])
        epscol = sb("eps", [128, 1], F32)
        pool = PsumPool(p, st, n=8, shape=(128, 512))

        p.dma("gpsimd", xbf[:], d_x.rearrange("c p t -> p c t"), sem="qgx")
        p.dma("sync", gam[:], d_gamma, sem="qinit")
        p.cnt.setdefault("qgx", 0)
        p.dma("sync", bet[:], d_beta, sem="qinit")
        p.dma("sync", ones128[:], d_ones128, sem="qinit")
        p.dma("sync", epscol[:], d_eps, sem="qinit")
        t_init = p.qtot("qinit")
        t_init_g = p.qtot("qgx")

        for tb in range(2):
            t0 = tb * 512

            def out_cb(c, t_n, rstd, nmr, tb=tb, t0=t0):
                w = [t_n, t_init]
                tt1 = p.op("vector", lambda e, c=c: e.tensor_tensor(
                    t1[:], xbf[:, c, t0:t0 + 512], rstd[:, :512], OP.mult), waits=w)
                tt2 = p.op("vector", lambda e: e.tensor_tensor(
                    t1[:], t1[:], nmr[:, :512], OP.add), waits=[tt1])
                return p.op("vector", lambda e, c=c: e.tensor_scalar(
                    xnt[:, c, t0:t0 + 512], t1[:], gam[:, c:c + 1], bet[:, c:c + 1],
                    OP.mult, OP.add), waits=[tt2])

            emit_ln(p, pool, lambda c, t0=t0: xbf[:, c, t0:t0 + 512], NCH, 512,
                    sq, mus, ex2s, vars_, rstd, nmr, epscol[:], ones128[:],
                    out_cb, stat_waits=[t_init, t_init_g, ("mm", p.cnt.get("mm", 0))])

        t_xnt = ("evector", p.cnt["evector"])

        wq = WStream(p, st, "wq", [128, NCH, 128])
        slot_use = {}
        for oc in range(NCH):
            j, buf, ldtok = wq.load(qkv_r[:, :, oc * 128:(oc + 1) * 128])
            last_use = None
            for tb in range(2):
                t0 = tb * 512
                slot = (oc * 2 + tb) % 8
                i, ps, war = pool.acquire()
                tiles = [(buf[:, ic, :], xnt[:, ic, t0:t0 + 512]) for ic in range(NCH)]
                mmtok = mm_group(p, ps[:], tiles, waits=war + [ldtok, t_xnt])
                ev = p.op("vector", lambda e, slot=slot, ps=ps: e.tensor_copy(
                    frot[:, slot, :], ps[:]), waits=[mmtok, slot_use.get(slot)])
                pool.release(i, ev)
                last_use = mmtok
                stok = p.dma("sync", d_qT[oc, :, t0:t0 + 512], frot[:, slot, :],
                             waits=[ev], sem=f"qs{slot}")
                slot_use[slot] = stok
            wq.used(j, last_use)

        wkv = WStream(p, st, "wkv", [128, 512])
        for which, dst_dram in (("k", d_ktm), ("v", d_vtm)):
            base = D if which == "k" else 2 * D
            for ob in range(4):
                o0 = base + ob * 512
                pss = []
                for tt in range(8):
                    i, ps, war = pool.acquire()
                    pss.append((i, ps, war))
                ldtoks = []
                jj = []
                for ic in range(NCH):
                    j, buf, ldtok = wkv.load(qkv_r[:, ic, o0:o0 + 512])
                    ldtoks.append(ldtok)
                    jj.append(j)
                    def emit_mm(e, ic=ic, buf=buf, pss=pss):
                        ins = None
                        for tt in range(8):
                            ins = e.matmul(pss[tt][1][:], xnt[:, ic, tt * 128:(tt + 1) * 128],
                                           buf[:], start=(ic == 0), stop=(ic == NCH - 1),
                                           skip_group_check=True)
                        return ins
                    w = [ldtok, t_xnt] + (sum((pw for _, _, pw in pss), []) if ic == 0 else [])
                    mmtok = p.add("tensor", emit_mm, waits=w, inc=("mm", 1))
                    wkv.used(j, mmtok)
                for tt in range(8):
                    i, ps, _ = pss[tt]
                    ev = p.op("vector", lambda e, tt=tt, ps=ps:
                              e.tensor_copy(frot[:, tt, :], ps[:]),
                              waits=[mmtok, slot_use.get(tt)])
                    pool.release(i, ev)
                    stok = p.dma("sync", dst_dram[tt, :, ob * 512:ob * 512 + 512],
                                 frot[:, tt, :], waits=[ev], sem=f"qs{tt}")
                    slot_use[tt] = stok

        p.add("sync", lambda e: e.nop(),
              waits=[p.qtot(s) for s in sorted(p.cnt) if s.startswith("q")])
        emit_prog(p, st)
    return nc


def w4(w):
    Din, Dout = w.shape
    return np.ascontiguousarray(
        w.reshape(Din // 128, 128, Dout // 128, 128).transpose(2, 1, 0, 3))


def kv4(x_2d):
    return np.ascontiguousarray(
        x_2d.reshape(16, 128, 16, 128).transpose(2, 1, 0, 3))


def build_launchB():
    nc = bass.Bass("TRN2", target_bir_lowering=False)
    d_qT = nc.dram_tensor("qT", [NCH, 128, TOK], F32, kind="ExternalInput").ap()
    d_kT4 = nc.dram_tensor("kT4", [16, 128, 16, 128], F32, kind="ExternalInput").ap()
    d_v4 = nc.dram_tensor("v4", [16, 128, 16, 128], F32, kind="ExternalInput").ap()
    d_wo4 = nc.dram_tensor("wo4", [16, 128, 16, 128], F32, kind="ExternalInput").ap()
    d_x = nc.dram_tensor("x", [NCH, 128, TOK], F32, kind="ExternalInput").ap()
    d_masks = nc.dram_tensor("masks", [2, 16, 128, 512], BF16, kind="ExternalInput").ap()
    d_ones128 = nc.dram_tensor("ones128", [128, 128], BF16, kind="ExternalInput").ap()
    d_ones_col = nc.dram_tensor("ones_col", [128, 1], BF16, kind="ExternalInput").ap()
    d_ones1 = nc.dram_tensor("ones1", [1, 128], BF16, kind="ExternalInput").ap()
    d_eps = nc.dram_tensor("epscol", [128, 1], F32, kind="ExternalInput").ap()
    d_x1 = nc.dram_tensor("x1", [NCH, 128, TOK], F32, kind="ExternalOutput").ap()

    p = Prog(nc)
    with ExitStack() as st:
        sb = lambda name, shape, dt=BF16: st.enter_context(nc.sbuf_tensor(name, list(shape), dt))
        qT = sb("s_qT", [128, NCH, TOK])
        PT = sb("s_PT", [128, NCH, 512])
        AVn = sb("s_AVn", [128, NCH, 512])
        y1pre = sb("s_y1p", [128, NCH, 512], F32)
        y1bf = sb("s_y1b", [128, NCH, 512])
        x1rot = sb("s_x1r", [128, 4, 512], F32)
        xr = sb("s_xr", [128, 2, 512], F32)
        masks = sb("s_mask", [128, 16, 512])
        ones128 = sb("s_o128", [128, 128])
        ones_col = sb("s_oc", [128, 1])
        ones1 = sb("s_o1", [1, 128])
        epscol = sb("s_eps", [128, 1], F32)
        sqn = [sb(f"s_sqn{i}", [128, D]) for i in range(2)]
        qn2 = sb("s_qn2", [1, TOK], F32)
        kn2 = sb("s_kn2", [1, D], F32)
        sc1 = sb("s_sc1", [1, 1], F32)
        sc2 = sb("s_sc2", [1, 1], F32)
        sc2b = sb("s_sc2b", [1, 1], BF16)
        biascol = sb("s_bias", [128, 1], F32)
        linv = sb("s_linv", [128, 512], F32)
        sq = [sb(f"s_sq{i}", [128, 512]) for i in range(2)]
        mus = sb("s_mus", [128, 512], F32)
        ex2s = sb("s_ex2s", [128, 512], F32)
        vars_ = sb("s_vars", [128, 512], F32)
        rstd = sb("s_rstd", [128, 512], F32)
        nmr = sb("s_nmr", [128, 512], F32)
        t1 = sb("s_t1", [128, 512], F32)

        pool = PsumPool(p, st, n=6, shape=(128, 512))
        psn = [st.enter_context(nc.psum_tensor(f"psn{i}", [1, 512], F32)) for i in range(2)]

        p.dma("gpsimd", qT[:], d_qT.rearrange("c p t -> p c t"), sem="qgq")
        p.dma("sync", ones128[:], d_ones128, sem="qinit")
        p.dma("sync", ones_col[:], d_ones_col, sem="qinit")
        p.dma("sync", ones1[:], d_ones1, sem="qinit")
        p.dma("sync", epscol[:], d_eps, sem="qinit")
        t_init = p.qtot("qinit")
        t_gq = p.qtot("qgq")

        kst = WStream(p, st, "kst", [128, NCH, 128], queue="gpsimd")
        vst = WStream(p, st, "vst", [128, NCH, 128], queue="gpsimd")
        wo = WStream(p, st, "wo", [128, NCH, 128], queue="gpsimd")

        qn_toks = []
        for dc in range(NCH):
            sw = [t_init, t_gq] if dc == 0 else [qn_toks[dc - 2]] if dc >= 2 else []
            sqtok = p.op("scalar", lambda e, dc=dc: e.activation(
                sqn[dc % 2][:, 0:TOK], qT[:, dc, :], AF.Square), waits=sw)

            def emm(e, dc=dc):
                e.matmul(psn[0][:], ones_col[:], sqn[dc % 2][:, 0:512],
                         start=(dc == 0), stop=(dc == NCH - 1), skip_group_check=True)
                return e.matmul(psn[1][:], ones_col[:], sqn[dc % 2][:, 512:1024],
                                start=(dc == 0), stop=(dc == NCH - 1), skip_group_check=True)
            mw = [sqtok] + ([t_init] if dc == 0 else [])
            qn_toks.append(p.add("tensor", emm, waits=mw, inc=("mm", 1)))
        e1 = p.op("vector", lambda e: e.tensor_copy(qn2[:, 0:512], psn[0][:]), waits=[qn_toks[-1]])
        e2 = p.op("vector", lambda e: e.tensor_copy(qn2[:, 512:1024], psn[1][:]), waits=[qn_toks[-1]])
        t_qmax = p.op("vector", lambda e: e.reduce_max(sc1[:], qn2[:], axis=AX.X), waits=[e1, e2])

        kn_toks = [t_qmax]
        for kc in range(NCH):
            j, kbuf, ldtok = kst.load(d_kT4[kc])
            sqtok = p.op("scalar", lambda e, kbuf=kbuf, kc=kc: e.activation(
                sqn[kc % 2][:, 0:D], kbuf[:].rearrange("p c k -> p (c k)"), AF.Square),
                waits=[ldtok] + ([qn_toks[-1]] if kc < 2 else []))
            def emm(e, kc=kc):
                ins = None
                for c in range(NCH):
                    ins = e.matmul(psn[0][:, 0:128], ones_col[:],
                                   sqn[kc % 2][:, c * 128:(c + 1) * 128],
                                   start=(c == 0), stop=(c == NCH - 1), skip_group_check=True)
                return ins
            mmt = p.add("tensor", emm, waits=[sqtok] + ([kn_toks[-1]] if True else []), inc=("mm", 1))
            kst.used(j, mmt)
            ev = p.op("vector", lambda e, kc=kc: e.tensor_copy(
                kn2[:, kc * 128:(kc + 1) * 128], psn[0][:, 0:128]), waits=[mmt])
            kn_toks.append(ev)
        t_kmax = p.op("vector", lambda e: e.reduce_max(sc2[:], kn2[:], axis=AX.X), waits=[kn_toks[-1]])
        t_m2 = p.op("vector", lambda e: e.tensor_tensor(sc1[:], sc1[:], sc2[:], OP.mult),
                    waits=[t_qmax, t_kmax])
        t_m = p.op("scalar", lambda e: e.activation(sc1[:], sc1[:], AF.Sqrt), waits=[t_m2])
        t_b0 = p.op("scalar", lambda e: e.mul(sc2b[:], sc1[:], -1.0 / math.sqrt(D)), waits=[t_m])
        ib, psb, warb = pool.acquire()
        t_bb = mm_group(p, psb[:, 0:1], [(ones1[:], sc2b[:])], waits=warb + [t_b0, t_init])
        t_bias = p.op("scalar", lambda e: e.copy(biascol[:], psb[:, 0:1]), waits=[t_bb])
        pool.release(ib, t_bias)

        scale = 1.0 / math.sqrt(D)
        store_toks = []
        xr_use = {}
        mask_use = None
        for qb in range(2):
            qsl = slice(qb * 512, qb * 512 + 512)
            live = list(range(NCH))
            ld_m = p.dma("sync", masks[:], d_masks[qb].rearrange("m p q -> p m q"),
                         waits=[mask_use], sem="qmask")
            for kc in live:
                j, kbuf, ldtok = kst.load(d_kT4[kc])
                i, ps, war = pool.acquire()
                tiles = [(kbuf[:, dc, :], qT[:, dc, qsl]) for dc in range(NCH)]
                mmt = mm_group(p, ps[:], tiles, waits=war + [ldtok])
                kst.used(j, mmt)
                t_exp = p.op("scalar", lambda e, kc=kc, ps=ps: e.activation(
                    PT[:, kc, :], ps[:], AF.Exp, bias=biascol[:], scale=scale),
                    waits=[mmt, t_bias])
                lastPT = p.op("vector", lambda e, kc=kc: e.tensor_tensor(
                    PT[:, kc, :], PT[:, kc, :], masks[:, kc, :], OP.mult),
                    waits=[t_exp, ld_m])
                pool.release(i, lastPT)
            mask_use = ("evector", p.cnt["evector"])
            t_allPT = [("escalar", p.cnt["escalar"]), ("evector", p.cnt["evector"])]
            i, ps, war = pool.acquire()
            mmt = mm_group(p, ps[:], [(ones128[:], PT[:, kc, :]) for kc in live],
                           waits=war + t_allPT + [t_init])
            t_rec = p.op("vector", lambda e, ps=ps: e.reciprocal(linv[:], ps[:]), waits=[mmt])
            pool.release(i, t_rec)
            for dv in range(NCH):
                j, vbuf, ldtok = vst.load(d_v4[dv])
                i, ps, war = pool.acquire()
                tiles = [(vbuf[:, kc, :], PT[:, kc, :]) for kc in live]
                mmt = mm_group(p, ps[:], tiles, waits=war + [ldtok])
                vst.used(j, mmt)
                ev = p.op("vector", lambda e, dv=dv, ps=ps: e.tensor_tensor(
                    AVn[:, dv, :], ps[:], linv[:], OP.mult), waits=[mmt, t_rec])
                pool.release(i, ev)
            t_avn = [("evector", p.cnt["evector"])]
            for oc in range(NCH):
                j, wbuf, ldtok = wo.load(d_wo4[oc])
                ld_x_tok = p.dma("sync", xr[:, oc % 2, :], d_x[oc, :, qsl],
                                 waits=[xr_use.get((oc % 2))], sem=f"qxr{oc % 2}")
                i, ps, war = pool.acquire()
                tiles = [(wbuf[:, ic, :], AVn[:, ic, :]) for ic in range(NCH)]
                mmt = mm_group(p, ps[:], tiles, waits=war + [ldtok] + t_avn)
                wo.used(j, mmt)
                ev = p.op("vector", lambda e, oc=oc, ps=ps: e.scalar_tensor_tensor(
                    y1pre[:, oc, :], xr[:, oc % 2, :], 2.0, ps[:], OP.mult, OP.add),
                    waits=[mmt, ld_x_tok])
                xr_use[oc % 2] = ev
                pool.release(i, ev)
                p.op("vector", lambda e, oc=oc: e.tensor_copy(y1bf[:, oc, :], y1pre[:, oc, :]))
            def out_cb(c, t_n, rstd_, nmr_, qb=qb, qsl=qsl):
                w = [t_n]
                if len(store_toks) >= 4:
                    w.append(store_toks[-4])
                tt1 = p.op("vector", lambda e, c=c: e.tensor_tensor(
                    t1[:], y1pre[:, c, :], rstd_[:, :512], OP.mult), waits=w)
                tt2 = p.op("vector", lambda e, c=c: e.tensor_tensor(
                    x1rot[:, c % 4, :], t1[:], nmr_[:, :512], OP.add), waits=[tt1])
                stok = p.dma("sync", d_x1[c, :, qsl], x1rot[:, c % 4, :],
                             waits=[tt2], sem=f"qst{c % 4}")
                store_toks.append(stok)
                return stok

            emit_ln(p, pool, lambda c: y1bf[:, c, :], NCH, 512,
                    sq, mus, ex2s, vars_, rstd, nmr, epscol[:], ones128[:],
                    out_cb, stat_waits=[t_init, ("evector", p.cnt["evector"]),
                                        ("mm", p.cnt["mm"])])

        p.add("sync", lambda e: e.nop(),
              waits=[p.qtot(s) for s in sorted(p.cnt) if s.startswith("q")])
        emit_prog(p, st)
    return nc


def build_launchC():
    nc = bass.Bass("TRN2", target_bir_lowering=False)
    d_x1 = nc.dram_tensor("x1", [NCH, 128, TOK], F32, kind="ExternalInput").ap()
    d_H4 = nc.dram_tensor("H4", [16, 128, 16, 128], BF16, kind="ExternalInput").ap()
    d_up4 = nc.dram_tensor("up4", [64, 128, 16, 128], F32, kind="ExternalInput").ap()
    d_gt4 = nc.dram_tensor("gt4", [64, 128, 16, 128], F32, kind="ExternalInput").ap()
    d_dn4 = nc.dram_tensor("dn4", [16, 128, 64, 128], F32, kind="ExternalInput").ap()
    d_C128 = nc.dram_tensor("C128", [128, 128], BF16, kind="ExternalInput").ap()
    d_suug = nc.dram_tensor("suug", [128, NCH], F32, kind="ExternalInput").ap()
    d_sudp = nc.dram_tensor("sudp", [128, 64], F32, kind="ExternalInput").ap()
    d_wsu = nc.dram_tensor("wsu", [128, 64], F32, kind="ExternalInput").ap()
    d_wsg = nc.dram_tensor("wsg", [128, 64], F32, kind="ExternalInput").ap()
    d_wsdp = nc.dram_tensor("wsdp", [128, NCH], F32, kind="ExternalInput").ap()
    d_ones128 = nc.dram_tensor("ones128", [128, 128], BF16, kind="ExternalInput").ap()
    d_eps = nc.dram_tensor("epscol", [128, 1], F32, kind="ExternalInput").ap()
    d_x2 = nc.dram_tensor("x2", [NCH, 128, TOK], F32, kind="ExternalOutput").ap()

    p = Prog(nc)
    with ExitStack() as st:
        sb = lambda name, shape, dt=BF16: st.enter_context(nc.sbuf_tensor(name, list(shape), dt))
        x1bf = sb("s_x1bf", [128, NCH, 512])
        zt = x1bf
        tt = sb("s_tt", [128, NCH, 512])
        mid = sb("s_mid", [128, 64, 512])
        tmp16 = sb("s_tmp16", [128, 16, 512])
        y2 = sb("s_y2", [128, NCH, 512], F32)
        g1f = sb("s_g1f", [128, 512], F32)
        silb = sb("s_silb", [128, 512])
        xr = sb("s_xr", [128, 2, 512], F32)
        x2rot = sb("s_x2r", [128, 2, 512], F32)
        C128 = sb("s_C128", [128, 128])
        suug = sb("s_suug", [128, NCH], F32)
        sudp = sb("s_sudp", [128, 64], F32)
        wsu = sb("s_wsu", [128, 64], F32)
        wsg = sb("s_wsg", [128, 64], F32)
        wsdp = sb("s_wsdp", [128, NCH], F32)
        wsud = sb("s_wsud", [128, 64], F32)
        ones128 = sb("s_o128", [128, 128])
        epscol = sb("s_eps", [128, 1], F32)
        sq = [sb(f"s_sq{i}", [128, 512]) for i in range(2)]
        mus = sb("s_mus", [128, 512], F32)
        ex2s = sb("s_ex2s", [128, 512], F32)
        vars_ = sb("s_vars", [128, 512], F32)
        rstd = sb("s_rstd", [128, 512], F32)
        nmr = sb("s_nmr", [128, 512], F32)
        t1 = sb("s_t1", [128, 512], F32)

        pool = PsumPool(p, st, n=6, shape=(128, 512))
        for dst, src in ((C128, d_C128), (suug, d_suug), (sudp, d_sudp),
                         (wsu, d_wsu), (wsg, d_wsg), (wsdp, d_wsdp),
                         (ones128, d_ones128), (epscol, d_eps)):
            p.dma("sync", dst[:], src, sem="qinit")
        t_init = p.qtot("qinit")
        t_wsud = p.op("vector", lambda e: e.tensor_tensor(wsud[:], wsu[:], sudp[:], OP.mult),
                      waits=[t_init])

        wA = WStream(p, st, "wA", [128, NCH, 128])
        wB = WStream(p, st, "wB", [128, NCH, 128])
        wD = WStream(p, st, "wD", [128, 16, 128])

        store_toks = {}
        xr_use = {}
        for tb in range(2):
            tsl = slice(tb * 512, tb * 512 + 512)
            ld_x = p.dma("gpsimd", x1bf[:], d_x1.rearrange("c p t -> p c t")[:, :, tsl],
                         sem=f"qx{tb % 2}",
                         waits=[("evector", p.cnt.get("evector", 0))])
            zt_toks = []
            for c in range(NCH):
                zt_toks.append(p.op("vector", lambda e, c=c: e.tensor_scalar_mul(
                    zt[:, c, :], x1bf[:, c, :], suug[:, c:c + 1]),
                    waits=[ld_x, t_init, ("mm", p.cnt.get("mm", 0))] if c == 0 else []))
            t_zt = zt_toks[-1]
            for oc in range(NCH):
                j, buf, ldtok = wA.load(d_H4[oc])
                i, ps, war = pool.acquire()
                mmt = mm_group(p, ps[:], [(buf[:, ic, :], zt[:, ic, :]) for ic in range(NCH)],
                               waits=war + [ldtok, t_zt])
                wA.used(j, mmt)
                ev = p.op("vector", lambda e, oc=oc, ps=ps: e.tensor_scalar_mul(
                    tt[:, oc, :], ps[:], 1.0 / math.sqrt(D)), waits=[mmt])
                pool.release(i, ev)
            t_tt = ("evector", p.cnt["evector"])
            for oc in range(64):
                ju, bu, ldu = wA.load(d_up4[oc])
                jg, bg, ldg = wB.load(d_gt4[oc])
                iu, psu, waru = pool.acquire()
                mmu = mm_group(p, psu[:], [(bu[:, ic, :], tt[:, ic, :]) for ic in range(NCH)],
                               waits=waru + [ldu, t_tt])
                wA.used(ju, mmu)
                ig, psg, warg = pool.acquire()
                mmg = mm_group(p, psg[:], [(bg[:, ic, :], tt[:, ic, :]) for ic in range(NCH)],
                               waits=warg + [ldg, t_tt])
                wB.used(jg, mmg)
                tg1 = p.op("vector", lambda e, oc=oc, psg=psg: e.tensor_scalar_mul(
                    g1f[:], psg[:], wsg[:, oc:oc + 1]), waits=[mmg])
                pool.release(ig, tg1)
                tsil = p.op("scalar", lambda e: e.activation(silb[:], g1f[:], AF.Silu),
                            waits=[tg1])
                tmid = p.op("vector", lambda e, oc=oc, psu=psu: e.scalar_tensor_tensor(
                    mid[:, oc, :], psu[:], wsud[:, oc:oc + 1], silb[:], OP.mult, OP.mult),
                    waits=[mmu, tsil, t_wsud])
                pool.release(iu, tmid)
            for a in range(64):
                i, ps, war = pool.acquire()
                mmt = mm_group(p, ps[:], [(C128[:], mid[:, a, :])],
                               waits=war + ([t_init, ("evector", p.cnt["evector"])] if a == 0 else []))
                ev = p.op("vector", lambda e, a=a, ps=ps: e.tensor_scalar_mul(
                    mid[:, a, :], ps[:], 1.0 / math.sqrt(INTER)), waits=[mmt])
                pool.release(i, ev)
            midf = mid[:].rearrange("p c e -> p (c e)")
            tmpf = tmp16[:].rearrange("p c e -> p (c e)")
            for s_ in range(6):
                dd = 1 << s_
                views = []
                if 32 // dd >= 2:
                    gh = (32 // dd) // 2
                    v_ = midf.rearrange("p (g two dd e) -> p g two dd e",
                                        two=2, dd=dd, e=512)
                    tv_ = tmpf.rearrange("p (g dd e) -> p g dd e", dd=dd, e=512)
                    for h in range(2):
                        views.append((v_[:, h * gh:(h + 1) * gh, 0, :, :],
                                      v_[:, h * gh:(h + 1) * gh, 1, :, :],
                                      tv_[:, 0:gh, :, :]))
                else:
                    v_ = midf.rearrange("p (two dd e) -> p two dd e", dd=32, e=512)
                    tv_ = tmpf.rearrange("p (dd e) -> p dd e", e=512)
                    for h in range(2):
                        views.append((v_[:, 0, h * 16:(h + 1) * 16, :],
                                      v_[:, 1, h * 16:(h + 1) * 16, :],
                                      tv_[:, 0:16, :]))
                for lo, hi, tv in views:
                    p.op("vector", lambda e, lo=lo, hi=hi, tv=tv: e.tensor_tensor(tv, lo, hi, OP.add))
                    p.op("vector", lambda e, lo=lo, hi=hi: e.tensor_tensor(hi, lo, hi, OP.subtract))
                    p.op("vector", lambda e, lo=lo, tv=tv: e.tensor_copy(lo, tv))
            t_bfly = ("evector", p.cnt["evector"])
            for oc in range(NCH):
                ldr = p.dma("sync", xr[:, oc % 2, :], d_x1[oc, :, tsl],
                            waits=[xr_use.get(oc % 2)], sem=f"qxr{oc % 2}")
                i, ps, war = pool.acquire()
                mmq = None
                for qtr in range(4):
                    jq, bq, ldq = wD.load(d_dn4[oc, :, 16 * qtr:16 * qtr + 16, :])
                    w = [ldq]
                    if qtr == 0:
                        w += war + [t_bfly]
                    mmq = mm_group(
                        p, ps[:],
                        [(bq[:, ic, :], mid[:, 16 * qtr + ic, :]) for ic in range(16)],
                        waits=w, start=(qtr == 0), stop=(qtr == 3))
                    wD.used(jq, mmq)
                ev = p.op("vector", lambda e, oc=oc, ps=ps: e.scalar_tensor_tensor(
                    y2[:, oc, :], ps[:], wsdp[:, oc:oc + 1], xr[:, oc % 2, :], OP.mult, OP.add),
                    waits=[mmq, ldr])
                xr_use[oc % 2] = ev
                pool.release(i, ev)
            for c in range(NCH):
                p.op("vector", lambda e, c=c: e.tensor_copy(tt[:, c, :], y2[:, c, :]),
                     waits=[("mm", p.cnt["mm"])] if c == 0 else [])
            t_cast = ("evector", p.cnt["evector"])

            def out_cb(c, t_n, rstd_, nmr_, tb=tb, tsl=tsl):
                w = [t_n, store_toks.get(c % 2)]
                tt1 = p.op("vector", lambda e, c=c: e.tensor_tensor(
                    t1[:], y2[:, c, :], rstd_[:, :512], OP.mult), waits=w)
                tt2 = p.op("vector", lambda e, c=c: e.tensor_tensor(
                    x2rot[:, c % 2, :], t1[:], nmr_[:, :512], OP.add), waits=[tt1])
                stok = p.dma("sync", d_x2[c, :, tsl], x2rot[:, c % 2, :],
                             waits=[tt2], sem=f"qst{c % 2}")
                store_toks[c % 2] = stok
                return stok

            emit_ln(p, pool, lambda c: tt[:, c, :], NCH, 512,
                    sq, mus, ex2s, vars_, rstd, nmr, epscol[:], ones128[:],
                    out_cb, stat_waits=[t_init, t_cast, ("mm", p.cnt["mm"])])

        p.add("sync", lambda e: e.nop(),
              waits=[p.qtot(s) for s in sorted(p.cnt) if s.startswith("q")])
        emit_prog(p, st)
    return nc


def build_launchG():
    nc = bass.Bass("TRN2", target_bir_lowering=False)
    d_x2 = nc.dram_tensor("x2", [NCH, 128, TOK], F32, kind="ExternalInput").ap()
    d_wqT = nc.dram_tensor("wqT", [D, D], F32, kind="ExternalInput").ap()
    d_wkT = nc.dram_tensor("wkT", [D, D], F32, kind="ExternalInput").ap()
    d_wvT = nc.dram_tensor("wvT", [D, D], F32, kind="ExternalInput").ap()
    d_wo4 = nc.dram_tensor("wo4", [16, 128, 16, 128], F32, kind="ExternalInput").ap()
    d_E16 = nc.dram_tensor("E16", [16, 128, 16], BF16, kind="ExternalInput").ap()
    d_sel16 = nc.dram_tensor("sel16", [16, 16, 128], BF16, kind="ExternalInput").ap()
    d_ones1_16 = nc.dram_tensor("ones1_16", [1, 16], BF16, kind="ExternalInput").ap()
    d_ones1_16f = nc.dram_tensor("ones1_16f", [1, 16], F32, kind="ExternalInput").ap()
    d_ones16c = nc.dram_tensor("ones16c", [16, 1], BF16, kind="ExternalInput").ap()
    d_ones16cf = nc.dram_tensor("ones16cf", [16, 1], F32, kind="ExternalInput").ap()
    d_ones_col = nc.dram_tensor("ones_col", [128, 1], BF16, kind="ExternalInput").ap()
    d_ones128 = nc.dram_tensor("ones128", [128, 128], BF16, kind="ExternalInput").ap()
    d_eps = nc.dram_tensor("epscol", [128, 1], F32, kind="ExternalInput").ap()
    d_x3 = nc.dram_tensor("x3", [NCH, 128, TOK], F32, kind="ExternalOutput").ap()

    wqT_r = d_wqT.rearrange("(c p) o -> p c o", p=128)
    wkT_r = d_wkT.rearrange("(c p) o -> p c o", p=128)
    wvT_r = d_wvT.rearrange("(c p) o -> p c o", p=128)

    p = Prog(nc)
    with ExitStack() as st:
        sb = lambda name, shape, dt=BF16: st.enter_context(nc.sbuf_tensor(name, list(shape), dt))
        x2bf = sb("s_x2bf", [128, NCH, TOK])
        attn = x2bf
        qT = sb("s_qT", [128, NCH, TOK])
        kT = sb("s_kT", [128, NCH, TOK])
        vT = sb("s_vT", [128, NCH, TOK])
        y3pre = sb("s_y3p", [128, NCH, 512], F32)
        prodb = [sb(f"s_prod{i}", [128, 512]) for i in range(2)]
        sqs = sb("s_sqs", [128, 512])
        acc = sb("s_acc", [128, 512], F32)
        tmpm = sb("s_tmpm", [128, 512], F32)
        PL = sb("s_PL", [16, 512], F32)
        PLb = sb("s_PLb", [16, 512])
        knrow = sb("s_knr", [1, TOK])
        qnrow = sb("s_qnr", [1, TOK])
        negb = qnrow
        lrow = sb("s_lrow", [1, 512], F32)
        E16t = sb("s_E16", [128, 16, 16])
        sel16 = sb("s_sel", [16, 16, 128])
        ones1_16 = sb("s_o116", [1, 16])
        ones1_16f = sb("s_o116f", [1, 16], F32)
        ones16c = sb("s_o16c", [16, 1])
        ones16cf = sb("s_o16cf", [16, 1], F32)
        ones_col = sb("s_oc", [128, 1])
        ones128 = sb("s_o128", [128, 128])
        epscol = sb("s_eps", [128, 1], F32)
        xr = sb("s_xr", [128, 2, 512], F32)
        x3rot = sb("s_x3r", [128, 2, 512], F32)
        sq = [sb(f"s_sq{i}", [128, 512]) for i in range(2)]
        mus = sb("s_mus", [128, 512], F32)
        vars_ = sb("s_vars", [128, 512], F32)
        rstd = sb("s_rstd", [128, 512], F32)
        nmr = sb("s_nmr", [128, 512], F32)
        y3bf = kT

        pool = PsumPool(p, st, n=6, shape=(128, 512))
        psn = [st.enter_context(nc.psum_tensor(f"psn{i}", [1, 512], F32)) for i in range(2)]

        for dst, src in ((E16t, d_E16.rearrange("t p k -> p t k")), (sel16, d_sel16),
                         (ones1_16, d_ones1_16), (ones1_16f, d_ones1_16f),
                         (ones16c, d_ones16c), (ones16cf, d_ones16cf),
                         (ones_col, d_ones_col), (ones128, d_ones128), (epscol, d_eps)):
            p.dma("sync", dst[:], src, sem="qinit")
        p.dma("gpsimd", x2bf[:], d_x2.rearrange("c p t -> p c t"), sem="qgx")
        t_init = p.qtot("qinit")
        t_gx = p.qtot("qgx")

        wst = WStream(p, st, "wst", [128, NCH, 128])

        for dstT, wsrc in ((qT, wqT_r), (kT, wkT_r), (vT, wvT_r)):
            for oc in range(NCH):
                j, buf, ldtok = wst.load(wsrc[:, :, oc * 128:(oc + 1) * 128])
                last = None
                for tb in range(2):
                    tsl = slice(tb * 512, tb * 512 + 512)
                    i, ps, war = pool.acquire()
                    mmt = mm_group(p, ps[:], [(buf[:, ic, :], x2bf[:, ic, tsl]) for ic in range(NCH)],
                                   waits=war + [ldtok, t_gx])
                    last = mmt
                    ev = p.op("vector", lambda e, dstT=dstT, oc=oc, tsl=tsl, ps=ps:
                              e.tensor_copy(dstT[:, oc, tsl], ps[:]), waits=[mmt])
                    pool.release(i, ev)
                wst.used(j, last)
        t_proj = ("evector", p.cnt["evector"])

        kn_toks = []
        for c in range(NCH):
            for blk in range(2):
                bsl = slice(blk * 512, blk * 512 + 512)
                sw = [t_proj] if c == 0 and blk == 0 else [kn_toks[-1]]
                tsqk = p.op("scalar", lambda e, c=c, bsl=bsl: e.activation(
                    sqs[:], kT[:, c, bsl], AF.Square), waits=sw)
                mmt = p.add("tensor", lambda e, c=c, blk=blk: e.matmul(
                    psn[blk][:], ones_col[:], sqs[:], start=(c == 0), stop=(c == NCH - 1),
                    skip_group_check=True),
                    waits=[tsqk] + ([t_init] if c == 0 and blk == 0 else []), inc=("mm", 1))
                kn_toks.append(mmt)
        e1 = p.op("vector", lambda e: e.tensor_copy(knrow[:, 0:512], psn[0][:]), waits=[kn_toks[-1]])
        e2 = p.op("vector", lambda e: e.tensor_copy(knrow[:, 512:1024], psn[1][:]), waits=[kn_toks[-1]])
        t_kn = p.op("scalar", lambda e: e.activation(knrow[:], knrow[:], AF.Sqrt), waits=[e1, e2])

        scale = 1.0 / math.sqrt(HD)
        NCASC = 4
        psn_use = [t_kn, t_kn]
        for h in range(NH):
            mmq = None
            for blk in range(2):
                bsl = slice(blk * 512, blk * 512 + 512)
                tsq = p.op("scalar", lambda e, h=h, bsl=bsl: e.activation(
                    sqs[:], qT[:, h, bsl], AF.Square),
                    waits=[t_kn, ("mm", p.cnt["mm"])])
                mmq = p.add("tensor", lambda e, blk=blk: e.matmul(
                    psn[blk][:], ones_col[:], sqs[:], start=True, stop=True,
                    skip_group_check=True), waits=[tsq] + (psn_use if blk == 0 else []),
                    inc=("mm", 1))
            e1 = p.op("vector", lambda e: e.tensor_copy(qnrow[:, 0:512], psn[0][:]), waits=[mmq])
            e2 = p.op("vector", lambda e: e.tensor_copy(qnrow[:, 512:1024], psn[1][:]), waits=[mmq])
            psn_use = [e2, e2]
            t_q = p.op("scalar", lambda e: e.activation(qnrow[:], qnrow[:], AF.Sqrt), waits=[e1, e2])
            t_b = p.op("vector", lambda e: e.scalar_tensor_tensor(
                negb[:], qnrow[:], -1.0, knrow[:], OP.mult, OP.mult), waits=[t_q])

            for blk in range(2):
                bsl = slice(blk * 512, blk * 512 + 512)
                ip0, ps0, war0 = pool.acquire()
                pt_toks = []
                for th in range(NH):
                    pb = prodb[th % 2]
                    pw = [t_proj] if h == 0 and blk == 0 and th == 0 else []
                    if th >= 2:
                        pw.append(pt_toks[th - 2])
                    if th < 2:
                        pw.append(("mm", p.cnt["mm"]))
                    t_pr = p.op("vector", lambda e, h=h, th=th, pb=pb, bsl=bsl: e.tensor_tensor(
                        pb[:], qT[:, h, bsl], kT[:, th, bsl], OP.mult), waits=pw)
                    pt_toks.append(p.add("tensor", lambda e, th=th, pb=pb, ps0=ps0: e.matmul(
                        ps0[0:16, :], E16t[:, th, :], pb[:], start=(th == 0), stop=False,
                        skip_group_check=True), waits=[t_pr, t_init] + (war0 if th == 0 else []),
                        inc=("mm", 1)))
                mmb = p.add("tensor", lambda e, ps0=ps0, bsl=bsl: e.matmul(
                    ps0[0:16, :], ones1_16[:], negb[:, bsl], start=False, stop=True,
                    skip_group_check=True), waits=[t_b, pt_toks[-1]], inc=("mm", 1))
                tx0 = p.op("scalar", lambda e, ps0=ps0: e.activation(
                    PL[:], ps0[0:16, :], AF.Exp, scale=scale / (1 << NCASC)), waits=[mmb])
                pool.release(ip0, tx0)
                t_prev = tx0
                for r_ in range(NCASC + 1):
                    iL0, psL0, warL0 = pool.acquire()
                    mml = p.add("tensor", lambda e, psL0=psL0: e.matmul(
                        psL0[0:1, :], ones16cf[:], PL[:], start=True, stop=True,
                        skip_group_check=True), waits=warL0 + [t_prev], inc=("mm", 1))
                    tl0 = p.op("vector", lambda e, psL0=psL0: e.tensor_copy(lrow[:], psL0[0:1, :]),
                               waits=[mml])
                    pool.release(iL0, tl0)
                    t_lr = p.op("vector", lambda e: e.reciprocal(lrow[:], lrow[:]), waits=[tl0])
                    iB0, psB0, warB0 = pool.acquire()
                    mmB = p.add("tensor", lambda e, psB0=psB0: e.matmul(
                        psB0[0:16, :], ones1_16f[:], lrow[:], start=True, stop=True,
                        skip_group_check=True), waits=warB0 + [t_lr], inc=("mm", 1))
                    tn0 = p.op("vector", lambda e, psB0=psB0: e.tensor_tensor(
                        PL[:], PL[:], psB0[0:16, :], OP.mult), waits=[mmB])
                    pool.release(iB0, tn0)
                    if r_ < NCASC:
                        t_prev = p.op("vector", lambda e: e.tensor_tensor(
                            PL[:], PL[:], PL[:], OP.mult), waits=[tn0])
                    else:
                        t_prev = tn0
                t_plb = p.op("vector", lambda e: e.tensor_copy(PLb[:], PL[:]), waits=[t_prev])
                for th in range(NH):
                    iW, psW, warW = pool.acquire()
                    mmW = p.add("tensor", lambda e, th=th, psW=psW: e.matmul(
                        psW[:], sel16[:, th, :], PLb[:], start=True, stop=True,
                        skip_group_check=True),
                        waits=warW + ([t_plb] if th == 0 else []), inc=("mm", 1))
                    tmul = p.op("vector", lambda e, th=th, psW=psW, bsl=bsl: e.tensor_tensor(
                        tmpm[:], psW[:], vT[:, th, bsl], OP.mult), waits=[mmW])
                    if th == 0:
                        p.op("vector", lambda e: e.tensor_copy(acc[:], tmpm[:]), waits=[tmul])
                    else:
                        p.op("vector", lambda e: e.tensor_tensor(
                            acc[:], acc[:], tmpm[:], OP.add), waits=[tmul])
                    pool.release(iW, tmul)
                p.op("vector", lambda e, h=h, bsl=bsl: e.tensor_copy(attn[:, h, bsl], acc[:]))
        t_attn = ("evector", p.cnt["evector"])

        store_toks = {}
        xr_use = {}
        wo = wst
        for qb in range(2):
            qsl = slice(qb * 512, qb * 512 + 512)
            for oc in range(NCH):
                j, wbuf, ldtok = wo.load(d_wo4[oc])
                ld_x_tok = p.dma("sync", xr[:, oc % 2, :], d_x2[oc, :, qsl],
                                 waits=[xr_use.get(oc % 2)], sem=f"qxr{oc % 2}")
                i, ps, war = pool.acquire()
                mmt = mm_group(p, ps[:], [(wbuf[:, ic, :], attn[:, ic, qsl]) for ic in range(NCH)],
                               waits=war + [ldtok, t_attn])
                wo.used(j, mmt)
                ev = p.op("vector", lambda e, oc=oc, ps=ps: e.tensor_tensor(
                    y3pre[:, oc, :], ps[:], xr[:, oc % 2, :], OP.add), waits=[mmt, ld_x_tok])
                xr_use[oc % 2] = ev
                pool.release(i, ev)
                p.op("vector", lambda e, oc=oc, qsl=qsl: e.tensor_copy(
                    y3bf[:, oc, 0:512], y3pre[:, oc, :]))

            def out_cb(c, t_n, rstd_, nmr_, qb=qb, qsl=qsl):
                w = [t_n, store_toks.get(c % 2)]
                tt1 = p.op("vector", lambda e, c=c: e.tensor_tensor(
                    x3rot[:, c % 2, :], y3pre[:, c, :], rstd_[:, :512], OP.mult), waits=w)
                tt2 = p.op("vector", lambda e, c=c: e.tensor_tensor(
                    x3rot[:, c % 2, :], x3rot[:, c % 2, :], nmr_[:, :512], OP.add), waits=[tt1])
                stok = p.dma("sync", d_x3[c, :, qsl], x3rot[:, c % 2, :],
                             waits=[tt2], sem=f"qst{c % 2}")
                store_toks[c % 2] = stok
                return stok

            emit_ln(p, pool, lambda c: y3bf[:, c, 0:512], NCH, 512,
                    sq, mus, rstd, vars_, rstd, nmr, epscol[:], ones128[:],
                    out_cb, stat_waits=[t_init, ("evector", p.cnt["evector"]),
                                        ("mm", p.cnt["mm"])])

        p.add("sync", lambda e: e.nop(),
              waits=[p.qtot(s) for s in sorted(p.cnt) if s.startswith("q")])
        emit_prog(p, st)
    return nc


def build_launchF():
    nc = bass.Bass("TRN2", target_bir_lowering=False)
    d_x3 = nc.dram_tensor("x3", [NCH, 128, TOK], F32, kind="ExternalInput").ap()
    d_w14 = nc.dram_tensor("w14", [16, 128, 16, 128], F32, kind="ExternalInput").ap()
    d_b1 = nc.dram_tensor("b1col", [128, NCH], F32, kind="ExternalInput").ap()
    d_w2 = nc.dram_tensor("w2t", [128, NCH, 8], F32, kind="ExternalInput").ap()
    d_b2 = nc.dram_tensor("b2col", [8, 1], F32, kind="ExternalInput").ap()
    d_va4 = nc.dram_tensor("va4", [4, 128, 16, 128], F32, kind="ExternalInput").ap()
    d_ub4 = nc.dram_tensor("ub4", [16, 128, 4, 128], F32, kind="ExternalInput").ap()
    d_E4 = nc.dram_tensor("E4", [4, 8, 128], BF16, kind="ExternalInput").ap()
    d_I = nc.dram_tensor("I128f", [128, 128], F32, kind="ExternalInput").ap()
    d_Ib = nc.dram_tensor("I128", [128, 128], BF16, kind="ExternalInput").ap()
    d_ones128 = nc.dram_tensor("ones128", [128, 128], BF16, kind="ExternalInput").ap()
    d_eps = nc.dram_tensor("epscol", [128, 1], F32, kind="ExternalInput").ap()
    d_out = nc.dram_tensor("out", [NCH, 128, TOK], F32, kind="ExternalOutput").ap()

    p = Prog(nc)
    with ExitStack() as st:
        sb = lambda name, shape, dt=BF16: st.enter_context(nc.sbuf_tensor(name, list(shape), dt))
        x3bf = sb("s_x3bf", [128, NCH, 512])
        hbuf = sb("s_h", [128, NCH, 512])
        zx = sb("s_zx", [128, 4, 512])
        aex = sb("s_aex", [128, 512])
        y4 = sb("s_y4", [128, NCH, 512], F32)
        y4bf = sb("s_y4b", [128, NCH, 512])
        scT = sb("s_scT", [8, 512], F32)
        alpaT = sb("s_alpaT", [128, 4, 8], F32)
        expT = sb("s_expT", [128, 4, 8], F32)
        anT = sb("s_anT", [128, 4, 8])
        alphaR = sb("s_alphaR", [8, 512])
        negmax4 = sb("s_ngm", [128, 4], F32)
        lsum4 = sb("s_ls4", [128, 4], F32)
        b1col = sb("s_b1", [128, NCH], F32)
        w2t = sb("s_w2", [128, NCH, 8])
        b2col = sb("s_b2", [8, 1], F32)
        E4t = sb("s_E4", [8, 4, 128])
        I128f = sb("s_If", [128, 128], F32)
        I128b = sb("s_Ib", [128, 128])
        ones128 = sb("s_o128", [128, 128])
        epscol = sb("s_eps", [128, 1], F32)
        xr = sb("s_xr", [128, 2, 512], F32)
        orot = sb("s_or", [128, 2, 512], F32)
        sq = [sb(f"s_sq{i}", [128, 512]) for i in range(2)]
        mus = sb("s_mus", [128, 512], F32)
        ex2s = sb("s_ex2s", [128, 512], F32)
        vars_ = sb("s_vars", [128, 512], F32)
        rstd = sb("s_rstd", [128, 512], F32)
        nmr = sb("s_nmr", [128, 512], F32)
        t1 = sb("s_t1", [128, 512], F32)

        pool = PsumPool(p, st, n=6, shape=(128, 512))
        psn8 = st.enter_context(nc.psum_tensor("psn8", [8, 512], F32))
        psb8 = st.enter_context(nc.psum_tensor("psb8", [8, 512], BF16))

        for dst, src in ((b1col, d_b1), (b2col, d_b2), (I128f, d_I),
                         (I128b, d_Ib), (ones128, d_ones128), (epscol, d_eps)):
            p.dma("sync", dst[:], src, sem="qinit")
        p.dma("gpsimd", w2t[:], d_w2, sem="qgw2")
        p.dma("sync", E4t[:], d_E4.rearrange("c e p -> e c p"), sem="qinit")
        t_init = p.qtot("qinit")
        t_gw2 = p.qtot("qgw2")

        w1 = WStream(p, st, "w1", [128, NCH, 128])
        va = WStream(p, st, "va", [128, NCH, 128])
        ub = WStream(p, st, "ub", [128, 4, 128])

        store_toks = {}
        xr_use = {}
        for tb in range(2):
            tsl = slice(tb * 512, tb * 512 + 512)
            ld_x = p.dma("gpsimd", x3bf[:], d_x3.rearrange("c p t -> p c t")[:, :, tsl],
                         sem=f"qx{tb % 2}",
                         waits=[("mm", p.cnt.get("mm", 0))])
            for oc in range(NCH):
                j, buf, ldtok = w1.load(d_w14[oc])
                i, ps, war = pool.acquire()
                mmt = mm_group(p, ps[:], [(buf[:, ic, :], x3bf[:, ic, :]) for ic in range(NCH)],
                               waits=war + [ldtok, ld_x, t_init])
                w1.used(j, mmt)
                ev = p.op("scalar", lambda e, oc=oc, ps=ps: e.activation(
                    hbuf[:, oc, :], ps[:], AF.Relu, bias=b1col[:, oc:oc + 1]),
                    waits=[mmt])
                pool.release(i, ev)
            t_h = ("escalar", p.cnt["escalar"])
            mmt = mm_group(p, psn8[:], [(w2t[:, ic, :], hbuf[:, ic, :]) for ic in range(NCH)],
                           waits=[t_h, t_init, t_gw2, ("escalar", p.cnt["escalar"])])
            t_sc = p.op("scalar", lambda e: e.activation(scT[:], psn8[:], AF.Identity,
                                                         bias=b2col[:]), waits=[mmt])
            tr_toks = []
            for blk in range(4):
                i, ps, war = pool.acquire()
                tt_ = p.add("tensor", lambda e, blk=blk, ps=ps: e.transpose(
                    ps[:, 0:8], scT[:, blk * 128:(blk + 1) * 128], I128f[0:8, 0:8]),
                    waits=war + [t_sc, t_init], inc=("mm", 1))
                ev = p.op("vector", lambda e, blk=blk, ps=ps: e.tensor_copy(
                    alpaT[:, blk, :], ps[:, 0:8]),
                    waits=[tt_, ("escalar", p.cnt["escalar"])])
                pool.release(i, ev)
                tr_toks.append(ev)
            t_ngm = p.op("vector", lambda e: e.reduce_max(
                negmax4[:], alpaT[:], axis=AX.X, negate=True),
                waits=[tr_toks[-1]])
            exp_toks = []
            for blk in range(4):
                exp_toks.append(p.op("scalar", lambda e, blk=blk: e.activation(
                    expT[:, blk, :], alpaT[:, blk, :], AF.Exp, bias=negmax4[:, blk:blk + 1]),
                    waits=[t_ngm]))
            t_ls = p.op("vector", lambda e: e.reduce_sum(
                lsum4[:], expT[:], axis=AX.X), waits=[exp_toks[-1]])
            t_lr = p.op("vector", lambda e: e.reciprocal(lsum4[:], lsum4[:]), waits=[t_ls])
            an_toks = []
            for blk in range(4):
                an_toks.append(p.op("vector", lambda e, blk=blk: e.tensor_scalar_mul(
                    anT[:, blk, :], expT[:, blk, :], lsum4[:, blk:blk + 1]),
                    waits=[t_lr, ("mm", p.cnt["mm"])]))
            trb_toks = []
            for blk in range(4):
                tt_ = p.add("tensor", lambda e, blk=blk: e.transpose(
                    psb8[:, blk * 128:(blk + 1) * 128], anT[:, blk, :], I128b[:]),
                    waits=[an_toks[blk], ("evector", p.cnt["evector"]), t_init],
                    inc=("mm", 1))
                trb_toks.append(tt_)
            t_aR = p.op("vector", lambda e: e.tensor_copy(alphaR[:], psb8[:]),
                        waits=[trb_toks[-1]])
            for krc in range(4):
                j, buf, ldtok = va.load(d_va4[krc])
                i, ps, war = pool.acquire()
                mmv = mm_group(p, ps[:], [(buf[:, ic, :], x3bf[:, ic, :]) for ic in range(NCH)],
                               waits=war + [ldtok])
                va.used(j, mmv)
                i2, ps2, war2 = pool.acquire()
                mme = mm_group(p, ps2[:], [(E4t[:, krc, :], alphaR[:])],
                               waits=war2 + [t_aR, t_init])
                eva = p.op("vector", lambda e, ps2=ps2: e.tensor_copy(aex[:], ps2[:]),
                           waits=[mme])
                pool.release(i2, eva)
                evz = p.op("vector", lambda e, krc=krc, ps=ps: e.tensor_tensor(
                    zx[:, krc, :], ps[:], aex[:], OP.mult), waits=[mmv, eva])
                pool.release(i, evz)
            t_zx = ("evector", p.cnt["evector"])
            for oc in range(NCH):
                j, buf, ldtok = ub.load(d_ub4[oc])
                ldr = p.dma("sync", xr[:, oc % 2, :], d_x3[oc, :, tsl],
                            waits=[xr_use.get(oc % 2)], sem=f"qxr{oc % 2}")
                i, ps, war = pool.acquire()
                mmt = mm_group(p, ps[:], [(buf[:, k_, :], zx[:, k_, :]) for k_ in range(4)],
                               waits=war + [ldtok, t_zx])
                ub.used(j, mmt)
                ev = p.op("vector", lambda e, oc=oc, ps=ps: e.tensor_tensor(
                    y4[:, oc, :], ps[:], xr[:, oc % 2, :], OP.add), waits=[mmt, ldr])
                xr_use[oc % 2] = ev
                pool.release(i, ev)
                p.op("vector", lambda e, oc=oc: e.tensor_copy(y4bf[:, oc, :], y4[:, oc, :]))
            def out_cb(c, t_n, rstd_, nmr_, tb=tb, tsl=tsl):
                w = [t_n, store_toks.get(c % 2)]
                tt1 = p.op("vector", lambda e, c=c: e.tensor_tensor(
                    t1[:], y4[:, c, :], rstd_[:, :512], OP.mult), waits=w)
                tt2 = p.op("vector", lambda e, c=c: e.tensor_tensor(
                    orot[:, c % 2, :], t1[:], nmr_[:, :512], OP.add), waits=[tt1])
                stok = p.dma("sync", d_out[c, :, tsl], orot[:, c % 2, :],
                             waits=[tt2], sem=f"qst{c % 2}")
                store_toks[c % 2] = stok
                return stok

            emit_ln(p, pool, lambda c: y4bf[:, c, :], NCH, 512,
                    sq, mus, ex2s, vars_, rstd, nmr, epscol[:], ones128[:],
                    out_cb, stat_waits=[t_init, ("evector", p.cnt["evector"]),
                                        ("mm", p.cnt["mm"])])

        p.add("sync", lambda e: e.nop(),
              waits=[p.qtot(s) for s in sorted(p.cnt) if s.startswith("q")])
        emit_prog(p, st)
    return nc


def _run_many(nc, in_maps):
    res = run_bass_kernel_spmd(nc, in_maps, core_ids=list(range(8)))
    return res.results


_PROGS = {}


def _prog(name):
    if name not in _PROGS:
        _PROGS[name] = {
            "A": build_launchA, "B": build_launchB, "C": build_launchC,
            "G": build_launchG, "F": build_launchF,
        }[name]()
    return _PROGS[name]


def barrier(p):
    snap = [(s, v) for s, v in sorted(p.cnt.items())]
    for eng in ENGINES:
        p.add(eng, lambda e: e.nop(), waits=snap)


def emit_phaseA(p, st, nc, io, qTs):
    sb = lambda name, shape, dt=BF16: st.enter_context(nc.sbuf_tensor(name, list(shape), dt))
    d_x, d_qkv = io["x"], io["qkv"]
    qkv_r = d_qkv.rearrange("(c p) o -> p c o", p=128)
    xbf = sb("a_xbf", [128, NCH, TOK])
    xnt = sb("a_xnt", [128, NCH, TOK])
    frot = sb("a_frot", [128, 8, 512], F32)
    sq = [sb(f"a_sq{i}", [128, 512]) for i in range(2)]
    mus = sb("a_mus", [128, 512], F32)
    ex2s = sb("a_ex2s", [128, 512], F32)
    vars_ = sb("a_vars", [128, 512], F32)
    rstd = sb("a_rstd", [128, 512], F32)
    nmr = sb("a_nmr", [128, 512], F32)
    t1 = sb("a_t1", [128, 512], F32)
    gam = sb("a_gam", [128, NCH], F32)
    bet = sb("a_bet", [128, NCH], F32)
    ones128 = sb("a_o128", [128, 128])
    epscol = sb("a_eps", [128, 1], F32)
    pool = PsumPool(p, st, n=8, shape=(128, 512), name="a_ps")

    p.dma("gpsimd", xbf[:], d_x.rearrange("c p t -> p c t"), sem="qagx")
    p.dma("sync", gam[:], io["gamma"], sem="qainit")
    p.dma("sync", bet[:], io["beta"], sem="qainit")
    p.dma("sync", ones128[:], io["ones128"], sem="qainit")
    p.dma("sync", epscol[:], io["epscol"], sem="qainit")
    t_init = p.qtot("qainit")
    t_init_g = p.qtot("qagx")

    for tb in range(2):
        t0 = tb * 512

        def out_cb(c, t_n, rstd_, nmr_, tb=tb, t0=t0):
            w = [t_n, t_init]
            tt1 = p.op("vector", lambda e, c=c: e.tensor_tensor(
                t1[:], xbf[:, c, t0:t0 + 512], rstd_[:, :512], OP.mult), waits=w)
            tt2 = p.op("vector", lambda e: e.tensor_tensor(
                t1[:], t1[:], nmr_[:, :512], OP.add), waits=[tt1])
            return p.op("vector", lambda e, c=c: e.tensor_scalar(
                xnt[:, c, t0:t0 + 512], t1[:], gam[:, c:c + 1], bet[:, c:c + 1],
                OP.mult, OP.add), waits=[tt2])

        emit_ln(p, pool, lambda c, t0=t0: xbf[:, c, t0:t0 + 512], NCH, 512,
                sq, mus, ex2s, vars_, rstd, nmr, epscol[:], ones128[:],
                out_cb, stat_waits=[t_init, t_init_g, ("mm", p.cnt.get("mm", 0))])

    t_xnt = ("evector", p.cnt["evector"])

    rg = [[0, 1], [2, 3], [4, 5], [6, 7]]
    wq = WStream(p, st, "aw", [128, NCH, 128])
    slot_use = {}
    for oc in range(NCH):
        j, buf, ldtok = wq.load(qkv_r[:, :, D + oc * 128:D + (oc + 1) * 128])
        last_use = None
        for tb in range(2):
            t0 = tb * 512
            i, ps, war = pool.acquire()
            tiles = [(buf[:, ic, :], xnt[:, ic, t0:t0 + 512]) for ic in range(NCH)]
            mmtok = mm_group(p, ps[:], tiles, waits=war + [ldtok, t_xnt])
            last_use = mmtok
            slot = (oc * 2 + tb) % 8
            ev = p.op("vector", lambda e, slot=slot, ps=ps: e.tensor_copy(
                frot[:, slot, :], ps[:]), waits=[mmtok, slot_use.get(slot)])
            pool.release(i, ev)
            stok = p.dma("sync", io["agk_in"][oc * 128:(oc + 1) * 128, t0:t0 + 512],
                         frot[:, slot, :], waits=[ev], sem=f"qas{slot}")
            slot_use[slot] = stok
        wq.used(j, last_use)
    p.add("gpsimd", lambda e: e.collective_compute(
        "AllGather", mybir.AluOpType.bypass, replica_groups=rg,
        ins=[io["agk_in"].tensor.ap().opt()], outs=[io["agk_out"].tensor.ap().opt()]),
        waits=[p.qtot(s) for s in sorted(p.cnt) if s.startswith("qas")],
        inc=("cck", 1))

    wkv = WStream(p, st, "awv", [128, 512])
    for ob in range(4):
        o0 = 2 * D + ob * 512
        pss = [pool.acquire() for _ in range(8)]
        mmtok = None
        for ic in range(NCH):
            j, buf, ldtok = wkv.load(qkv_r[:, ic, o0:o0 + 512])

            def emit_mm(e, ic=ic, buf=buf, pss=pss):
                ins = None
                for tt in range(8):
                    ins = e.matmul(pss[tt][1][:], xnt[:, ic, tt * 128:(tt + 1) * 128],
                                   buf[:], start=(ic == 0), stop=(ic == NCH - 1),
                                   skip_group_check=True)
                return ins
            w = [ldtok, t_xnt] + (sum((pw for _, _, pw in pss), []) if ic == 0 else [])
            mmtok = p.add("tensor", emit_mm, waits=w, inc=("mm", 1))
            wkv.used(j, mmtok)
        for tt in range(8):
            i, ps, _ = pss[tt]
            ev = p.op("vector", lambda e, tt=tt, ps=ps: e.tensor_copy(frot[:, tt, :], ps[:]),
                      waits=[mmtok, slot_use.get(tt)])
            pool.release(i, ev)
            stok = p.dma("sync", io["agv_in"][tt * 128:(tt + 1) * 128, ob * 512:ob * 512 + 512],
                         frot[:, tt, :], waits=[ev], sem=f"qas{tt}")
            slot_use[tt] = stok
    p.add("gpsimd", lambda e: e.collective_compute(
        "AllGather", mybir.AluOpType.bypass, replica_groups=rg,
        ins=[io["agv_in"].tensor.ap().opt()], outs=[io["agv_out"].tensor.ap().opt()]),
        waits=[p.qtot(s) for s in sorted(p.cnt) if s.startswith("qas")],
        inc=("ccv", 1))

    for oc in range(NCH):
        j, buf, ldtok = wq.load(qkv_r[:, :, oc * 128:(oc + 1) * 128])
        last_use = None
        for tb in range(2):
            t0 = tb * 512
            i, ps, war = pool.acquire()
            tiles = [(buf[:, ic, :], xnt[:, ic, t0:t0 + 512]) for ic in range(NCH)]
            mmtok = mm_group(p, ps[:], tiles, waits=war + [ldtok, t_xnt])
            last_use = mmtok
            ev = p.op("vector", lambda e, oc=oc, t0=t0, ps=ps: e.tensor_copy(
                qTs[:, oc, t0:t0 + 512], ps[:]), waits=[mmtok])
            pool.release(i, ev)
        wq.used(j, last_use)
    return [p.qtot("cck"), p.qtot("ccv")]
def emit_phaseAG(p, io):
    waits = [p.qtot(s) for s in sorted(p.cnt) if s.startswith("qas")]
    rg = [[0, 1], [2, 3], [4, 5], [6, 7]]
    p.add("gpsimd", lambda e: e.collective_compute(
        "AllGather", mybir.AluOpType.bypass, replica_groups=rg,
        ins=[io["agk_in"].tensor.ap().opt()], outs=[io["agk_out"].tensor.ap().opt()]),
        waits=waits, inc=("cck", 1))
    p.add("gpsimd", lambda e: e.collective_compute(
        "AllGather", mybir.AluOpType.bypass, replica_groups=rg,
        ins=[io["agv_in"].tensor.ap().opt()], outs=[io["agv_out"].tensor.ap().opt()]),
        inc=("ccv", 1))
    return [p.qtot("cck"), p.qtot("ccv")]


def emit_phaseB(p, st, nc, io, qTs, t_ag):
    sb = lambda name, shape, dt=BF16: st.enter_context(nc.sbuf_tensor(name, list(shape), dt))
    d_kT = io["agk_out"].rearrange("(h c p) t -> h p c t", h=2, p=128)
    d_v = io["agv_out"].rearrange("(kt p) d -> p kt d", p=128)
    d_x, d_x1 = io["x"], io["x1s"]

    qT = qTs
    PT = sb("b_PT", [128, NCH, 512])
    AVn = sb("b_AVn", [128, NCH, 512])
    y1pre = sb("b_y1p", [128, NCH, 512], F32)
    y1bf = sb("b_y1b", [128, NCH, 512])
    x1rot = sb("b_x1r", [128, 4, 512], F32)
    xr = sb("b_xr", [128, 2, 512], F32)
    masks = sb("b_mask", [128, 16, 512])
    ones128 = sb("b_o128", [128, 128])
    ones_col = sb("b_oc", [128, 1])
    ones1 = sb("b_o1", [1, 128])
    epscol = sb("b_eps", [128, 1], F32)
    sqn = [sb(f"b_sqn{i}", [128, D]) for i in range(2)]
    qn2 = sb("b_qn2", [1, TOK], F32)
    kn2 = sb("b_kn2", [1, D], F32)
    sc1 = sb("b_sc1", [1, 1], F32)
    sc2 = sb("b_sc2", [1, 1], F32)
    sc2b = sb("b_sc2b", [1, 1], BF16)
    biascol = sb("b_bias", [128, 1], F32)
    linv = sb("b_linv", [128, 512], F32)
    sq = [sb(f"b_sq{i}", [128, 512]) for i in range(2)]
    mus = sb("b_mus", [128, 512], F32)
    ex2s = sb("b_ex2s", [128, 512], F32)
    vars_ = sb("b_vars", [128, 512], F32)
    rstd = sb("b_rstd", [128, 512], F32)
    nmr = sb("b_nmr", [128, 512], F32)
    t1 = sb("b_t1", [128, 512], F32)

    pool = PsumPool(p, st, n=6, shape=(128, 512), name="b_ps")
    psn = [st.enter_context(nc.psum_tensor(f"b_psn{i}", [1, 512], F32)) for i in range(2)]

    p.dma("sync", ones128[:], io["ones128"], sem="qbinit")
    p.dma("sync", ones_col[:], io["ones_col"], sem="qbinit")
    p.dma("sync", ones1[:], io["ones1"], sem="qbinit")
    p.dma("sync", epscol[:], io["epscol"], sem="qbinit")
    t_init = p.qtot("qbinit")

    kst = WStream(p, st, "bk", [128, NCH, 128], queue="gpsimd")
    vst = WStream(p, st, "bv", [128, NCH, 128], queue="gpsimd")
    wo = WStream(p, st, "bw", [128, NCH, 128], queue="gpsimd")

    def ksrc(kc):
        half = kc // 8
        c0 = (kc % 8) * 128
        return d_kT[half, :, :, c0:c0 + 128]

    qn_toks = []
    for dc in range(NCH):
        sw = [("mm", p.cnt["mm"])] if dc == 0 else [qn_toks[dc - 2]] if dc >= 2 else []
        sqtok = p.op("scalar", lambda e, dc=dc: e.activation(
            sqn[dc % 2][:, 0:TOK], qT[:, dc, :], AF.Square), waits=sw)

        def emm(e, dc=dc):
            e.matmul(psn[0][:], ones_col[:], sqn[dc % 2][:, 0:512],
                     start=(dc == 0), stop=(dc == NCH - 1), skip_group_check=True)
            return e.matmul(psn[1][:], ones_col[:], sqn[dc % 2][:, 512:1024],
                            start=(dc == 0), stop=(dc == NCH - 1), skip_group_check=True)
        mw = [sqtok] + ([t_init] if dc == 0 else [])
        qn_toks.append(p.add("tensor", emm, waits=mw, inc=("mm", 1)))
    e1 = p.op("vector", lambda e: e.tensor_copy(qn2[:, 0:512], psn[0][:]), waits=[qn_toks[-1]])
    e2 = p.op("vector", lambda e: e.tensor_copy(qn2[:, 512:1024], psn[1][:]), waits=[qn_toks[-1]])
    t_qmax = p.op("vector", lambda e: e.reduce_max(sc1[:], qn2[:], axis=AX.X), waits=[e1, e2])

    kn_toks = [t_qmax]
    for kc in range(NCH):
        j, kbuf, ldtok = kst.load(ksrc(kc), extra_waits=t_ag if kc < 2 else ())
        sqtok = p.op("scalar", lambda e, kbuf=kbuf, kc=kc: e.activation(
            sqn[kc % 2][:, 0:D], kbuf[:].rearrange("p c k -> p (c k)"), AF.Square),
            waits=[ldtok] + ([qn_toks[-1]] if kc < 2 else []))

        def emm(e, kc=kc):
            ins = None
            for c in range(NCH):
                ins = e.matmul(psn[0][:, 0:128], ones_col[:],
                               sqn[kc % 2][:, c * 128:(c + 1) * 128],
                               start=(c == 0), stop=(c == NCH - 1), skip_group_check=True)
            return ins
        mmt = p.add("tensor", emm, waits=[sqtok, kn_toks[-1]], inc=("mm", 1))
        kst.used(j, mmt)
        ev = p.op("vector", lambda e, kc=kc: e.tensor_copy(
            kn2[:, kc * 128:(kc + 1) * 128], psn[0][:, 0:128]), waits=[mmt])
        kn_toks.append(ev)
    t_kmax = p.op("vector", lambda e: e.reduce_max(sc2[:], kn2[:], axis=AX.X), waits=[kn_toks[-1]])
    t_m2 = p.op("vector", lambda e: e.tensor_tensor(sc1[:], sc1[:], sc2[:], OP.mult),
                waits=[t_qmax, t_kmax])
    t_m = p.op("scalar", lambda e: e.activation(sc1[:], sc1[:], AF.Sqrt), waits=[t_m2])
    t_b0 = p.op("scalar", lambda e: e.mul(sc2b[:], sc1[:], -1.0 / math.sqrt(D)), waits=[t_m])
    ib, psb, warb = pool.acquire()
    t_bb = mm_group(p, psb[:, 0:1], [(ones1[:], sc2b[:])], waits=warb + [t_b0, t_init])
    t_bias = p.op("scalar", lambda e, psb=psb: e.copy(biascol[:], psb[:, 0:1]), waits=[t_bb])
    pool.release(ib, t_bias)

    scale = 1.0 / math.sqrt(D)
    store_toks = {}
    xr_use = {}
    mask_use = None
    for qb in range(2):
        qsl = slice(qb * 512, qb * 512 + 512)
        live = list(range(NCH))
        ld_m = p.dma("sync", masks[:], io["masks"][qb].rearrange("m p q -> p m q"),
                     waits=[mask_use], sem="qbmask")
        for kc in live:
            j, kbuf, ldtok = kst.load(ksrc(kc))
            i, ps, war = pool.acquire()
            tiles = [(kbuf[:, dc, :], qT[:, dc, qsl]) for dc in range(NCH)]
            mmt = mm_group(p, ps[:], tiles, waits=war + [ldtok])
            kst.used(j, mmt)
            t_exp = p.op("scalar", lambda e, kc=kc, ps=ps: e.activation(
                PT[:, kc, :], ps[:], AF.Exp, bias=biascol[:], scale=scale),
                waits=[mmt, t_bias])
            lastPT = p.op("vector", lambda e, kc=kc: e.tensor_tensor(
                PT[:, kc, :], PT[:, kc, :], masks[:, kc, :], OP.mult),
                waits=[t_exp, ld_m])
            pool.release(i, lastPT)
        mask_use = ("evector", p.cnt["evector"])
        t_allPT = [("escalar", p.cnt["escalar"]), ("evector", p.cnt["evector"])]
        i, ps, war = pool.acquire()
        mmt = mm_group(p, ps[:], [(ones128[:], PT[:, kc, :]) for kc in live],
                       waits=war + t_allPT + [t_init])
        t_rec = p.op("vector", lambda e, ps=ps: e.reciprocal(linv[:], ps[:]), waits=[mmt])
        pool.release(i, t_rec)
        for dv in range(NCH):
            j, vbuf, ldtok = vst.load(d_v[:, :, dv * 128:(dv + 1) * 128],
                                      extra_waits=t_ag if qb == 0 and dv < 2 else ())
            i, ps, war = pool.acquire()
            tiles = [(vbuf[:, kc, :], PT[:, kc, :]) for kc in live]
            mmt = mm_group(p, ps[:], tiles, waits=war + [ldtok])
            vst.used(j, mmt)
            ev = p.op("vector", lambda e, dv=dv, ps=ps: e.tensor_tensor(
                AVn[:, dv, :], ps[:], linv[:], OP.mult), waits=[mmt, t_rec])
            pool.release(i, ev)
        t_avn = [("evector", p.cnt["evector"])]
        for oc in range(NCH):
            j, wbuf, ldtok = wo.load(io["o_proj"].rearrange("(c p) o -> p c o", p=128)
                                     [:, :, oc * 128:(oc + 1) * 128])
            ld_x_tok = p.dma("sync", xr[:, oc % 2, :], d_x[oc, :, qsl],
                             waits=[xr_use.get(oc % 2)], sem=f"qbxr{oc % 2}")
            i, ps, war = pool.acquire()
            tiles = [(wbuf[:, ic, :], AVn[:, ic, :]) for ic in range(NCH)]
            mmt = mm_group(p, ps[:], tiles, waits=war + [ldtok] + t_avn)
            wo.used(j, mmt)
            ev = p.op("vector", lambda e, oc=oc, ps=ps: e.scalar_tensor_tensor(
                y1pre[:, oc, :], xr[:, oc % 2, :], 2.0, ps[:], OP.mult, OP.add),
                waits=[mmt, ld_x_tok])
            xr_use[oc % 2] = ev
            pool.release(i, ev)
            p.op("vector", lambda e, oc=oc: e.tensor_copy(y1bf[:, oc, :], y1pre[:, oc, :]))

        def out_cb(c, t_n, rstd_, nmr_, qb=qb, qsl=qsl):
            w = [t_n, store_toks.get(c % 4)]
            tt1 = p.op("vector", lambda e, c=c: e.tensor_tensor(
                t1[:], y1pre[:, c, :], rstd_[:, :512], OP.mult), waits=w)
            tt2 = p.op("vector", lambda e, c=c: e.tensor_tensor(
                x1rot[:, c % 4, :], t1[:], nmr_[:, :512], OP.add), waits=[tt1])
            stok = p.dma("sync", d_x1[c, :, qsl], x1rot[:, c % 4, :],
                         waits=[tt2], sem=f"qbst{c % 4}")
            store_toks[c % 4] = stok
            return stok

        emit_ln(p, pool, lambda c: y1bf[:, c, :], NCH, 512,
                sq, mus, ex2s, vars_, rstd, nmr, epscol[:], ones128[:],
                out_cb, stat_waits=[t_init, ("evector", p.cnt["evector"]),
                                    ("mm", p.cnt["mm"])])


def emit_phaseC(p, st, nc, io):
    sb = lambda name, shape, dt=BF16: st.enter_context(nc.sbuf_tensor(name, list(shape), dt))
    d_x1, d_x2 = io["x1s"], io["x2s"]
    x1bf = sb("c_x1bf", [128, NCH, 512])
    zt = x1bf
    tt = sb("c_tt", [128, NCH, 512])
    mid = sb("c_mid", [128, 64, 512])
    tmp16 = sb("c_tmp16", [128, 16, 512])
    y2 = sb("c_y2", [128, NCH, 512], F32)
    g1f = sb("c_g1f", [128, 512], F32)
    silb = sb("c_silb", [128, 512])
    xr = sb("c_xr", [128, 2, 512], F32)
    x2rot = sb("c_x2r", [128, 2, 512], F32)
    C128 = sb("c_C128", [128, 128])
    suug = sb("c_suug", [128, NCH], F32)
    sudp = sb("c_sudp", [128, 64], F32)
    wsu = sb("c_wsu", [128, 64], F32)
    wsg = sb("c_wsg", [128, 64], F32)
    wsdp = sb("c_wsdp", [128, NCH], F32)
    wsud = sb("c_wsud", [128, 64], F32)
    ones128 = sb("c_o128", [128, 128])
    epscol = sb("c_eps", [128, 1], F32)
    sq = [sb(f"c_sq{i}", [128, 512]) for i in range(2)]
    mus = sb("c_mus", [128, 512], F32)
    ex2s = sb("c_ex2s", [128, 512], F32)
    vars_ = sb("c_vars", [128, 512], F32)
    rstd = sb("c_rstd", [128, 512], F32)
    nmr = sb("c_nmr", [128, 512], F32)
    t1 = sb("c_t1", [128, 512], F32)

    pool = PsumPool(p, st, n=6, shape=(128, 512), name="c_ps")
    for dst, src in ((C128, io["C128"]), (suug, io["suug"]), (sudp, io["sudp"]),
                     (wsu, io["wsu"]), (wsg, io["wsg"]), (wsdp, io["wsdp"]),
                     (ones128, io["ones128"]), (epscol, io["epscol"])):
        p.dma("sync", dst[:], src, sem="qcinit")
    t_init = p.qtot("qcinit")
    t_wsud = p.op("vector", lambda e: e.tensor_tensor(wsud[:], wsu[:], sudp[:], OP.mult),
                  waits=[t_init])

    wA = WStream(p, st, "cwA", [128, NCH, 128])
    wB = WStream(p, st, "cwB", [128, NCH, 128])
    wD = WStream(p, st, "cwD", [128, 16, 128])

    store_toks = {}
    xr_use = {}
    for tb in range(2):
        tsl = slice(tb * 512, tb * 512 + 512)
        ld_x = p.dma("gpsimd", x1bf[:], d_x1.rearrange("c p t -> p c t")[:, :, tsl],
                     sem=f"qcx{tb % 2}",
                     waits=[("evector", p.cnt.get("evector", 0))])
        zt_toks = []
        for c in range(NCH):
            zt_toks.append(p.op("vector", lambda e, c=c: e.tensor_scalar_mul(
                zt[:, c, :], x1bf[:, c, :], suug[:, c:c + 1]),
                waits=[ld_x, t_init, ("mm", p.cnt.get("mm", 0))] if c == 0 else []))
        t_zt = zt_toks[-1]
        for oc in range(NCH):
            j, buf, ldtok = wA.load(io["H4"][oc])
            i, ps, war = pool.acquire()
            mmt = mm_group(p, ps[:], [(buf[:, ic, :], zt[:, ic, :]) for ic in range(NCH)],
                           waits=war + [ldtok, t_zt])
            wA.used(j, mmt)
            ev = p.op("vector", lambda e, oc=oc, ps=ps: e.tensor_scalar_mul(
                tt[:, oc, :], ps[:], 1.0 / math.sqrt(D)), waits=[mmt])
            pool.release(i, ev)
        t_tt = ("evector", p.cnt["evector"])
        for oc in range(64):
            ju, bu, ldu = wA.load(io["up4"][oc])
            jg, bg, ldg = wB.load(io["gt4"][oc])
            iu, psu, waru = pool.acquire()
            mmu = mm_group(p, psu[:], [(bu[:, ic, :], tt[:, ic, :]) for ic in range(NCH)],
                           waits=waru + [ldu, t_tt])
            wA.used(ju, mmu)
            ig, psg, warg = pool.acquire()
            mmg = mm_group(p, psg[:], [(bg[:, ic, :], tt[:, ic, :]) for ic in range(NCH)],
                           waits=warg + [ldg, t_tt])
            wB.used(jg, mmg)
            tg1 = p.op("vector", lambda e, oc=oc, psg=psg: e.tensor_scalar_mul(
                g1f[:], psg[:], wsg[:, oc:oc + 1]), waits=[mmg])
            pool.release(ig, tg1)
            tsil = p.op("scalar", lambda e: e.activation(silb[:], g1f[:], AF.Silu),
                        waits=[tg1])
            tmid = p.op("vector", lambda e, oc=oc, psu=psu: e.scalar_tensor_tensor(
                mid[:, oc, :], psu[:], wsud[:, oc:oc + 1], silb[:], OP.mult, OP.mult),
                waits=[mmu, tsil, t_wsud])
            pool.release(iu, tmid)
        for a in range(64):
            i, ps, war = pool.acquire()
            mmt = mm_group(p, ps[:], [(C128[:], mid[:, a, :])],
                           waits=war + ([t_init, ("evector", p.cnt["evector"])] if a == 0 else []))
            ev = p.op("vector", lambda e, a=a, ps=ps: e.tensor_scalar_mul(
                mid[:, a, :], ps[:], 1.0 / math.sqrt(INTER)), waits=[mmt])
            pool.release(i, ev)
        midf = mid[:].rearrange("p c e -> p (c e)")
        tmpf = tmp16[:].rearrange("p c e -> p (c e)")
        for s_ in range(6):
            dd = 1 << s_
            views = []
            if 32 // dd >= 2:
                gh = (32 // dd) // 2
                v_ = midf.rearrange("p (g two dd e) -> p g two dd e", two=2, dd=dd, e=512)
                tv_ = tmpf.rearrange("p (g dd e) -> p g dd e", dd=dd, e=512)
                for h in range(2):
                    views.append((v_[:, h * gh:(h + 1) * gh, 0, :, :],
                                  v_[:, h * gh:(h + 1) * gh, 1, :, :],
                                  tv_[:, 0:gh, :, :]))
            else:
                v_ = midf.rearrange("p (two dd e) -> p two dd e", dd=32, e=512)
                tv_ = tmpf.rearrange("p (dd e) -> p dd e", e=512)
                for h in range(2):
                    views.append((v_[:, 0, h * 16:(h + 1) * 16, :],
                                  v_[:, 1, h * 16:(h + 1) * 16, :],
                                  tv_[:, 0:16, :]))
            for lo, hi, tv in views:
                p.op("vector", lambda e, lo=lo, hi=hi, tv=tv: e.tensor_tensor(tv, lo, hi, OP.add))
                p.op("vector", lambda e, lo=lo, hi=hi: e.tensor_tensor(hi, lo, hi, OP.subtract))
                p.op("vector", lambda e, lo=lo, tv=tv: e.tensor_copy(lo, tv))
        t_bfly = ("evector", p.cnt["evector"])
        for oc in range(NCH):
            ldr = p.dma("sync", xr[:, oc % 2, :], d_x1[oc, :, tsl],
                        waits=[xr_use.get(oc % 2)], sem=f"qcxr{oc % 2}")
            i, ps, war = pool.acquire()
            mmq = None
            for qtr in range(4):
                jq, bq, ldq = wD.load(io["dn4"][oc, :, 16 * qtr:16 * qtr + 16, :])
                w = [ldq]
                if qtr == 0:
                    w += war + [t_bfly]
                mmq = mm_group(
                    p, ps[:],
                    [(bq[:, ic, :], mid[:, 16 * qtr + ic, :]) for ic in range(16)],
                    waits=w, start=(qtr == 0), stop=(qtr == 3))
                wD.used(jq, mmq)
            ev = p.op("vector", lambda e, oc=oc, ps=ps: e.scalar_tensor_tensor(
                y2[:, oc, :], ps[:], wsdp[:, oc:oc + 1], xr[:, oc % 2, :], OP.mult, OP.add),
                waits=[mmq, ldr])
            xr_use[oc % 2] = ev
            pool.release(i, ev)
        for c in range(NCH):
            p.op("vector", lambda e, c=c: e.tensor_copy(tt[:, c, :], y2[:, c, :]),
                 waits=[("mm", p.cnt["mm"])] if c == 0 else [])
        t_cast = ("evector", p.cnt["evector"])

        def out_cb(c, t_n, rstd_, nmr_, tb=tb, tsl=tsl):
            w = [t_n, store_toks.get(c % 2)]
            tt1 = p.op("vector", lambda e, c=c: e.tensor_tensor(
                t1[:], y2[:, c, :], rstd_[:, :512], OP.mult), waits=w)
            tt2 = p.op("vector", lambda e, c=c: e.tensor_tensor(
                x2rot[:, c % 2, :], t1[:], nmr_[:, :512], OP.add), waits=[tt1])
            stok = p.dma("sync", d_x2[c, :, tsl], x2rot[:, c % 2, :],
                         waits=[tt2], sem=f"qcst{c % 2}")
            store_toks[c % 2] = stok
            return stok

        emit_ln(p, pool, lambda c: tt[:, c, :], NCH, 512,
                sq, mus, ex2s, vars_, rstd, nmr, epscol[:], ones128[:],
                out_cb, stat_waits=[t_init, t_cast, ("mm", p.cnt["mm"])])


def emit_phaseG(p, st, nc, io):
    sb = lambda name, shape, dt=BF16: st.enter_context(nc.sbuf_tensor(name, list(shape), dt))
    d_x2, d_x3 = io["x2s"], io["x3s"]
    wqT_r = io["wqT"].rearrange("(c p) o -> p c o", p=128)
    wkT_r = io["wkT"].rearrange("(c p) o -> p c o", p=128)
    wvT_r = io["wvT"].rearrange("(c p) o -> p c o", p=128)

    x2bf = sb("g_x2bf", [128, NCH, TOK])
    attn = x2bf
    qT = sb("g_qT", [128, NCH, TOK])
    kT = sb("g_kT", [128, NCH, TOK])
    vT = sb("g_vT", [128, NCH, TOK])
    y3pre = sb("g_y3p", [128, NCH, 512], F32)
    prod4 = [sb(f"g_prod{i}", [128, 2, 512]) for i in range(2)]
    sqs = prod4[0][:, 0, :]
    acc = [sb(f"g_acc{i}", [128, 512], F32) for i in range(2)]
    tmp2 = [sb(f"g_tmpm{i}", [128, 512]) for i in range(4)]
    wbb = [sb(f"g_wbb{i}", [128, 512]) for i in range(4)]
    PL = [sb(f"g_PL{i}", [16, 512], F32) for i in range(2)]
    PLb = [sb(f"g_PLb{i}", [16, 512]) for i in range(2)]
    biasG = sb("g_biasG", [16, 1], F32)
    sc1 = sb("g_sc1", [1, 1], F32)
    sc2 = sb("g_sc2", [1, 1], F32)
    lrow = sb("g_lrow", [1, TOK], F32)
    nrm2 = lrow
    E16t = sb("g_E16", [128, 16, 16])
    sel16 = sb("g_sel", [16, 16, 128])
    ones1_16 = sb("g_o116", [1, 16])
    ones1_16f = sb("g_o116f", [1, 16], F32)
    ones16cf = sb("g_o16cf", [16, 1], F32)
    ones_col = sb("g_oc", [128, 1])
    ones128 = sb("g_o128", [128, 128])
    epscol = sb("g_eps", [128, 1], F32)
    xr = sb("g_xr", [128, 1, 512], F32)
    sq = [prod4[0][:, 0, :], prod4[1][:, 0, :]]
    vars_ = sb("g_vars", [128, 512], F32)
    rstd = sb("g_rstd", [128, 512], F32)
    mus = acc[0]
    nmr = acc[1]
    y3bf = kT

    pool = PsumPool(p, st, n=6, shape=(128, 512), name="g_ps")
    psn = [st.enter_context(nc.psum_tensor(f"g_psn{i}", [1, 512], F32)) for i in range(2)]

    for dst, src in ((E16t, io["E16"].rearrange("t p k -> p t k")), (sel16, io["sel16"]),
                     (ones1_16, io["ones1_16"]), (ones1_16f, io["ones1_16f"]),
                     (ones16cf, io["ones16cf"]),
                     (ones_col, io["ones_col"]), (ones128, io["ones128"]),
                     (epscol, io["epscol"])):
        p.dma("sync", dst[:], src, sem="qginit")
    p.dma("gpsimd", x2bf[:], d_x2.rearrange("c p t -> p c t"), sem="qggx")
    t_init = p.qtot("qginit")
    t_gx = p.qtot("qggx")

    wst = WStream(p, st, "gw", [128, NCH, 128])

    for dstT, wsrc in ((qT, wqT_r), (kT, wkT_r), (vT, wvT_r)):
        for oc in range(NCH):
            j, buf, ldtok = wst.load(wsrc[:, :, oc * 128:(oc + 1) * 128])
            last = None
            for tb in range(2):
                tsl = slice(tb * 512, tb * 512 + 512)
                i, ps, war = pool.acquire()
                mmt = mm_group(p, ps[:], [(buf[:, ic, :], x2bf[:, ic, tsl]) for ic in range(NCH)],
                               waits=war + [ldtok, t_gx])
                last = mmt
                ev = p.op("vector", lambda e, dstT=dstT, oc=oc, tsl=tsl, ps=ps:
                          e.tensor_copy(dstT[:, oc, tsl], ps[:]), waits=[mmt])
                pool.release(i, ev)
            wst.used(j, last)
    t_proj = ("evector", p.cnt["evector"])

    kn_toks = []
    for src_T, dst_sc in ((kT, sc1), (qT, sc2)):
        for c in range(NCH):
            for blk in range(2):
                bsl = slice(blk * 512, blk * 512 + 512)
                sw = [t_proj] if not kn_toks else [kn_toks[-1]]
                tsqk = p.op("scalar", lambda e, src_T=src_T, c=c, bsl=bsl: e.activation(
                    sqs, src_T[:, c, bsl], AF.Square), waits=sw)
                mmt = p.add("tensor", lambda e, c=c, blk=blk: e.matmul(
                    psn[blk][:], ones_col[:], sqs, start=(c == 0), stop=(c == NCH - 1),
                    skip_group_check=True),
                    waits=[tsqk] + ([t_init] if len(kn_toks) == 0 else []), inc=("mm", 1))
                kn_toks.append(mmt)
        e1 = p.op("vector", lambda e: e.tensor_copy(nrm2[:, 0:512], psn[0][:]),
                  waits=[kn_toks[-1]])
        e2 = p.op("vector", lambda e: e.tensor_copy(nrm2[:, 512:1024], psn[1][:]),
                  waits=[kn_toks[-1]])
        kn_toks.append(p.op("vector", lambda e, dst_sc=dst_sc: e.reduce_max(
            dst_sc[:], nrm2[:], axis=AX.X), waits=[e1, e2]))
    t1_ = p.op("vector", lambda e: e.tensor_tensor(sc1[:], sc1[:], sc2[:], OP.mult),
               waits=[kn_toks[-1]])
    t2_ = p.op("scalar", lambda e: e.activation(sc1[:], sc1[:], AF.Sqrt), waits=[t1_])
    t3_ = p.op("vector", lambda e: e.tensor_scalar(
        sc1[:], sc1[:], -1.0 / (math.sqrt(HD) * 4.0), 87.0, OP.mult, OP.add), waits=[t2_])
    ib, psb, warb = pool.acquire()
    t4_ = p.add("tensor", lambda e, psb=psb: e.matmul(
        psb[0:16, 0:1], ones1_16f[:], sc1[:], start=True, stop=True,
        skip_group_check=True), waits=warb + [t3_, t_init], inc=("mm", 1))
    t_bias = p.op("scalar", lambda e, psb=psb: e.copy(biasG[:], psb[0:16, 0:1]), waits=[t4_])
    pool.release(ib, t_bias)

    scale = 1.0 / math.sqrt(HD)
    NCASC = 2
    acc_tok = [None, None]
    for h in range(NH):
        S = [{}, {}]
        for blk in range(2):
            bsl = slice(blk * 512, blk * 512 + 512)
            ip0, ps0, war0 = pool.acquire()
            pt_toks = []
            qbc = qT[:, h:h + 1, bsl].broadcast_to((128, 2, 512))
            for tq in range(8):
                pb = prod4[tq % 2]
                pw = []
                if h == 0 and blk == 0 and tq == 0:
                    pw.append(t_proj)
                if tq >= 2:
                    pw.append(pt_toks[2 * (tq - 2) + 1])
                if tq < 2:
                    pw.append(("mm", p.cnt["mm"]))
                t_pr = p.op("vector", lambda e, h=h, tq=tq, pb=pb, bsl=bsl, qbc=qbc:
                            e.tensor_tensor(pb[:], qbc, kT[:, 2 * tq:2 * tq + 2, bsl],
                                            OP.mult), waits=pw)
                for i4 in range(2):
                    th = 2 * tq + i4
                    pt_toks.append(p.add("tensor", lambda e, th=th, i4=i4, pb=pb, ps0=ps0: e.matmul(
                        ps0[0:16, :], E16t[:, th, :], pb[:, i4, :],
                        start=(th == 0), stop=(th == NH - 1),
                        skip_group_check=True),
                        waits=[t_pr, t_init] + (war0 if th == 0 else []),
                        inc=("mm", 1)))
            tx0 = p.op("scalar", lambda e, ps0=ps0, blk=blk: e.activation(
                PL[blk][:], ps0[0:16, :], AF.Exp, scale=scale / (1 << NCASC),
                bias=biasG[:]), waits=[pt_toks[-1], t_bias])
            pool.release(ip0, tx0)
            S[blk]["prev"] = tx0

        for r_ in range(NCASC + 1):
            mls = {}
            for blk in range(2):
                iL0, psL0, warL0 = pool.acquire()
                mml = p.add("tensor", lambda e, psL0=psL0, blk=blk: e.matmul(
                    psL0[0:1, :], ones16cf[:], PL[blk][:], start=True, stop=True,
                    skip_group_check=True), waits=warL0 + [S[blk]["prev"]], inc=("mm", 1))
                mls[blk] = (iL0, psL0, mml)
            tls = {}
            for blk in range(2):
                iL0, psL0, mml = mls[blk]
                lsl = slice(blk * 512, blk * 512 + 512)
                tl0 = p.op("vector", lambda e, psL0=psL0, lsl=lsl: e.tensor_copy(
                    lrow[:, lsl], psL0[0:1, :]), waits=[mml])
                pool.release(iL0, tl0)
                tls[blk] = p.op("vector", lambda e, lsl=lsl: e.reciprocal(
                    lrow[:, lsl], lrow[:, lsl]), waits=[tl0])
            mbs = {}
            for blk in range(2):
                iB0, psB0, warB0 = pool.acquire()
                lsl = slice(blk * 512, blk * 512 + 512)
                mmB = p.add("tensor", lambda e, psB0=psB0, lsl=lsl: e.matmul(
                    psB0[0:16, :], ones1_16f[:], lrow[:, lsl], start=True, stop=True,
                    skip_group_check=True), waits=warB0 + [tls[blk]], inc=("mm", 1))
                mbs[blk] = (iB0, psB0, mmB)
            for blk in range(2):
                iB0, psB0, mmB = mbs[blk]
                if r_ < NCASC:
                    tn0 = p.op("vector", lambda e, psB0=psB0, blk=blk: e.tensor_tensor(
                        PL[blk][:], PL[blk][:], psB0[0:16, :], OP.mult), waits=[mmB])
                    S[blk]["prev"] = p.op("vector", lambda e, blk=blk: e.tensor_tensor(
                        PL[blk][:], PL[blk][:], PL[blk][:], OP.mult), waits=[tn0])
                else:
                    S[blk]["prev"] = p.op("vector", lambda e, psB0=psB0, blk=blk: e.tensor_tensor(
                        PLb[blk][:], PL[blk][:], psB0[0:16, :], OP.mult), waits=[mmB])
                pool.release(iB0, S[blk]["prev"])

        g_toks = {0: [], 1: []}
        wb_toks = {0: [], 1: []}
        for th in range(NH):
            for blk in range(2):
                bsl = slice(blk * 512, blk * 512 + 512)
                iW, psW, warW = pool.acquire()
                mmW = p.add("tensor", lambda e, th=th, psW=psW, blk=blk: e.matmul(
                    psW[:], sel16[:, th, :], PLb[blk][:], start=True, stop=True,
                    skip_group_check=True),
                    waits=warW + ([S[blk]["prev"]] if th == 0 else []), inc=("mm", 1))
                wb = wbb[2 * blk + th % 2]
                war_wb = ([wb_toks[blk][th - 2]] if th >= 2 else
                          ([acc_tok[blk]] if acc_tok[blk] else []))
                tev = p.op("scalar", lambda e, psW=psW, wb=wb: e.copy(wb[:], psW[:]),
                           waits=[mmW] + war_wb)
                pool.release(iW, tev)
                tm = tmp2[2 * blk + th % 2]
                war_tm = ([g_toks[blk][th - 2]] if th >= 2 else
                          ([acc_tok[blk]] if acc_tok[blk] else []))
                tmul = p.op("vector", lambda e, th=th, bsl=bsl, tm=tm, wb=wb: e.tensor_tensor(
                    tm[:], wb[:], vT[:, th, bsl], OP.mult),
                    waits=[tev] + war_tm)
                wb_toks[blk].append(tmul)
                if th == 0:
                    au = p.op("gpsimd", lambda e, tm=tm, blk=blk: e.tensor_copy(
                        acc[blk][:], tm[:]), waits=[tmul])
                else:
                    au = p.op("gpsimd", lambda e, tm=tm, blk=blk: e.tensor_tensor(
                        acc[blk][:], acc[blk][:], tm[:], OP.add), waits=[tmul])
                g_toks[blk].append(au)
        for blk in range(2):
            bsl = slice(blk * 512, blk * 512 + 512)
            acc_tok[blk] = p.op("gpsimd", lambda e, h=h, bsl=bsl, blk=blk: e.tensor_copy(
                attn[:, h, bsl], acc[blk][:]), waits=[g_toks[blk][-1]])
    t_attn = ("egpsimd", p.cnt["egpsimd"])

    store_toks = {}
    xr_use = {}
    wo = wst
    for qb in range(2):
        qsl = slice(qb * 512, qb * 512 + 512)
        for oc in range(NCH):
            j, wbuf, ldtok = wo.load(io["woT"].rearrange("(c p) o -> p c o", p=128)
                                     [:, :, oc * 128:(oc + 1) * 128])
            ld_x_tok = p.dma("sync", xr[:, 0, :], d_x2[oc, :, qsl],
                             waits=[xr_use.get(0)], sem="qgxr0")
            i, ps, war = pool.acquire()
            mmt = mm_group(p, ps[:], [(wbuf[:, ic, :], attn[:, ic, qsl]) for ic in range(NCH)],
                           waits=war + [ldtok, t_attn])
            wo.used(j, mmt)
            ev = p.op("vector", lambda e, oc=oc, ps=ps: e.tensor_tensor(
                y3pre[:, oc, :], ps[:], xr[:, 0, :], OP.add),
                waits=[mmt, ld_x_tok, store_toks.get(oc)])
            xr_use[0] = ev
            pool.release(i, ev)
            p.op("vector", lambda e, oc=oc: e.tensor_copy(
                y3bf[:, oc, 0:512], y3pre[:, oc, :]))

        def out_cb(c, t_n, rstd_, nmr_, qb=qb, qsl=qsl):
            tt1 = p.op("vector", lambda e, c=c: e.tensor_tensor(
                y3pre[:, c, :], y3pre[:, c, :], rstd_[:, :512], OP.mult), waits=[t_n])
            tt2 = p.op("vector", lambda e, c=c: e.tensor_tensor(
                y3pre[:, c, :], y3pre[:, c, :], nmr_[:, :512], OP.add), waits=[tt1])
            stok = p.dma("sync", d_x3[c, :, qsl], y3pre[:, c, :],
                         waits=[tt2], sem=f"qgst{c}")
            store_toks[c] = stok
            return stok

        emit_ln(p, pool, lambda c: y3bf[:, c, 0:512], NCH, 512,
                sq, mus, rstd, vars_, rstd, nmr, epscol[:], ones128[:],
                out_cb, stat_waits=[t_init, ("evector", p.cnt["evector"]),
                                    ("egpsimd", p.cnt.get("egpsimd", 0)),
                                    ("mm", p.cnt["mm"])])


def emit_phaseF(p, st, nc, io):
    sb = lambda name, shape, dt=BF16: st.enter_context(nc.sbuf_tensor(name, list(shape), dt))
    d_x3, d_out = io["x3s"], io["out"]
    x3bf = sb("f_x3bf", [128, NCH, 512])
    hbuf = sb("f_h", [128, NCH, 512])
    zx = sb("f_zx", [128, 4, 512])
    aex = sb("f_aex", [128, 512])
    y4 = sb("f_y4", [128, NCH, 512], F32)
    y4bf = sb("f_y4b", [128, NCH, 512])
    scT = sb("f_scT", [8, 512], F32)
    alpaT = sb("f_alpaT", [128, 4, 8], F32)
    expT = sb("f_expT", [128, 4, 8], F32)
    anT = sb("f_anT", [128, 4, 8])
    alphaR = sb("f_alphaR", [8, 512])
    negmax4 = sb("f_ngm", [128, 4], F32)
    lsum4 = sb("f_ls4", [128, 4], F32)
    b1col = sb("f_b1", [128, NCH], F32)
    w2t = sb("f_w2", [128, NCH, 8])
    b2col = sb("f_b2", [8, 1], F32)
    E4t = sb("f_E4", [8, 4, 128])
    I128f = sb("f_If", [128, 128], F32)
    I128b = sb("f_Ib", [128, 128])
    ones128 = sb("f_o128", [128, 128])
    epscol = sb("f_eps", [128, 1], F32)
    xr = sb("f_xr", [128, 2, 512], F32)
    orot = sb("f_or", [128, 2, 512], F32)
    sq = [sb(f"f_sq{i}", [128, 512]) for i in range(2)]
    mus = sb("f_mus", [128, 512], F32)
    ex2s = sb("f_ex2s", [128, 512], F32)
    vars_ = sb("f_vars", [128, 512], F32)
    rstd = sb("f_rstd", [128, 512], F32)
    nmr = sb("f_nmr", [128, 512], F32)
    t1 = sb("f_t1", [128, 512], F32)

    pool = PsumPool(p, st, n=6, shape=(128, 512), name="f_ps")
    psn8 = st.enter_context(nc.psum_tensor("f_psn8", [8, 512], F32))
    psb8 = st.enter_context(nc.psum_tensor("f_psb8", [8, 512], BF16))

    for dst, src in ((b1col, io["b1col"]), (b2col, io["b2col"]), (I128f, io["I128f"]),
                     (I128b, io["I128"]), (ones128, io["ones128"]), (epscol, io["epscol"])):
        p.dma("sync", dst[:], src, sem="qfinit")
    p.dma("gpsimd", w2t[:], io["w2t"], sem="qfgw2")
    p.dma("sync", E4t[:], io["E4"].rearrange("c e p -> e c p"), sem="qfinit")
    t_init = p.qtot("qfinit")
    t_gw2 = p.qtot("qfgw2")

    w1 = WStream(p, st, "fw1", [128, NCH, 128])
    va = WStream(p, st, "fva", [128, NCH, 128])
    ub = WStream(p, st, "fub", [128, 4, 128])

    store_toks = {}
    xr_use = {}
    for tb in range(2):
        tsl = slice(tb * 512, tb * 512 + 512)
        ld_x = p.dma("gpsimd", x3bf[:], d_x3.rearrange("c p t -> p c t")[:, :, tsl],
                     sem=f"qfx{tb % 2}",
                     waits=[("mm", p.cnt.get("mm", 0))])
        for oc in range(NCH):
            j, buf, ldtok = w1.load(io["w14"][oc])
            i, ps, war = pool.acquire()
            mmt = mm_group(p, ps[:], [(buf[:, ic, :], x3bf[:, ic, :]) for ic in range(NCH)],
                           waits=war + [ldtok, ld_x, t_init])
            w1.used(j, mmt)
            ev = p.op("scalar", lambda e, oc=oc, ps=ps: e.activation(
                hbuf[:, oc, :], ps[:], AF.Relu, bias=b1col[:, oc:oc + 1]),
                waits=[mmt])
            pool.release(i, ev)
        t_h = ("escalar", p.cnt["escalar"])
        mmt = mm_group(p, psn8[:], [(w2t[:, ic, :], hbuf[:, ic, :]) for ic in range(NCH)],
                       waits=[t_h, t_init, t_gw2, ("escalar", p.cnt["escalar"])])
        t_sc = p.op("scalar", lambda e: e.activation(scT[:], psn8[:], AF.Identity,
                                                     bias=b2col[:]), waits=[mmt])
        tr_toks = []
        for blk in range(4):
            i, ps, war = pool.acquire()
            tt_ = p.add("tensor", lambda e, blk=blk, ps=ps: e.transpose(
                ps[:, 0:8], scT[:, blk * 128:(blk + 1) * 128], I128f[0:8, 0:8]),
                waits=war + [t_sc, t_init], inc=("mm", 1))
            ev = p.op("vector", lambda e, blk=blk, ps=ps: e.tensor_copy(
                alpaT[:, blk, :], ps[:, 0:8]),
                waits=[tt_, ("escalar", p.cnt["escalar"])])
            pool.release(i, ev)
            tr_toks.append(ev)
        t_ngm = p.op("vector", lambda e: e.reduce_max(
            negmax4[:], alpaT[:], axis=AX.X, negate=True), waits=[tr_toks[-1]])
        exp_toks = []
        for blk in range(4):
            exp_toks.append(p.op("scalar", lambda e, blk=blk: e.activation(
                expT[:, blk, :], alpaT[:, blk, :], AF.Exp, bias=negmax4[:, blk:blk + 1]),
                waits=[t_ngm]))
        t_ls = p.op("vector", lambda e: e.reduce_sum(
            lsum4[:], expT[:], axis=AX.X), waits=[exp_toks[-1]])
        t_lr = p.op("vector", lambda e: e.reciprocal(lsum4[:], lsum4[:]), waits=[t_ls])
        an_toks = []
        for blk in range(4):
            an_toks.append(p.op("vector", lambda e, blk=blk: e.tensor_scalar_mul(
                anT[:, blk, :], expT[:, blk, :], lsum4[:, blk:blk + 1]),
                waits=[t_lr, ("mm", p.cnt["mm"])]))
        trb_toks = []
        for blk in range(4):
            tt_ = p.add("tensor", lambda e, blk=blk: e.transpose(
                psb8[:, blk * 128:(blk + 1) * 128], anT[:, blk, :], I128b[:]),
                waits=[an_toks[blk], ("evector", p.cnt["evector"]), t_init],
                inc=("mm", 1))
            trb_toks.append(tt_)
        t_aR = p.op("vector", lambda e: e.tensor_copy(alphaR[:], psb8[:]),
                    waits=[trb_toks[-1]])
        for krc in range(4):
            j, buf, ldtok = va.load(io["va4"][krc])
            i, ps, war = pool.acquire()
            mmv = mm_group(p, ps[:], [(buf[:, ic, :], x3bf[:, ic, :]) for ic in range(NCH)],
                           waits=war + [ldtok])
            va.used(j, mmv)
            i2, ps2, war2 = pool.acquire()
            mme = mm_group(p, ps2[:], [(E4t[:, krc, :], alphaR[:])],
                           waits=war2 + [t_aR, t_init])
            eva = p.op("vector", lambda e, ps2=ps2: e.tensor_copy(aex[:], ps2[:]),
                       waits=[mme])
            pool.release(i2, eva)
            evz = p.op("vector", lambda e, krc=krc, ps=ps: e.tensor_tensor(
                zx[:, krc, :], ps[:], aex[:], OP.mult), waits=[mmv, eva])
            pool.release(i, evz)
        t_zx = ("evector", p.cnt["evector"])
        for oc in range(NCH):
            j, buf, ldtok = ub.load(io["ub4"][oc])
            ldr = p.dma("sync", xr[:, oc % 2, :], d_x3[oc, :, tsl],
                        waits=[xr_use.get(oc % 2)], sem=f"qfxr{oc % 2}")
            i, ps, war = pool.acquire()
            mmt = mm_group(p, ps[:], [(buf[:, k_, :], zx[:, k_, :]) for k_ in range(4)],
                           waits=war + [ldtok, t_zx])
            ub.used(j, mmt)
            ev = p.op("vector", lambda e, oc=oc, ps=ps: e.tensor_tensor(
                y4[:, oc, :], ps[:], xr[:, oc % 2, :], OP.add), waits=[mmt, ldr])
            xr_use[oc % 2] = ev
            pool.release(i, ev)
            p.op("vector", lambda e, oc=oc: e.tensor_copy(y4bf[:, oc, :], y4[:, oc, :]))

        def out_cb(c, t_n, rstd_, nmr_, tb=tb, tsl=tsl):
            w = [t_n, store_toks.get(c % 2)]
            tt1 = p.op("vector", lambda e, c=c: e.tensor_tensor(
                t1[:], y4[:, c, :], rstd_[:, :512], OP.mult), waits=w)
            tt2 = p.op("vector", lambda e, c=c: e.tensor_tensor(
                orot[:, c % 2, :], t1[:], nmr_[:, :512], OP.add), waits=[tt1])
            stok = p.dma("sync", d_out[c, :, tsl], orot[:, c % 2, :],
                         waits=[tt2], sem=f"qfst{c % 2}")
            store_toks[c % 2] = stok
            return stok

        emit_ln(p, pool, lambda c: y4bf[:, c, :], NCH, 512,
                sq, mus, ex2s, vars_, rstd, nmr, epscol[:], ones128[:],
                out_cb, stat_waits=[t_init, ("evector", p.cnt["evector"]),
                                    ("mm", p.cnt["mm"])])


def build_fused(phases="ABCGF"):
    nc = bass.Bass("TRN2", target_bir_lowering=False)
    io = {}
    ext = [
        ("x", [NCH, 128, TOK], F32), ("qkv", [D, 3 * D], F32),
        ("gamma", [128, NCH], F32), ("beta", [128, NCH], F32),
        ("o_proj", [D, D], F32), ("masks", [2, 16, 128, 512], BF16),
        ("H4", [16, 128, 16, 128], BF16), ("up4", [64, 128, 16, 128], F32),
        ("gt4", [64, 128, 16, 128], F32), ("dn4", [16, 128, 64, 128], F32),
        ("C128", [128, 128], BF16), ("suug", [128, NCH], F32),
        ("sudp", [128, 64], F32), ("wsu", [128, 64], F32), ("wsg", [128, 64], F32),
        ("wsdp", [128, NCH], F32), ("wqT", [D, D], F32), ("wkT", [D, D], F32),
        ("wvT", [D, D], F32), ("woT", [D, D], F32),
        ("E16", [16, 128, 16], BF16), ("sel16", [16, 16, 128], BF16),
        ("ones1_16", [1, 16], BF16), ("ones1_16f", [1, 16], F32),
        ("ones16cf", [16, 1], F32), ("ones_col", [128, 1], BF16),
        ("ones128", [128, 128], BF16), ("ones1", [1, 128], BF16),
        ("epscol", [128, 1], F32),
        ("w14", [16, 128, 16, 128], F32), ("b1col", [128, NCH], F32),
        ("w2t", [128, NCH, 8], F32), ("b2col", [8, 1], F32),
        ("va4", [4, 128, 16, 128], F32), ("ub4", [16, 128, 4, 128], F32),
        ("E4", [4, 8, 128], BF16), ("I128f", [128, 128], F32),
        ("I128", [128, 128], BF16),
    ]
    for name, shape, dt in ext:
        io[name] = nc.dram_tensor(name, shape, dt, kind="ExternalInput").ap()
    io["out"] = nc.dram_tensor("out", [NCH, 128, TOK], F32, kind="ExternalOutput").ap()
    for name, shape in (("agk_in", [D, TOK]), ("agk_out", [2 * D, TOK]),
                        ("agv_in", [TOK, D]), ("agv_out", [2 * TOK, D]),
                        ("x1s", [NCH, 128, TOK]), ("x2s", [NCH, 128, TOK]),
                        ("x3s", [NCH, 128, TOK])):
        io[name] = nc.dram_tensor(name, shape, F32).ap()

    p = Prog(nc)
    with ExitStack() as st_top:
        with ExitStack() as stAB:
            qTs = stAB.enter_context(nc.sbuf_tensor("qTspan", [128, NCH, TOK], BF16))
            with ExitStack() as stA:
                t_ag = emit_phaseA(p, stA, nc, io, qTs)
            barrier(p)
            if "B" in phases:
                with ExitStack() as stB:
                    emit_phaseB(p, stB, nc, io, qTs, t_ag)
        barrier(p)
        if "C" in phases:
            with ExitStack() as stC:
                emit_phaseC(p, stC, nc, io)
        barrier(p)
        if "G" in phases:
            with ExitStack() as stG:
                emit_phaseG(p, stG, nc, io)
        barrier(p)
        if "F" in phases:
            with ExitStack() as stF:
                emit_phaseF(p, stF, nc, io)
        if "F" not in phases:
            p.dma("sync", io["out"][0, :, :], io["x"][0, :, :], sem="qdbg")
        p.add("sync", lambda e: e.nop(),
              waits=[p.qtot(s) for s in sorted(p.cnt) if s.startswith("q")])
        emit_prog(p, st_top)
    return nc


def kernel(**inputs):
    g = host_prep(inputs)
    if "fused" not in _PROGS:
        _PROGS["fused"] = build_fused()
    nc = _PROGS["fused"]
    H4 = w4(g["H2048s"])
    up4 = w4(g["w_upT"])
    gt4 = w4(g["w_gateT"])
    dn4 = w4(g["w_downT"])
    w14 = w4(g["W1"])
    va4 = w4(g["VallT"])
    ub4 = w4(g["Ubar"])
    in_maps = []
    for c in range(8):
        in_maps.append({
            "x": g["x_fm"][c], "qkv": g["qkv"], "gamma": g["gamma"], "beta": g["beta"],
            "o_proj": g["o_proj"], "masks": g["masks"][c],
            "H4": H4, "up4": up4, "gt4": gt4, "dn4": dn4, "C128": g["C128"],
            "suug": g["su_ug"], "sudp": g["su_dp"], "wsu": g["wsu"], "wsg": g["wsg"],
            "wsdp": g["wsdp"], "wqT": g["wqT"], "wkT": g["wkT"], "wvT": g["wvT"],
            "woT": g["woT"], "E16": g["E16"], "sel16": g["sel16"],
            "ones1_16": g["ones1_16"], "ones1_16f": g["ones1_16f"],
            "ones16cf": g["ones16cf"], "ones_col": g["ones_col"],
            "ones128": g["ones128"], "ones1": g["ones1"], "epscol": g["epscol"],
            "w14": w14, "b1col": g["b1"], "w2t": g["w2t"], "b2col": g["b2"],
            "va4": va4, "ub4": ub4, "E4": g["E4"], "I128f": g["I128f"],
            "I128": g["I128"],
        })
    res = _run_many(nc, in_maps)
    out = np.empty((B, S, D), dtype=np.float32)
    for c in range(8):
        b, h = c // 2, c % 2
        out[b, h * TOK:(h + 1) * TOK, :] = fm_inv(np.asarray(res[c]["out"]))
    return out
